# version 02
# baseline (speedup 1.0000x reference)
import contextlib
import math
import numpy as np
import ml_dtypes
import concourse.bass as bass
import concourse.mybir as mybir
from concourse.bass_utils import run_bass_kernel_spmd

F32 = mybir.dt.float32
BF16 = mybir.dt.bfloat16
FP16 = mybir.dt.float16
AF = mybir.ActivationFunctionType
OP = mybir.AluOpType
BFD = ml_dtypes.bfloat16

ENGS = ("sync", "tensor", "vector", "scalar", "gpsimd")

S = 2048
DM = 2048
NCD = 16
TB = 512
NBLK = 4
TCH = 4
HA = 16
HKV = 4
WIN = 256
HBH = 16
DN = 128
DR = 64
DV = 128
HID = 8192
NHC = 64
RSQ_DN = 1.0 / math.sqrt(128.0)
FWHT_SC = 1.0 / math.sqrt(8192.0)


class Sem:
    __slots__ = ("h", "n")
    def __init__(self, h):
        self.h = h
        self.n = 0


class Rec:
    def __init__(self, nc, stack):
        self.nc = nc
        self.ops = {e: [] for e in ENGS}
        self.stack = stack
        self._semid = 0

    def sem(self, name=None):
        self._semid += 1
        if name is None:
            name = f"s{self._semid}"
        if not hasattr(self, "_semcache"):
            self._semcache = {}
        if name not in self._semcache:
            h = self.stack.enter_context(self.nc.semaphore(name))
            self._semcache[name] = Sem(h)
        return self._semcache[name]

    def emit(self, eng, fn):
        self.ops[eng].append(fn)

    def wait(self, eng, sem, val=None):
        v = sem.n if val is None else val
        if v <= 0:
            return
        self.emit(eng, lambda e, h=sem.h, v=v: e.wait_ge(h, v))

    def inc(self, eng, sem, k=1):
        sem.n += k
        self.emit(eng, lambda e, h=sem.h, k=k: e.nop().then_inc(h, k))

    def _fin(self, inst, sem, k):
        if sem is not None:
            inst.then_inc(sem.h, k)

    def dma(self, eng, out, in_, sem=None, **kw):
        if sem is not None:
            sem.n += 16
        def f(e, out=out, in_=in_, s=sem, kw=kw):
            i = e.dma_start(out=out, in_=in_, **kw)
            if s is not None:
                i.then_inc(s.h, 16)
        self.emit(eng, f)

    def dmat(self, out, in_, sem=None):
        if sem is not None:
            sem.n += 16
        def f(e, out=out, in_=in_, s=sem):
            i = e.dma_start_transpose(out, in_)
            if s is not None:
                i.then_inc(s.h, 16)
        self.emit("sync", f)

    def mm(self, out, lhsT, rhs, start, stop, sem=None, k=1):
        if sem is not None:
            sem.n += k
        def f(e, out=out, lhsT=lhsT, rhs=rhs, st=start, sp=stop, s=sem, k=k):
            i = e.matmul(out, lhsT, rhs, start=st, stop=sp)
            if s is not None:
                i.then_inc(s.h, k)
        self.emit("tensor", f)

    def tr(self, out, in_, iden, sem=None):
        if sem is not None:
            sem.n += 1
        def f(e, out=out, in_=in_, iden=iden, s=sem):
            i = e.transpose(out, in_, iden)
            if s is not None:
                i.then_inc(s.h, 1)
        self.emit("tensor", f)

    def act(self, eng, out, in_, func, bias=0.0, scale=1.0, sem=None):
        if sem is not None:
            sem.n += 1
        def f(e, out=out, in_=in_, func=func, b=bias, sc=scale, s=sem):
            i = e.activation(out, in_, func, bias=b, scale=sc)
            if s is not None:
                i.then_inc(s.h, 1)
        self.emit(eng, f)

    def ew(self, eng, name, *args, sem=None, **kw):
        if sem is not None:
            sem.n += 1
        def f(e, name=name, args=args, s=sem, kw=kw):
            i = getattr(e, name)(*args, **kw)
            if s is not None:
                i.then_inc(s.h, 1)
        self.emit(eng, f)

    def barrier(self, bsem, engines=ENGS):
        base = bsem.n
        for e in engines:
            self.emit(e, lambda en, h=bsem.h: en.nop().then_inc(h, 1))
        bsem.n = base + len(engines)
        for e in engines:
            self.wait(e, bsem)

    def replay(self):
        with self.nc.Block() as blk:
            mk = {}
            def bind(ename):
                ops = self.ops[ename]
                def runner(eng):
                    for f in ops:
                        f(eng)
                return runner
            blk.sync(bind("sync"))
            blk.tensor(bind("tensor"))
            blk.vector(bind("vector"))
            blk.scalar(bind("scalar"))
            blk.gpsimd(bind("gpsimd"))
        self.ops = {e: [] for e in ENGS}



def _rope_tables_a():
    inv = 1.0 / (10000.0 ** (np.arange(0, 128, 2, dtype=np.float64) / 128.0))
    pos = np.arange(S, dtype=np.float64)
    ang = pos[None, :] * inv[:, None]
    c = np.cos(ang); s = np.sin(ang)
    C = np.concatenate([c, c], axis=0).astype(np.float32)
    Sp = np.concatenate([-s, s], axis=0).astype(np.float32)
    return C, Sp


def _rope_tables_b():
    inv = 1.0 / (10000.0 ** (np.arange(0, 64, 2, dtype=np.float64) / 64.0))
    pos = np.arange(S, dtype=np.float64)
    ang = pos[None, :] * inv[:, None]
    c = np.cos(ang); s = np.sin(ang)
    C = np.concatenate([c, c, c, c], axis=0).astype(np.float32)
    Sp = np.concatenate([-s, s, -s, s], axis=0).astype(np.float32)
    return C, Sp


def _consts():
    d = {}
    C_A, S_A = _rope_tables_a()
    C_B, S_B = _rope_tables_b()
    d["cs_a"] = C_A; d["sn_a"] = S_A
    d["cs_b"] = C_B; d["sn_b"] = S_B
    d["csk_b"] = (C_B * 0.125).astype(np.float32)
    d["snk_b"] = (S_B * 0.125).astype(np.float32)
    ii = np.arange(128)
    d["p64"] = np.eye(128, dtype=np.float32)[ii ^ 64].T.astype(BFD)
    d["p32"] = np.eye(128, dtype=np.float32)[ii ^ 32].T.astype(BFD)
    ki = ii[:, None]; qi = ii[None, :]
    d["tri0"] = (ki <= qi).astype(np.float32).astype(BFD)
    d["tri2"] = (ki > qi).astype(np.float32).astype(BFD)
    d["ones"] = np.ones((128, 128), dtype=np.float32).astype(BFD)
    d["idb"] = np.eye(128, dtype=np.float32).astype(BFD)
    d["idf"] = np.eye(128, dtype=np.float32)
    d["h64"] = ((-1.0) ** np.array([[bin(a & b).count("1") for b in range(64)]
                                    for a in range(64)])).astype(np.float32).astype(BFD)
    d["h128"] = ((-1.0) ** np.array([[bin(a & b).count("1") for b in range(128)]
                                     for a in range(128)])).astype(np.float32).astype(np.float16)
    seld = np.zeros((128, 16 * 16), dtype=np.float32)
    for h in range(16):
        seld[:, h * 16 + h] = 1.0
    d["seld"] = seld.astype(BFD)
    expd = np.zeros((16, 16 * 128), dtype=np.float32)
    for h in range(16):
        expd[h, h * 128:(h + 1) * 128] = 1.0
    d["expd"] = expd
    return d


_CONSTS = _consts()



def build_program():
    nc = bass.Bass("TRN2", target_bir_lowering=False,
                   detect_race_conditions=False)

    ap = {}
    def din(name, shape, dt):
        ap[name] = nc.dram_tensor(name, list(shape), dt, kind="ExternalInput").ap()
    def dscr(name, shape, dt):
        ap[name] = nc.dram_tensor(name, list(shape), dt, kind="Internal").ap()

    din("x", (S, DM), F32)
    din("wqa", (2048, 2048), F32)
    din("wkva", (1024, 2048), F32)
    din("woa", (2048, 2048), F32)
    din("temp", (16, 1), F32)
    din("wqb", (3072, 2048), F32)
    din("wkb", (3072, 2048), F32)
    din("wvb", (2048, 2048), F32)
    din("wob", (2048, 2048), F32)
    din("w1", (8192, 2048), F32)
    din("w2", (2048, 8192), F32)
    din("w3", (8192, 2048), F32)
    for n, v in _CONSTS.items():
        dt = F32 if v.dtype == np.float32 else (BF16 if v.dtype == BFD else FP16)
        din(n, v.shape, dt)
    ap["out"] = nc.dram_tensor("out", [S, DM], F32, kind="ExternalOutput").ap()

    dscr("xscr", (DM, S), F32)
    dscr("wqa_bf", (2048, 2048), BF16)
    dscr("wkva_bf", (1024, 2048), BF16)
    dscr("woa_bf", (2048, 2048), BF16)
    dscr("wqb_bf", (3072, 2048), BF16)
    dscr("wkb_bf", (3072, 2048), BF16)
    dscr("wvb_bf", (2048, 2048), BF16)
    dscr("wob_bf", (2048, 2048), BF16)
    dscr("w1_bf", (8192, 2048), BF16)
    dscr("w3_bf", (8192, 2048), BF16)
    dscr("s1h", (64, 2048 * 128), FP16)
    dscr("w2m", (2048, 8192), BF16)
    dscr("t16", (2, 16), F32)
    dscr("kscr", (16, 128, 3072), BF16)
    dscr("vscr", (16, 128, 2048), BF16)
    dscr("qscr", (24, 128, S), BF16)
    dscr("dbg", (DM, S), F32)

    stack = contextlib.ExitStack()
    with stack:
        r = Rec(nc, stack)
        E = stack.enter_context

        pb = [E(nc.psum_tensor(f"pb{i}", [128, 512], F32)) for i in range(8)]
        c_ones = E(nc.sbuf_tensor("c_ones", [128, 128], BF16))
        c_idb = E(nc.sbuf_tensor("c_idb", [128, 128], BF16))
        c_idf = E(nc.sbuf_tensor("c_idf", [128, 128], F32))
        c_tri0 = E(nc.sbuf_tensor("c_tri0", [128, 128], BF16))
        c_tri2 = E(nc.sbuf_tensor("c_tri2", [128, 128], BF16))
        c_p64 = E(nc.sbuf_tensor("c_p64", [128, 128], BF16))
        c_p32 = E(nc.sbuf_tensor("c_p32", [128, 128], BF16))
        c_seld = E(nc.sbuf_tensor("c_seld", [128, 256], BF16))
        c_tempbc = E(nc.sbuf_tensor("c_tempbc", [128, 32], F32))
        c_eps = E(nc.sbuf_tensor("c_eps", [128, 2], F32))
        ws_a = E(nc.sbuf_tensor("ws_a", [128, 2 * 2048], BF16))
        ws_b = E(nc.sbuf_tensor("ws_b", [128, 2 * 2048], BF16))

        bar = r.sem("bar")

        with contextlib.ExitStack() as sc:
            g0(r, nc, ap, sc.enter_context, pb,
               dict(ones=c_ones, idb=c_idb, idf=c_idf, tri0=c_tri0,
                    tri2=c_tri2, p64=c_p64, p32=c_p32, seld=c_seld,
                    tempbc=c_tempbc, eps=c_eps))
            r.barrier(bar)
            r.replay()

        with contextlib.ExitStack() as sc:
            g1a(r, nc, ap, sc.enter_context, pb,
                dict(ones=c_ones, idf=c_idf, tri0=c_tri0, tri2=c_tri2,
                     p64=c_p64, tempbc=c_tempbc, ws_a=ws_a, ws_b=ws_b,
                     eps=c_eps))
            r.barrier(bar)
            r.replay()

        cdict = dict(ones=c_ones, eps=c_eps)
        with contextlib.ExitStack() as sc:
            ffn_group(r, nc, ap, sc.enter_context, pb, ws_a, ws_b, cdict, "2f_")
            r.barrier(bar)
            r.replay()

        with contextlib.ExitStack() as sc:
            g1c(r, nc, ap, sc.enter_context, pb, ws_a, ws_b,
                dict(ones=c_ones, eps=c_eps, p32=c_p32))
            r.barrier(bar)
            r.replay()

        with contextlib.ExitStack() as sc:
            g2a(r, nc, ap, sc.enter_context, pb, ws_a, c_seld, c_ones)
            r.barrier(bar)
            r.replay()

        with contextlib.ExitStack() as sc:
            ffn_group(r, nc, ap, sc.enter_context, pb, ws_a, ws_b, cdict, "4f_")
            g2b_exit(r, nc, ap, sc.enter_context, pb, c_idf)
            r.barrier(bar)
            r.replay()

    return nc



def g0(r, nc, ap, E, pb, C):
    ld = r.sem("g0ld")
    cast = r.sem("g0cast")
    pe = r.sem("g0pe")
    ve = r.sem("g0ve")
    se = r.sem("g0se")
    stv = r.sem("g0st")

    for name, buf in (("ones", C["ones"]), ("idb", C["idb"]), ("idf", C["idf"]),
                      ("tri0", C["tri0"]), ("tri2", C["tri2"]), ("p64", C["p64"]),
                      ("p32", C["p32"]), ("seld", C["seld"])):
        r.dma("sync", buf[:], ap[name], sem=ld)

    r.ew("vector", "memset", C["eps"][:, 0:1], 1e-5)
    r.ew("vector", "memset", C["eps"][:, 1:2], 1e-24, sem=ve)
    tb1 = E(nc.sbuf_tensor("g0_tb1", [16, 2], F32))
    tbr = E(nc.sbuf_tensor("g0_tbr", [1, 32], BF16))
    onesrow = E(nc.sbuf_tensor("g0_or", [1, 32], BF16))
    r.dma("sync", tb1[:, 0:1], ap["temp"], sem=ld)
    n_ld = ld.n
    r.wait("scalar", ld, n_ld)
    r.act("scalar", tb1[:, 1:2], tb1[:, 0:1], AF.Abs, scale=1.0, sem=se)
    r.ew("scalar", "mul", tb1[:, 1:2], tb1[:, 1:2], -1.0, sem=se)
    tbs = r.sem("g0_tbs")
    r.wait("gpsimd", se, se.n)
    r.dma("gpsimd", bass.AP(ap["t16"].tensor, 0, [[1, 16], [16, 2]]), tb1[:, 0:2], sem=tbs,
          allow_slow_non_contiguous=True)
    r.wait("gpsimd", tbs, tbs.n)
    r.dma("gpsimd", tbr[:], bass.AP(ap["t16"].tensor, 0, [[32, 1], [1, 32]]), sem=tbs)
    r.wait("vector", ld, n_ld)
    r.ew("vector", "memset", onesrow[:], 1.0, sem=ve)
    r.wait("tensor", ld, n_ld)
    r.wait("tensor", tbs, tbs.n)
    r.wait("tensor", ve, ve.n)
    r.mm(pb[0][:, 0:32], C["ones"][0:1, 0:128], tbr[:], True, True, sem=pe)
    r.wait("vector", pe, pe.n)
    r.ew("vector", "tensor_copy", C["tempbc"][:, 0:32], pb[0][:, 0:32], sem=ve)

    for src, dst in (("wqa", "wqa_bf"), ("wkva", "wkva_bf"), ("woa", "woa_bf"),
                     ("w1", "w1_bf"), ("w3", "w3_bf"),
                     ("wqb", "wqb_bf"), ("wkb", "wkb_bf"), ("wvb", "wvb_bf"),
                     ("wob", "wob_bf")):
        n = ap[src].tensor.shape[0]
        half = n // 2
        r.dma("gpsimd", ap[dst][0:half, :], ap[src][0:half, :], sem=cast)
        r.dma("gpsimd", ap[dst][half:n, :], ap[src][half:n, :], sem=cast)

    h64 = E(nc.sbuf_tensor("g0_h64", [64, 64], BF16))
    h128 = E(nc.sbuf_tensor("g0_h128", [128, 128], FP16))
    ga = E(nc.sbuf_tensor("g0_ga", [64, 2 * 1024], BF16))
    ea = E(nc.sbuf_tensor("g0_ea", [64, 2 * 1024], FP16))
    ld2 = r.sem("g0ld2")
    r.dma("sync", h64[:], ap["h64"], sem=ld2)
    r.dma("sync", h128[:], ap["h128"], sem=ld2)
    pa_mm = r.sem("g0_pamm")
    pa_ev = r.sem("g0_paev")
    gas = [r.sem("g0_ga0"), r.sem("g0_ga1")]
    sas = [r.sem("g0_sa0"), r.sem("g0_sa1")]
    r.wait("tensor", ld2, ld2.n)
    NT_A = 256
    for i in range(NT_A):
        d0 = i * 8
        slot = i % 2
        if i >= 2:
            r.wait("gpsimd", pa_ev, i - 1)
        r.dma("gpsimd",
              ga[:, slot * 1024:(slot + 1) * 1024],
              bass.AP(ap["w2"].tensor, d0 * 8192, [[128, 64], [8192, 8], [1, 128]]),
              sem=gas[slot])
        bk = 4 + 2 * slot
        r.wait("tensor", gas[slot], gas[slot].n)
        if i >= 2:
            r.wait("tensor", pa_ev, i - 1)
        r.mm(pb[bk][0:64, :], h64[:], ga[0:64, slot * 1024: slot * 1024 + 512],
             True, True)
        r.mm(pb[bk + 1][0:64, :], h64[:],
             ga[0:64, slot * 1024 + 512: slot * 1024 + 1024], True, True, sem=pa_mm)
        r.wait("vector", pa_mm, pa_mm.n)
        if i >= 2:
            r.wait("vector", sas[slot], sas[slot].n)
        r.ew("vector", "tensor_copy", ea[0:64, slot * 1024: slot * 1024 + 512],
             pb[bk][0:64, :])
        r.ew("vector", "tensor_copy", ea[0:64, slot * 1024 + 512: slot * 1024 + 1024],
             pb[bk + 1][0:64, :], sem=pa_ev)
        r.wait("sync", pa_ev, pa_ev.n)
        r.dma("sync",
              bass.AP(ap["s1h"].tensor, d0 * 128, [[2048 * 128, 64], [128, 8], [1, 128]]),
              ea[0:64, slot * 1024:(slot + 1) * 1024], sem=sas[slot])

    xb = E(nc.sbuf_tensor("g0_xb", [128, 2 * 512], FP16))
    eb = E(nc.sbuf_tensor("g0_eb", [128, 2 * 512], BF16))
    xbs = [r.sem("g0_xb0"), r.sem("g0_xb1")]
    pb_mm = r.sem("g0_pbmm")
    pb_ev = r.sem("g0_pbev")
    sbs = [r.sem("g0_sb0"), r.sem("g0_sb1")]
    ti = 0
    for dblk in range(4):
        for cp in range(64):
            slot = ti % 2
            if ti >= 2:
                r.wait("sync", pb_ev, ti - 1)
            if cp == 0:
                r.wait("sync", sas[0], 16 * 32 * (dblk + 1))
                r.wait("sync", sas[1], 16 * 32 * (dblk + 1))
            srcap = bass.AP(ap["s1h"].tensor, cp * (2048 * 128) + dblk * 512 * 128,
                            [[128, 512], [1, 128]])
            r.dmat(xb[:, slot * 512:(slot + 1) * 512], srcap, sem=xbs[slot])
            bank = pb[4 + slot]
            r.wait("tensor", xbs[slot], xbs[slot].n)
            if ti < 2:
                r.wait("tensor", pa_ev, NT_A)
            else:
                r.wait("tensor", pb_ev, ti - 1)
            for k in range(4):
                r.mm(bank[:, k * 128:(k + 1) * 128],
                     xb[:, slot * 512 + k * 128: slot * 512 + (k + 1) * 128],
                     h128[:], k == 0, k == 3, sem=(pb_mm if k == 3 else None))
            r.wait("vector", pb_mm, pb_mm.n)
            if ti >= 2:
                r.wait("vector", sbs[slot], sbs[slot].n)
            r.ew("vector", "tensor_scalar_mul", eb[:, slot * 512:(slot + 1) * 512],
                 bank[:], FWHT_SC, sem=pb_ev)
            r.wait("gpsimd", pb_ev, pb_ev.n)
            for k in range(4):
                d0 = dblk * 512 + k * 128
                r.dma("gpsimd", ap["w2m"][d0:d0 + 128, cp * 128:(cp + 1) * 128],
                      eb[:, slot * 512 + k * 128: slot * 512 + (k + 1) * 128],
                      sem=sbs[slot])
            ti += 1

    r.wait("gpsimd", sbs[0], sbs[0].n)
    r.wait("gpsimd", sbs[1], sbs[1].n)
    r.wait("gpsimd", cast, cast.n)
    r.wait("sync", sas[0], sas[0].n)
    r.wait("sync", sas[1], sas[1].n)



def entry_block(r, nc, ap, pb, idf, xpan, xblk, b, sems):
    xin, tp, evd = sems
    for tc in range(TCH):
        g = b * TCH + tc
        slot = g % 2
        if g >= 2:
            r.wait("sync", tp, (g - 1) * NCD)
        r.dma("sync", xpan[:, slot * 2048:(slot + 1) * 2048],
              ap["x"][g * 128:(g + 1) * 128, :], sem=xin[slot])
        r.wait("tensor", xin[slot], xin[slot].n)
        for dc in range(NCD):
            bank = pb[dc % 2]
            j = g * NCD + dc
            if j >= 2:
                r.wait("tensor", evd, j - 1)
            r.tr(bank[:, 0:128],
                 xpan[:, slot * 2048 + dc * 128: slot * 2048 + (dc + 1) * 128],
                 idf[:], sem=tp)
            r.wait("vector", tp, tp.n)
            r.ew("vector", "tensor_copy",
                 xblk[:, dc * 512 + tc * 128: dc * 512 + tc * 128 + 128],
                 bank[:, 0:128], sem=evd)


def ln_block(r, bufs, banks, sems, b, ev):
    xblk, yblk = bufs["xblk"], bufs["yblk"]
    xb16, sq16 = bufs["xb16"], bufs["sq16"]
    rstd, negmr, lntmp = bufs["rstd"], bufs["negmr"], bufs["lntmp"]
    bm, bs = banks
    s_cast, s_sq, s_slot, s_lnmm, s_stat, s_rstd = (
        sems["ln_cast"], sems["ln_sq"], sems["ln_slot"], sems["ln_mm"],
        sems["ln_stat"], sems["ln_rstd"])
    ones = bufs["ones"]
    if ev.get("xblk_ready"):
        r.wait("scalar", *ev["xblk_ready"])
    for dc in range(NCD):
        slot = dc % 2
        if ev.get("ln_slot%d" % slot):
            r.wait("vector", *ev["ln_slot%d" % slot])
            r.wait("scalar", *ev["ln_slot%d" % slot])
        if dc == 0 and ev.get("yblk"):
            r.wait("vector", *ev["yblk"])
        r.ew("vector", "tensor_copy", xb16[:, slot * 512:(slot + 1) * 512],
             xblk[:, dc * 512:(dc + 1) * 512], sem=s_cast)
        r.act("scalar", sq16[:, slot * 512:(slot + 1) * 512],
              xblk[:, dc * 512:(dc + 1) * 512], AF.Square, sem=s_sq)
        r.wait("tensor", s_cast, s_cast.n)
        r.wait("tensor", s_sq, s_sq.n)
        if dc == 0 and ev.get("ln_banks"):
            r.wait("tensor", *ev["ln_banks"])
        r.mm(bm[:], ones[:], xb16[:, slot * 512:(slot + 1) * 512],
             dc == 0, dc == NCD - 1)
        r.mm(bs[:], ones[:], sq16[:, slot * 512:(slot + 1) * 512],
             dc == 0, dc == NCD - 1,
             sem=(s_lnmm if dc == NCD - 1 else s_slot))
        if dc != NCD - 1:
            ev["ln_slot%d" % slot] = (s_slot, s_slot.n)
    r.wait("vector", s_lnmm, s_lnmm.n)
    r.ew("vector", "tensor_scalar_mul", lntmp[:], bm[:], 1.0 / 2048.0)
    r.ew("vector", "tensor_mul", rstd[:], lntmp[:], lntmp[:])
    r.ew("vector", "scalar_tensor_tensor", negmr[:], bs[:], 1.0 / 2048.0,
         rstd[:], OP.mult, OP.subtract, sem=s_stat)
    ev["ln_banks"] = (s_stat, s_stat.n)
    r.wait("scalar", s_stat, s_stat.n)
    r.act("scalar", rstd[:], negmr[:], AF.Ln, bias=bufs["eps"][:, 0:1])
    r.act("scalar", rstd[:], rstd[:], AF.Exp, scale=-0.5, sem=s_rstd)
    r.wait("vector", s_rstd, s_rstd.n)
    r.ew("vector", "scalar_tensor_tensor", negmr[:], lntmp[:], -1.0, rstd[:],
         OP.mult, OP.mult, sem=s_stat)
    for dc in range(NCD):
        r.ew("vector", "tensor_mul", lntmp[:], xblk[:, dc * 512:(dc + 1) * 512], rstd[:])
        r.ew("vector", "tensor_add", yblk[:, dc * 512:(dc + 1) * 512], lntmp[:],
             negmr[:], sem=(sems["ln_y"] if dc == NCD - 1 else None))


SPANS_A = {-2: (0, 128), -1: (0, 256), 0: (0, 384), 1: (128, 384),
           2: (256, 256), 3: (384, 128)}

PH1 = {"kv": True, "q": True, "sc": True, "wo": True}
DBG_WO = False


def g1a(r, nc, ap, E, pb, C):
    xin = [r.sem("xld0"), r.sem("xld1")]
    tp = r.sem("tp")
    evd = r.sem("evd")
    xout = r.sem("xout")
    sems = {k: r.sem(k) for k in
            ("ln_cast", "ln_sq", "ln_slot", "ln_mm", "ln_stat", "ln_rstd", "ln_y",
             "pj", "ev", "rp", "sq", "nm", "f", "qk",
             "sc", "ex", "mk", "dn", "pv", "at", "wo", "res")}
    s = sems

    xpan = E(nc.sbuf_tensor("1a_xpan", [128, 2 * 2048], F32))
    xblk = E(nc.sbuf_tensor("1a_xblk", [128, NCD * 512], F32))
    yblk = E(nc.sbuf_tensor("1a_yblk", [128, NCD * 512], BF16))
    xb16 = E(nc.sbuf_tensor("1a_xb16", [128, 2 * 512], BF16))
    sq16 = E(nc.sbuf_tensor("1a_sq16", [128, 2 * 512], BF16))
    rstd = E(nc.sbuf_tensor("1a_rstd", [128, 512], F32))
    negmr = E(nc.sbuf_tensor("1a_negmr", [128, 512], F32))
    lntmp = E(nc.sbuf_tensor("1a_lntmp", [128, 512], F32))
    kbuf = E(nc.sbuf_tensor("1a_kbuf", [128, HKV * 2048], BF16))
    vtm = E(nc.sbuf_tensor("1a_vtm", [128, 16 * 512], BF16))
    qbuf = E(nc.sbuf_tensor("1a_qbuf", [128, 16 * 512], BF16))
    ebuf = E(nc.sbuf_tensor("1a_ebuf", [128, 8 * 512], BF16))
    rt1 = E(nc.sbuf_tensor("1a_rt1", [128, 512], F32))
    rt2 = E(nc.sbuf_tensor("1a_rt2", [128, 512], F32))
    rt3 = E(nc.sbuf_tensor("1a_rt3", [128, 512], F32))
    sqf = E(nc.sbuf_tensor("1a_sqf", [128, 512], BF16))
    tln = E(nc.sbuf_tensor("1a_tln", [128, 512], F32))
    fbuf = E(nc.sbuf_tensor("1a_fbuf", [128, 512], F32))
    exps = E(nc.sbuf_tensor("1a_expS", [128, 6 * 512], BF16))
    zbuf = E(nc.sbuf_tensor("1a_zbuf", [128, 128], BF16))
    atto = E(nc.sbuf_tensor("1a_atto", [128, 16 * 512], BF16))
    rcp = E(nc.sbuf_tensor("1a_rcp", [128, 512], F32))
    dcp = E(nc.sbuf_tensor("1a_dcp", [128, 512], F32))
    csa = E(nc.sbuf_tensor("1a_csa", [128, 2048], F32))
    sna = E(nc.sbuf_tensor("1a_sna", [128, 2048], F32))
    ws = C["ws_a"]
    ones, idf = C["ones"], C["idf"]
    p64, tempbc = C["p64"], C["tempbc"]
    tri0, tri2 = C["tri0"], C["tri2"]

    lnbufs = dict(xblk=xblk, yblk=yblk, xb16=xb16, sq16=sq16, rstd=rstd,
                  negmr=negmr, lntmp=lntmp, ones=ones, eps=C["eps"])
    ev = {}

    zrd = r.sem("zrd")
    r.ew("vector", "memset", zbuf[:], 0.0, sem=zrd)
    r.wait("tensor", zrd, zrd.n)
    tbl = r.sem("tbl")
    r.dma("sync", csa[:], ap["cs_a"], sem=tbl)
    r.dma("sync", sna[:], ap["sn_a"], sem=tbl)
    r.wait("vector", tbl, tbl.n)
    wl = [r.sem("wl0"), r.sem("wl1")]
    wtile = [0]

    def stream(src_rows, src_cols, nrows):
        w = wtile[0]; wtile[0] += 1
        slot = w % 2
        if ev.get("ws%d" % slot):
            r.wait("sync", *ev["ws%d" % slot])
        r.dmat(ws[:, slot * 2048: slot * 2048 + nrows],
               src_rows[0][src_rows[1]:src_rows[1] + nrows,
                           src_cols:src_cols + 128], sem=wl[slot])
        return slot, wl[slot], wl[slot].n

    def rope_norm(cchunk, bank, dst, dstcol, b, is_q, head, ready_ev):
        ecol = (cchunk % 8) * 512
        r.wait("vector", *ready_ev)
        if ev.get("eb%d" % (cchunk % 8)):
            r.wait("vector", *ev["eb%d" % (cchunk % 8)])
        r.ew("vector", "tensor_copy", ebuf[:, ecol:ecol + 512], bank[:], sem=s["ev"])
        r.wait("tensor", s["ev"], s["ev"].n)
        r.mm(bank[:], p64[:], ebuf[:, ecol:ecol + 512], True, True, sem=s["rp"])
        ev["eb%d" % (cchunk % 8)] = (s["rp"], s["rp"].n)
        r.wait("vector", s["rp"], s["rp"].n)
        r.ew("vector", "tensor_mul", rt1[:], ebuf[:, ecol:ecol + 512],
             csa[:, b * 512:(b + 1) * 512])
        r.ew("vector", "tensor_mul", rt2[:], bank[:], sna[:, b * 512:(b + 1) * 512])
        r.ew("vector", "tensor_add", rt3[:], rt1[:], rt2[:], sem=s["rp"])
        r.wait("scalar", s["rp"], s["rp"].n)
        r.act("scalar", sqf[:], rt3[:], AF.Square, sem=s["sq"])
        r.wait("tensor", s["sq"], s["sq"].n)
        r.mm(bank[:], ones[:], sqf[:], True, True, sem=s["nm"])
        r.wait("scalar", s["nm"], s["nm"].n)
        r.act("scalar", tln[:], bank[:], AF.Ln, bias=C["eps"][:, 1:2])
        r.act("scalar", fbuf[:], tln[:], AF.Exp, scale=-0.5, sem=s["f"])
        r.wait("vector", s["f"], s["f"].n)
        if is_q:
            r.ew("vector", "scalar_tensor_tensor", dst[:, dstcol:dstcol + 512],
                 rt3[:], tempbc[:, head:head + 1], fbuf[:], OP.mult, OP.mult,
                 sem=s["qk"])
        else:
            r.ew("vector", "tensor_mul", dst[:, dstcol:dstcol + 512], rt3[:],
                 fbuf[:], sem=s["qk"])

    for b in range(NBLK):
        if b > 0 and ev.get("xblk"):
            r.wait("vector", *ev["xblk"])
        entry_block(r, nc, ap, pb, idf, xpan, xblk, b, (xin, tp, evd))
        ev["xblk_ready"] = (evd, evd.n)
        ln_block(r, lnbufs, (pb[2][:], pb[3][:]), sems, b, ev)
        ev["yblk"] = None
        if not PH1["kv"]:
            continue
        r.wait("tensor", s["ln_y"], s["ln_y"].n)
        for i in range(NCD):
            slot, wsem, lv = stream((ap["wkva_bf"], 0), i * 128, 1024)
            r.wait("tensor", wsem, lv)
            if i == 0 and ev.get("pj_banks"):
                r.wait("tensor", *ev["pj_banks"])
            for ts in range(4):
                r.mm(pb[ts][:], yblk[:, i * 512 + ts * 128: i * 512 + ts * 128 + 128],
                     ws[:, slot * 2048 + 512: slot * 2048 + 1024],
                     i == 0, i == NCD - 1)
            for kc in range(4):
                last = (kc == 3)
                r.mm(pb[4 + kc][:],
                     ws[:, slot * 2048 + kc * 128: slot * 2048 + (kc + 1) * 128],
                     yblk[:, i * 512: (i + 1) * 512],
                     i == 0, i == NCD - 1, sem=(s["pj"] if last else None))
            ev["ws%d" % slot] = (s["pj"], s["pj"].n)
        r.wait("scalar", s["pj"], s["pj"].n)
        for ts in range(4):
            g = b * 4 + ts
            r.act("scalar", vtm[:, g * 512:(g + 1) * 512], pb[ts][:], AF.Copy,
                  sem=s["ev"])
        pj_ev = (s["pj"], s["pj"].n)
        for kc in range(4):
            rope_norm(kc, pb[4 + kc], kbuf, kc * 2048 + b * 512, b, False, 0, pj_ev)
        ev["pj_banks"] = (s["qk"], s["qk"].n)
        if not PH1["q"]:
            continue
        for half in range(2):
            for i in range(NCD):
                slot, wsem, lv = stream((ap["wqa_bf"], half * 1024), i * 128, 1024)
                r.wait("tensor", wsem, lv)
                if i == 0:
                    r.wait("tensor", *ev["pj_banks"])
                    if half == 0:
                        r.wait("tensor", s["ev"], s["ev"].n)
                for qc8 in range(8):
                    r.mm(pb[qc8][:],
                         ws[:, slot * 2048 + qc8 * 128: slot * 2048 + (qc8 + 1) * 128],
                         yblk[:, i * 512:(i + 1) * 512],
                         i == 0, i == NCD - 1,
                         sem=(s["pj"] if qc8 == 7 else None))
                ev["ws%d" % slot] = (s["pj"], s["pj"].n)
            pj_ev = (s["pj"], s["pj"].n)
            for qc8 in range(8):
                h = half * 8 + qc8
                rope_norm(qc8, pb[qc8], qbuf, h * 512, b, True, h, pj_ev)
            ev["pj_banks"] = (s["qk"], s["qk"].n)
        ev["yblk"] = (s["pj"], s["pj"].n)
        kcs = list(range(max(0, 4 * b - 2), 4 * b + 4))
        if not PH1["sc"]:
            continue
        r.wait("tensor", s["qk"], s["qk"].n)
        for h in range(16):
            kvh = h // 4
            par = h % 2
            bS = [pb[0 + 4 * par], pb[1 + 4 * par]]
            bD = pb[2 + 4 * par]
            bO = pb[3 + 4 * par]
            if ev.get("sc_banks%d" % par):
                r.wait("tensor", *ev["sc_banks%d" % par])
            r.wait("scalar", s["pv"], s["pv"].n)
            for i, kc in enumerate(kcs):
                off, span = SPANS_A[kc - 4 * b]
                bank = bS[i % 2]
                if i >= 2:
                    r.wait("tensor", s["ex"], s["ex"].n - 1)
                r.mm(bank[:, 0:span],
                     kbuf[:, kvh * 2048 + kc * 128: kvh * 2048 + (kc + 1) * 128],
                     qbuf[:, h * 512 + off: h * 512 + off + span],
                     True, True, sem=s["sc"])
                r.wait("scalar", s["sc"], s["sc"].n)
                r.act("scalar", exps[:, i * 512 + off: i * 512 + off + span],
                      bank[:, 0:span], AF.Exp,
                      bias=tempbc[:, 16 + h:17 + h], sem=s["ex"])
            r.wait("vector", s["ex"], s["ex"].n)
            mops = []
            for i, kc in enumerate(kcs):
                d = kc - 4 * b
                if 0 <= d <= 3:
                    mops.append((i, d, tri0))
                if 0 <= d + 2 <= 3:
                    mops.append((i, d + 2, tri2))
            for mi, (i, j, tri) in enumerate(mops):
                r.ew("vector", "tensor_mul",
                     exps[:, i * 512 + j * 128: i * 512 + (j + 1) * 128],
                     exps[:, i * 512 + j * 128: i * 512 + (j + 1) * 128],
                     tri[:], sem=(s["mk"] if mi == len(mops) - 1 else None))
            r.wait("tensor", s["mk"], s["mk"].n)
            nl = len(kcs)
            r.mm(bD[:, 0:512], zbuf[:], yblk[:, 0:512], True, False)
            for i, kc in enumerate(kcs):
                off, span = SPANS_A[kc - 4 * b]
                r.mm(bD[:, off:off + span], ones[:],
                     exps[:, i * 512 + off: i * 512 + off + span],
                     False, i == nl - 1, sem=(s["dn"] if i == nl - 1 else None))
            r.mm(bO[:, 0:512], zbuf[:], yblk[:, 0:512], True, False)
            for i, kc in enumerate(kcs):
                off, span = SPANS_A[kc - 4 * b]
                r.mm(bO[:, off:off + span],
                     vtm[:, kc * 512 + kvh * 128: kc * 512 + (kvh + 1) * 128],
                     exps[:, i * 512 + off: i * 512 + off + span],
                     False, i == nl - 1, sem=(s["pv"] if i == nl - 1 else None))
            r.wait("vector", s["dn"], s["dn"].n)
            r.ew("vector", "tensor_copy", dcp[:], bD[:])
            r.ew("vector", "reciprocal", rcp[:], dcp[:])
            r.wait("vector", s["pv"], s["pv"].n)
            r.ew("vector", "tensor_mul", atto[:, h * 512:(h + 1) * 512], bO[:],
                 rcp[:], sem=s["at"])
            ev["sc_banks%d" % par] = (s["at"], s["at"].n)
        if not PH1["wo"]:
            continue
        r.wait("tensor", s["at"], s["at"].n)
        for half in range(2):
            for vc in range(NCD):
                slot, wsem, lv = stream((ap["woa_bf"], half * 1024), vc * 128, 1024)
                r.wait("tensor", wsem, lv)
                if vc == 0:
                    r.wait("tensor", s["res"], s["res"].n)
                for oc8 in range(8):
                    r.mm(pb[oc8][:],
                         ws[:, slot * 2048 + oc8 * 128: slot * 2048 + (oc8 + 1) * 128],
                         atto[:, vc * 512:(vc + 1) * 512],
                         vc == 0, vc == NCD - 1,
                         sem=(s["wo"] if oc8 == 7 else None))
                ev["ws%d" % slot] = (s["wo"], s["wo"].n)
            r.wait("vector", s["wo"], s["wo"].n)
            for oc8 in range(8):
                oc = half * 8 + oc8
                r.ew("vector", "tensor_add", xblk[:, oc * 512:(oc + 1) * 512],
                     xblk[:, oc * 512:(oc + 1) * 512], pb[oc8][:],
                     sem=(s["res"] if oc8 == 7 else None))
            ev["wo_banks"] = (s["res"], s["res"].n)
        ev["xblk"] = None
        r.wait("gpsimd", s["res"], s["res"].n)
        for dc in range(NCD):
            r.dma("gpsimd", ap["xscr"][dc * 128:(dc + 1) * 128, b * 512:(b + 1) * 512],
                  xblk[:, dc * 512:(dc + 1) * 512], sem=xout)
        ev["xblk"] = (xout, xout.n)
        if b < NBLK - 1:
            r.wait("tensor", s["res"], s["res"].n)
    r.wait("gpsimd", xout, xout.n)
    r.wait("sync", xout, xout.n)


def ffn_group(r, nc, ap, E, pb, ws_a, ws_b, C, tag):
    s = {k: r.sem(k) for k in
         ("ln_cast", "ln_sq", "ln_slot", "ln_mm", "ln_stat", "ln_rstd", "ln_y",
          "pj1", "pj3", "hp", "sl", "gt", "w2", "res")}
    xld = [r.sem("xld0"), r.sem("xld1")]
    wl = [r.sem("wl0"), r.sem("wl1")]
    wl3 = [r.sem("wl30"), r.sem("wl31")]
    xout = r.sem("xout")

    xblk = E(nc.sbuf_tensor(tag + "xblk", [128, NCD * 512], F32))
    yblk = E(nc.sbuf_tensor(tag + "yblk", [128, NCD * 512], BF16))
    xb16 = E(nc.sbuf_tensor(tag + "xb16", [128, 2 * 512], BF16))
    sq16 = E(nc.sbuf_tensor(tag + "sq16", [128, 2 * 512], BF16))
    rstd = E(nc.sbuf_tensor(tag + "rstd", [128, 512], F32))
    negmr = E(nc.sbuf_tensor(tag + "negmr", [128, 512], F32))
    lntmp = E(nc.sbuf_tensor(tag + "lntmp", [128, 512], F32))
    hbuf = E(nc.sbuf_tensor(tag + "hbuf", [128, NHC * 512], BF16))
    sgb = E(nc.sbuf_tensor(tag + "sgb", [128, 4 * 512], BF16))

    lnbufs = dict(xblk=xblk, yblk=yblk, xb16=xb16, sq16=sq16, rstd=rstd,
                  negmr=negmr, lntmp=lntmp, ones=C["ones"], eps=C["eps"])
    ev = {}
    wtile = [0, 0]

    def stream(buf, wls, bi, src, row0, nrows, col0, ready_key):
        w = wtile[bi]; wtile[bi] += 1
        slot = w % 2
        if ev.get(ready_key + str(slot)):
            r.wait("sync", *ev[ready_key + str(slot)])
        r.dmat(buf[:, slot * 2048: slot * 2048 + nrows],
               src[row0:row0 + nrows, col0:col0 + 128], sem=wls[slot])
        return slot, wls[slot], wls[slot].n

    for b in range(NBLK):
        slot_b = b % 2
        if ev.get("xblk"):
            r.wait("sync", *ev["xblk"])
        for dc in range(NCD):
            r.dma("sync", xblk[:, dc * 512:(dc + 1) * 512],
                  ap["xscr"][dc * 128:(dc + 1) * 128, b * 512:(b + 1) * 512],
                  sem=xld[slot_b])
        ev["xblk_ready"] = (xld[slot_b], xld[slot_b].n)
        r.wait("vector", xld[slot_b], xld[slot_b].n)
        ln_block(r, lnbufs, (pb[2][:], pb[3][:]), s, b, ev)
        r.wait("tensor", s["ln_y"], s["ln_y"].n)
        for hcg in range(16):
            for i in range(NCD):
                sl1, w1s, lv1 = stream(ws_a, wl, 0, ap["w1_bf"],
                                       hcg * 512, 512, i * 128, "wsa")
                sl3, w3s, lv3 = stream(ws_b, wl3, 1, ap["w3_bf"],
                                       hcg * 512, 512, i * 128, "wsb")
                r.wait("tensor", w1s, lv1)
                r.wait("tensor", w3s, lv3)
                if i == 0:
                    r.wait("tensor", s["gt"], s["gt"].n)
                for k in range(4):
                    r.mm(pb[k][:],
                         ws_a[:, sl1 * 2048 + k * 128: sl1 * 2048 + (k + 1) * 128],
                         yblk[:, i * 512:(i + 1) * 512],
                         i == 0, i == NCD - 1,
                         sem=(s["pj1"] if k == 3 else None))
                for k in range(4):
                    r.mm(pb[4 + k][:],
                         ws_b[:, sl3 * 2048 + k * 128: sl3 * 2048 + (k + 1) * 128],
                         yblk[:, i * 512:(i + 1) * 512],
                         i == 0, i == NCD - 1,
                         sem=(s["pj3"] if k == 3 else None))
                ev["wsa%d" % sl1] = (s["pj1"], s["pj1"].n)
                ev["wsb%d" % sl3] = (s["pj3"], s["pj3"].n)
            r.wait("scalar", s["pj3"], s["pj3"].n)
            for k in range(4):
                r.act("scalar", sgb[:, k * 512:(k + 1) * 512], pb[k][:], AF.Silu,
                      sem=s["sl"])
            r.wait("vector", s["sl"], s["sl"].n)
            for k in range(4):
                hc = hcg * 4 + k
                r.ew("vector", "tensor_mul", hbuf[:, hc * 512:(hc + 1) * 512],
                     sgb[:, k * 512:(k + 1) * 512], pb[4 + k][:], sem=s["gt"])
        r.wait("tensor", s["gt"], s["gt"].n)
        for ohalf in range(2):
            if ohalf == 1:
                r.wait("tensor", s["res"], s["res"].n)
            for hc in range(NHC):
                sl2, w2s, lv2 = stream(ws_a, wl, 0, ap["w2m"],
                                       ohalf * 1024, 1024, hc * 128, "wsa")
                r.wait("tensor", w2s, lv2)
                for oc8 in range(8):
                    r.mm(pb[oc8][:],
                         ws_a[:, sl2 * 2048 + oc8 * 128: sl2 * 2048 + (oc8 + 1) * 128],
                         hbuf[:, hc * 512:(hc + 1) * 512],
                         hc == 0, hc == NHC - 1,
                         sem=(s["w2"] if oc8 == 7 else None))
                ev["wsa%d" % sl2] = (s["w2"], s["w2"].n)
            r.wait("vector", s["w2"], s["w2"].n)
            for oc8 in range(8):
                oc = ohalf * 8 + oc8
                r.ew("vector", "tensor_add", xblk[:, oc * 512:(oc + 1) * 512],
                     xblk[:, oc * 512:(oc + 1) * 512], pb[oc8][:],
                     sem=(s["res"] if oc8 == 7 else None))
        r.wait("gpsimd", s["res"], s["res"].n)
        for dc in range(NCD):
            r.dma("gpsimd", ap["xscr"][dc * 128:(dc + 1) * 128, b * 512:(b + 1) * 512],
                  xblk[:, dc * 512:(dc + 1) * 512], sem=xout)
        ev["xblk"] = (xout, xout.n)
        if b < NBLK - 1:
            r.wait("tensor", s["res"], s["res"].n)
    r.wait("gpsimd", xout, xout.n)
    r.wait("sync", xout, xout.n)
    return xout


def g1c(r, nc, ap, E, pb, ws_a, ws_b, C):
    s = {k: r.sem(k) for k in
         ("ln_cast", "ln_sq", "ln_slot", "ln_mm", "ln_stat", "ln_rstd", "ln_y",
          "pj", "ev", "rp", "st")}
    xld = [r.sem("xld0"), r.sem("xld1")]
    wl = [r.sem("wl0"), r.sem("wl1")]
    sto = [r.sem("sto0"), r.sem("sto1")]

    xblk = E(nc.sbuf_tensor("1c_xblk", [128, NCD * 512], F32))
    yblk = E(nc.sbuf_tensor("1c_yblk", [128, NCD * 512], BF16))
    xb16 = E(nc.sbuf_tensor("1c_xb16", [128, 2 * 512], BF16))
    sq16 = E(nc.sbuf_tensor("1c_sq16", [128, 2 * 512], BF16))
    rstd = E(nc.sbuf_tensor("1c_rstd", [128, 512], F32))
    negmr = E(nc.sbuf_tensor("1c_negmr", [128, 512], F32))
    lntmp = E(nc.sbuf_tensor("1c_lntmp", [128, 512], F32))
    stg = E(nc.sbuf_tensor("1c_stg", [128, 2 * 8 * 512], BF16))
    ebuf = E(nc.sbuf_tensor("1c_ebuf", [128, 8 * 512], BF16))
    rt1 = E(nc.sbuf_tensor("1c_rt1", [128, 512], F32))
    rt2 = E(nc.sbuf_tensor("1c_rt2", [128, 512], F32))
    csb = E(nc.sbuf_tensor("1c_csb", [128, 2048], F32))
    snb = E(nc.sbuf_tensor("1c_snb", [128, 2048], F32))
    cskb = E(nc.sbuf_tensor("1c_cskb", [128, 2048], F32))
    snkb = E(nc.sbuf_tensor("1c_snkb", [128, 2048], F32))

    tbl = r.sem("tbl")
    for name, buf in (("cs_b", csb), ("sn_b", snb), ("csk_b", cskb), ("snk_b", snkb)):
        r.dma("sync", buf[:], ap[name], sem=tbl)
    r.wait("vector", tbl, tbl.n)

    lnbufs = dict(xblk=xblk, yblk=yblk, xb16=xb16, sq16=sq16, rstd=rstd,
                  negmr=negmr, lntmp=lntmp, ones=C["ones"], eps=C["eps"])
    ev = {}
    wtile = [0]

    def stream(src, row0):
        w = wtile[0]; wtile[0] += 1
        slot = w % 2
        if ev.get("ws%d" % slot):
            r.wait("sync", *ev["ws%d" % slot])
        r.dmat(ws_a[:, slot * 2048: slot * 2048 + 1024],
               src[0][row0:row0 + 1024, src[1]:src[1] + 128], sem=wl[slot])
        return slot, wl[slot], wl[slot].n

    groups = [
        (ap["wqb_bf"], 0, "qn", 0), (ap["wqb_bf"], 1024, "qn", 8),
        (ap["wqb_bf"], 2048, "qr", 0),
        (ap["wkb_bf"], 0, "kn", 0), (ap["wkb_bf"], 1024, "kn", 8),
        (ap["wkb_bf"], 2048, "kr", 0),
        (ap["wvb_bf"], 0, "v", 0), (ap["wvb_bf"], 1024, "v", 1),
    ]

    for b in range(NBLK):
        slot_b = b % 2
        if ev.get("xblk"):
            r.wait("sync", *ev["xblk"])
        for dc in range(NCD):
            r.dma("sync", xblk[:, dc * 512:(dc + 1) * 512],
                  ap["xscr"][dc * 128:(dc + 1) * 128, b * 512:(b + 1) * 512],
                  sem=xld[slot_b])
        ev["xblk_ready"] = (xld[slot_b], xld[slot_b].n)
        r.wait("vector", xld[slot_b], xld[slot_b].n)
        ln_block(r, lnbufs, (pb[2][:], pb[3][:]), s, b, ev)
        r.wait("tensor", s["ln_y"], s["ln_y"].n)
        for gi, (wsrc, row0, mode, arg) in enumerate(groups):
            gslot = gi % 2
            stgo = gslot * 8 * 512
            if mode == "v":
                for i in range(NCD):
                    slw, wsem, lv = stream((wsrc, i * 128), row0)
                    r.wait("tensor", wsem, lv)
                    if i == 0 and ev.get("pj_banks"):
                        r.wait("tensor", *ev["pj_banks"])
                    for tsub in range(4):
                        for vfb2 in range(2):
                            r.mm(pb[tsub * 2 + vfb2][:],
                                 yblk[:, i * 512 + tsub * 128: i * 512 + tsub * 128 + 128],
                                 ws_a[:, slw * 2048 + vfb2 * 512: slw * 2048 + (vfb2 + 1) * 512],
                                 i == 0, i == NCD - 1,
                                 sem=(s["pj"] if (tsub, vfb2) == (3, 1) else None))
                    ev["ws%d" % slw] = (s["pj"], s["pj"].n)
            else:
                for i in range(NCD):
                    slw, wsem, lv = stream((wsrc, i * 128), row0)
                    r.wait("tensor", wsem, lv)
                    if i == 0 and ev.get("pj_banks"):
                        r.wait("tensor", *ev["pj_banks"])
                    for c8 in range(8):
                        r.mm(pb[c8][:],
                             ws_a[:, slw * 2048 + c8 * 128: slw * 2048 + (c8 + 1) * 128],
                             yblk[:, i * 512:(i + 1) * 512],
                             i == 0, i == NCD - 1,
                             sem=(s["pj"] if c8 == 7 else None))
                    ev["ws%d" % slw] = (s["pj"], s["pj"].n)
            pj_ev = (s["pj"], s["pj"].n)
            if ev.get("stg%d" % gslot):
                r.wait("scalar", *ev["stg%d" % gslot])
                r.wait("vector", *ev["stg%d" % gslot])
            if mode in ("qn", "kn", "v"):
                r.wait("scalar", s["pj"], s["pj"].n)
                sc = RSQ_DN if mode == "kn" else 1.0
                for c8 in range(8):
                    r.act("scalar", stg[:, stgo + c8 * 512: stgo + (c8 + 1) * 512],
                          pb[c8][:], AF.Copy, scale=sc, sem=s["ev"])
                r.wait("gpsimd", s["ev"], s["ev"].n)
            else:
                for c8 in range(8):
                    r.wait("vector", *pj_ev)
                    if ev.get("eb%d" % c8):
                        r.wait("vector", *ev["eb%d" % c8])
                    r.ew("vector", "tensor_copy", ebuf[:, c8 * 512:(c8 + 1) * 512],
                         pb[c8][:], sem=s["ev"])
                    r.wait("tensor", s["ev"], s["ev"].n)
                    r.mm(pb[c8][:], C["p32"][:], ebuf[:, c8 * 512:(c8 + 1) * 512],
                         True, True, sem=s["rp"])
                    ev["eb%d" % c8] = (s["rp"], s["rp"].n)
                    ct = csb if mode == "qr" else cskb
                    st_ = snb if mode == "qr" else snkb
                    r.wait("vector", s["rp"], s["rp"].n)
                    r.ew("vector", "tensor_mul", rt1[:],
                         ebuf[:, c8 * 512:(c8 + 1) * 512],
                         ct[:, b * 512:(b + 1) * 512])
                    r.ew("vector", "tensor_mul", rt2[:], pb[c8][:],
                         st_[:, b * 512:(b + 1) * 512])
                    r.ew("vector", "tensor_add",
                         stg[:, stgo + c8 * 512: stgo + (c8 + 1) * 512],
                         rt1[:], rt2[:], sem=s["ev"])
                r.wait("gpsimd", s["ev"], s["ev"].n)
            ev["pj_banks"] = (s["ev"], s["ev"].n)
            for c8 in range(8):
                srcap = stg[:, stgo + c8 * 512: stgo + (c8 + 1) * 512]
                if mode == "qn":
                    ch = arg + c8
                    dst = bass.AP(ap["qscr"].tensor, ch * 128 * S + b * 512,
                                  [[S, 128], [1, 512]])
                elif mode == "qr":
                    ch = 16 + c8
                    dst = bass.AP(ap["qscr"].tensor, ch * 128 * S + b * 512,
                                  [[S, 128], [1, 512]])
                elif mode in ("kn", "kr"):
                    colb = (2048 + c8 * 128) if mode == "kr" else (arg + c8) * 128
                    dst = bass.AP(ap["kscr"].tensor, b * 4 * 128 * 3072 + colb,
                                  [[3072, 128], [128 * 3072, 4], [1, 128]])
                else:
                    tsub = c8 // 2; vfb2 = c8 % 2
                    kc = b * 4 + tsub
                    colb = arg * 1024 + vfb2 * 512
                    dst = bass.AP(ap["vscr"].tensor, kc * 128 * 2048 + colb,
                                  [[2048, 128], [1, 512]])
                r.dma("gpsimd", dst, srcap, sem=sto[gslot])
            ev["stg%d" % gslot] = (sto[gslot], sto[gslot].n)
        ev["xblk"] = None
        ev["yblk"] = (s["pj"], s["pj"].n)
    r.wait("gpsimd", sto[0], sto[0].n)
    r.wait("gpsimd", sto[1], sto[1].n)
    r.wait("sync", sto[0], sto[0].n)
    r.wait("sync", sto[1], sto[1].n)


def g2a(r, nc, ap, E, pb, ws_a, c_seld, c_ones):
    s = {k: r.sem(k) for k in
         ("ql", "sc", "ex", "pvc", "oc", "dn", "dloc", "rc", "exf", "at",
          "wo", "res", "zp")}
    xld = [r.sem("xld0"), r.sem("xld1")]
    kld = [r.sem("kld0"), r.sem("kld1")]
    wl = [r.sem("wl0"), r.sem("wl1")]
    xout = r.sem("xout")

    qbn = E(nc.sbuf_tensor("2a_qbn", [128, 16 * 512], BF16))
    qrp = E(nc.sbuf_tensor("2a_qrp", [128, 16 * 512], BF16))
    kbb = E(nc.sbuf_tensor("2a_kbb", [128, 8 * 3072], BF16))
    vbb = E(nc.sbuf_tensor("2a_vbb", [128, 8 * 2048], BF16))
    expt = E(nc.sbuf_tensor("2a_expt", [128, 2 * 512], BF16))
    obuf = E(nc.sbuf_tensor("2a_obuf", [128, 16 * 512], BF16))
    dloc = E(nc.sbuf_tensor("2a_dloc", [16, 512], F32))
    drec = E(nc.sbuf_tensor("2a_drec", [16, 512], F32))
    xblk = E(nc.sbuf_tensor("2a_xblk", [128, NCD * 512], F32))
    expd = E(nc.sbuf_tensor("2a_expd", [16, 2048], F32))
    dbgs = E(nc.sbuf_tensor("2a_dbgs", [128, 512], F32)) if DBG_WO else None
    dbsem = r.sem("dbsem") if DBG_WO else None
    dbsd = r.sem("dbsd") if DBG_WO else None

    tbl = r.sem("tbl")
    r.dma("sync", expd[:], ap["expd"], sem=tbl)
    r.wait("tensor", tbl, tbl.n)
    for h in range(16):
        if h % 2 == 0:
            r.ew("vector", "memset", qrp[64:128, h * 512:(h + 1) * 512], 0.0,
                 sem=(s["zp"] if h == 15 else None))
        else:
            r.ew("vector", "memset", qrp[0:64, h * 512:(h + 1) * 512], 0.0,
                 sem=(s["zp"] if h == 15 else None))

    ev = {}
    wtile = [0]

    def stream(row0, col0):
        w = wtile[0]; wtile[0] += 1
        slot = w % 2
        if ev.get("ws%d" % slot):
            r.wait("sync", *ev["ws%d" % slot])
        r.dmat(ws_a[:, slot * 2048: slot * 2048 + 1024],
               ap["wob_bf"][row0:row0 + 1024, col0:col0 + 128], sem=wl[slot])
        return slot, wl[slot], wl[slot].n

    for qb in range(NBLK):
        qslot = qb % 2
        if ev.get("qbuf"):
            r.wait("sync", *ev["qbuf"])
        for ch in range(16):
            r.dma("sync", qbn[:, ch * 512:(ch + 1) * 512],
                  bass.AP(ap["qscr"].tensor, ch * 128 * S + qb * 512,
                          [[S, 128], [1, 512]]), sem=s["ql"])
        for c in range(8):
            r.dma("sync", qrp[0:64, (2 * c) * 512:(2 * c + 1) * 512],
                  bass.AP(ap["qscr"].tensor, (16 + c) * 128 * S + qb * 512,
                          [[S, 64], [1, 512]]), sem=s["ql"])
            r.dma("sync", qrp[64:128, (2 * c + 1) * 512:(2 * c + 2) * 512],
                  bass.AP(ap["qscr"].tensor, (16 + c) * 128 * S + 64 * S + qb * 512,
                          [[S, 64], [1, 512]]), sem=s["ql"])
        if ev.get("xblk"):
            r.wait("sync", *ev["xblk"])
        for dc in range(NCD):
            r.dma("sync", xblk[:, dc * 512:(dc + 1) * 512],
                  ap["xscr"][dc * 128:(dc + 1) * 128, qb * 512:(qb + 1) * 512],
                  sem=xld[qslot])
        ql_ev = (s["ql"], s["ql"].n)
        for kh in range(2):
            kslot = kh
            if ev.get("kvb"):
                r.wait("sync", *ev["kvb"])
            for kc8 in range(8):
                kc = kh * 8 + kc8
                r.dma("sync", kbb[:, kc8 * 3072:(kc8 + 1) * 3072],
                      bass.AP(ap["kscr"].tensor, kc * 128 * 3072,
                              [[3072, 128], [1, 3072]]), sem=kld[kslot])
                r.dma("sync", vbb[:, kc8 * 2048:(kc8 + 1) * 2048],
                      bass.AP(ap["vscr"].tensor, kc * 128 * 2048,
                              [[2048, 128], [1, 2048]]), sem=kld[kslot])
            r.wait("tensor", kld[kslot], kld[kslot].n)
            r.wait("tensor", *ql_ev)
            if kh == 0:
                r.wait("tensor", s["zp"], s["zp"].n)
            r.wait("tensor", s["dloc"], s["dloc"].n)
            bD = pb[4 + kh]
            ii = 0
            for h in range(16):
                bS = [pb[0], pb[1]]
                bO = pb[2 + (h % 2)]
                if s["oc"].n >= 1:
                    r.wait("tensor", s["oc"], s["oc"].n - 1)
                for kc8 in range(8):
                    bank = bS[ii % 2]
                    if ev.get("sb%d" % (ii % 2)):
                        r.wait("tensor", *ev["sb%d" % (ii % 2)])
                    r.mm(bank[:], kbb[:, kc8 * 3072 + h * 128: kc8 * 3072 + (h + 1) * 128],
                         qbn[:, h * 512:(h + 1) * 512], True, False)
                    r.mm(bank[:],
                         kbb[:, kc8 * 3072 + 2048 + (h // 2) * 128: kc8 * 3072 + 2048 + (h // 2 + 1) * 128],
                         qrp[:, h * 512:(h + 1) * 512], False, True, sem=s["sc"])
                    r.wait("scalar", s["sc"], s["sc"].n)
                    if ev.get("et%d" % (ii % 2)):
                        r.wait("scalar", *ev["et%d" % (ii % 2)])
                    r.act("scalar", expt[:, (ii % 2) * 512: (ii % 2 + 1) * 512],
                          bank[:], AF.Exp, sem=s["ex"])
                    ev["sb%d" % (ii % 2)] = (s["ex"], s["ex"].n)
                    r.wait("tensor", s["ex"], s["ex"].n)
                    r.mm(bD[0:16, :], c_seld[:, h * 16:(h + 1) * 16],
                         expt[:, (ii % 2) * 512: (ii % 2 + 1) * 512],
                         ii == 0, ii == 127, sem=(s["dn"] if ii == 127 else None))
                    r.mm(bO[:], vbb[:, kc8 * 2048 + h * 128: kc8 * 2048 + (h + 1) * 128],
                         expt[:, (ii % 2) * 512: (ii % 2 + 1) * 512],
                         kc8 == 0, kc8 == 7,
                         sem=(s["pvc"]))
                    ev["et%d" % (ii % 2)] = (s["pvc"], s["pvc"].n)
                    ii += 1
                r.wait("vector", s["pvc"], s["pvc"].n)
                if kh == 0:
                    r.ew("vector", "tensor_copy", obuf[:, h * 512:(h + 1) * 512],
                         bO[:], sem=s["oc"])
                else:
                    r.ew("vector", "tensor_add", obuf[:, h * 512:(h + 1) * 512],
                         obuf[:, h * 512:(h + 1) * 512], bO[:], sem=s["oc"])
                if h >= 2:
                    pass
                ev["ob%d" % (h % 2)] = (s["oc"], s["oc"].n)
            r.wait("vector", s["dn"], s["dn"].n)
            if kh == 0:
                r.ew("vector", "tensor_copy", dloc[:], bD[0:16, :], sem=s["dloc"])
            else:
                r.ew("vector", "tensor_add", dloc[:], dloc[:], bD[0:16, :],
                     sem=s["dloc"])
            ev["kvb"] = (s["oc"], s["oc"].n)
        r.wait("vector", s["dloc"], s["dloc"].n)
        r.ew("vector", "reciprocal", drec[:], dloc[:], sem=s["rc"])
        r.wait("tensor", s["rc"], s["rc"].n)
        for h in range(16):
            bE = pb[6 + (h % 2)]
            if ev.get("be%d" % (h % 2)):
                r.wait("tensor", *ev["be%d" % (h % 2)])
            r.mm(bE[:], expd[:, h * 128:(h + 1) * 128], drec[:], True, True,
                 sem=s["exf"])
            r.wait("vector", s["exf"], s["exf"].n)
            r.ew("vector", "tensor_mul", obuf[:, h * 512:(h + 1) * 512],
                 obuf[:, h * 512:(h + 1) * 512], bE[:], sem=s["at"])
            ev["be%d" % (h % 2)] = (s["at"], s["at"].n)
        ev["qbuf"] = (s["at"], s["at"].n)
        r.wait("tensor", s["at"], s["at"].n)
        r.wait("tensor", xld[qslot], xld[qslot].n)
        for half in range(2):
            for vc in range(NCD):
                slw, wsem, lv = stream(half * 1024, vc * 128)
                r.wait("tensor", wsem, lv)
                if vc == 0:
                    r.wait("tensor", s["res"], s["res"].n)
                for oc8 in range(8):
                    r.mm(pb[oc8][:],
                         ws_a[:, slw * 2048 + oc8 * 128: slw * 2048 + (oc8 + 1) * 128],
                         obuf[:, vc * 512:(vc + 1) * 512],
                         vc == 0, vc == NCD - 1,
                         sem=(s["wo"] if oc8 == 7 else None))
                ev["ws%d" % slw] = (s["wo"], s["wo"].n)
            r.wait("vector", s["wo"], s["wo"].n)
            r.wait("vector", xld[qslot], xld[qslot].n)
            for oc8 in range(8):
                oc = half * 8 + oc8
                if DBG_WO:
                    r.ew("vector", "tensor_copy", dbgs[:], pb[oc8][:], sem=dbsem)
                    r.wait("gpsimd", dbsem, dbsem.n)
                    r.dma("gpsimd", ap["dbg"][oc * 128:(oc + 1) * 128,
                                              qb * 512:(qb + 1) * 512],
                          dbgs[:], sem=dbsd)
                    r.wait("vector", dbsd, dbsd.n)
                r.ew("vector", "tensor_add", xblk[:, oc * 512:(oc + 1) * 512],
                     xblk[:, oc * 512:(oc + 1) * 512], pb[oc8][:],
                     sem=(s["res"] if oc8 == 7 else None))
        r.wait("gpsimd", s["res"], s["res"].n)
        for dc in range(NCD):
            r.dma("gpsimd", ap["xscr"][dc * 128:(dc + 1) * 128, qb * 512:(qb + 1) * 512],
                  xblk[:, dc * 512:(dc + 1) * 512], sem=xout)
        ev["xblk"] = (xout, xout.n)
        if qb < NBLK - 1:
            r.wait("tensor", s["res"], s["res"].n)
    r.wait("gpsimd", xout, xout.n)
    r.wait("sync", xout, xout.n)



def g2b_exit(r, nc, ap, E, pb, c_idf):
    xin = [r.sem("xld0"), r.sem("xld1")]
    tp = r.sem("tp")
    evd = r.sem("evd")
    ost = r.sem("xout")

    xcol = E(nc.sbuf_tensor("2b_xcol", [128, 2 * 2048], F32))
    opan = E(nc.sbuf_tensor("2b_opan", [128, 2 * 2048], F32))

    tp_base = tp.n
    evd_base = evd.n
    ost_base = ost.n
    for g in range(16):
        slot = g % 2
        if g >= 2:
            r.wait("sync", tp, tp_base + (g - 1) * NCD)
        for dc in range(NCD):
            r.dma("sync", xcol[:, slot * 2048 + dc * 128: slot * 2048 + (dc + 1) * 128],
                  ap["xscr"][dc * 128:(dc + 1) * 128, g * 128:(g + 1) * 128],
                  sem=xin[slot])
        r.wait("tensor", xin[slot], xin[slot].n)
        for dc in range(NCD):
            bank = pb[dc % 2]
            j = g * NCD + dc
            if j >= 2:
                r.wait("tensor", evd, evd_base + j - 1)
            r.tr(bank[:, 0:128],
                 xcol[:, slot * 2048 + dc * 128: slot * 2048 + (dc + 1) * 128],
                 c_idf[:], sem=tp)
            r.wait("vector", tp, tp.n)
            if g >= 2 and dc == 0:
                r.wait("vector", ost, ost_base + 16 * (g - 1))
            r.ew("vector", "tensor_copy",
                 opan[:, slot * 2048 + dc * 128: slot * 2048 + (dc + 1) * 128],
                 bank[:, 0:128], sem=evd)
        r.wait("gpsimd", evd, evd.n)
        r.dma("gpsimd", ap["out"][g * 128:(g + 1) * 128, :],
              opan[:, slot * 2048:(slot + 1) * 2048], sem=ost)
    r.wait("gpsimd", ost, ost.n)
    r.wait("sync", ost, ost.n)



_NC_CACHE = [None]

def kernel(L0_attn_a_x, L0_attn_a_Wq, L0_attn_a_Wkv, L0_attn_a_Wo,
           L0_attn_a_qk_temp, L0_attn_b_Wq, L0_attn_b_Wk, L0_attn_b_Wv,
           L0_attn_b_Wo, L0_ffn_W1, L0_ffn_W2, L0_ffn_W3):
    x = np.ascontiguousarray(np.asarray(L0_attn_a_x, dtype=np.float32))
    shared = {
        "wqa": np.ascontiguousarray(L0_attn_a_Wq, dtype=np.float32),
        "wkva": np.ascontiguousarray(L0_attn_a_Wkv, dtype=np.float32),
        "woa": np.ascontiguousarray(L0_attn_a_Wo, dtype=np.float32),
        "temp": np.ascontiguousarray(np.asarray(L0_attn_a_qk_temp, dtype=np.float32).reshape(16, 1)),
        "wqb": np.ascontiguousarray(L0_attn_b_Wq, dtype=np.float32),
        "wkb": np.ascontiguousarray(L0_attn_b_Wk, dtype=np.float32),
        "wvb": np.ascontiguousarray(L0_attn_b_Wv, dtype=np.float32),
        "wob": np.ascontiguousarray(L0_attn_b_Wo, dtype=np.float32),
        "w1": np.ascontiguousarray(L0_ffn_W1, dtype=np.float32),
        "w2": np.ascontiguousarray(L0_ffn_W2, dtype=np.float32),
        "w3": np.ascontiguousarray(L0_ffn_W3, dtype=np.float32),
    }
    shared.update(_CONSTS)
    if _NC_CACHE[0] is None:
        _NC_CACHE[0] = build_program()
    nc = _NC_CACHE[0]
    in_maps = [dict(shared, x=x[i]) for i in range(8)]
    res = run_bass_kernel_spmd(nc, in_maps, core_ids=list(range(8)))
    return np.stack([np.asarray(res.results[i]["out"]) for i in range(8)], axis=0)


# version 04
# speedup vs baseline: 1.1993x; 1.1993x over previous
import contextlib
import math
import numpy as np
import ml_dtypes
import concourse.bass as bass
import concourse.mybir as mybir
from concourse.bass_utils import run_bass_kernel_spmd

F32 = mybir.dt.float32
BF16 = mybir.dt.bfloat16
FP16 = mybir.dt.float16
AF = mybir.ActivationFunctionType
OP = mybir.AluOpType
BFD = ml_dtypes.bfloat16

ENGS = ("sync", "tensor", "vector", "scalar", "gpsimd")

S = 2048
DM = 2048
NCD = 16
TB = 512
NBLK = 4
TCH = 4
HA = 16
HKV = 4
WIN = 256
HBH = 16
DN = 128
DR = 64
DV = 128
HID = 8192
NHC = 64
RSQ_DN = 1.0 / math.sqrt(128.0)
FWHT_SC = 1.0 / math.sqrt(8192.0)


class Sem:
    __slots__ = ("h", "n")
    def __init__(self, h):
        self.h = h
        self.n = 0


class Streamer:
    def __init__(self, r, buf, nslots, span, eng, name):
        self.r = r
        self.buf = buf
        self.nslots = nslots
        self.span = span
        self.eng = eng
        self.sems = [r.sem(f"{name}{i}") for i in range(nslots)]
        self.free = [None] * nslots
        self.w = 0

    def load(self, src_ap, ncols):
        slot = self.w % self.nslots
        self.w += 1
        if self.free[slot] is not None:
            self.r.wait(self.eng, *self.free[slot])
        self.r.dmat(self.buf[:, slot * self.span: slot * self.span + ncols],
                    src_ap, sem=self.sems[slot], eng=self.eng)
        return slot, self.sems[slot], self.sems[slot].n

    def view(self, slot, c0, c1):
        return self.buf[:, slot * self.span + c0: slot * self.span + c1]

    def set_free(self, slot, sem):
        self.free[slot] = (sem, sem.n)


class Rec:
    def __init__(self, nc, stack):
        self.nc = nc
        self.ops = {e: [] for e in ENGS}
        self.stack = stack
        self._semid = 0

    def sem(self, name=None):
        self._semid += 1
        if name is None:
            name = f"s{self._semid}"
        if not hasattr(self, "_semcache"):
            self._semcache = {}
        if name not in self._semcache:
            h = self.stack.enter_context(self.nc.semaphore(name))
            self._semcache[name] = Sem(h)
        return self._semcache[name]

    def emit(self, eng, fn):
        self.ops[eng].append(fn)

    def wait(self, eng, sem, val=None):
        v = sem.n if val is None else val
        if v <= 0:
            return
        self.emit(eng, lambda e, h=sem.h, v=v: e.wait_ge(h, v))

    def inc(self, eng, sem, k=1):
        sem.n += k
        self.emit(eng, lambda e, h=sem.h, k=k: e.nop().then_inc(h, k))

    def _fin(self, inst, sem, k):
        if sem is not None:
            inst.then_inc(sem.h, k)

    def dma(self, eng, out, in_, sem=None, **kw):
        if sem is not None:
            sem.n += 16
        def f(e, out=out, in_=in_, s=sem, kw=kw):
            i = e.dma_start(out=out, in_=in_, **kw)
            if s is not None:
                i.then_inc(s.h, 16)
        self.emit(eng, f)

    def dmat(self, out, in_, sem=None, eng="sync"):
        if sem is not None:
            sem.n += 16
        def f(e, out=out, in_=in_, s=sem):
            i = e.dma_start_transpose(out, in_)
            if s is not None:
                i.then_inc(s.h, 16)
        self.emit(eng, f)

    def mm(self, out, lhsT, rhs, start, stop, sem=None, k=1):
        if sem is not None:
            sem.n += k
        def f(e, out=out, lhsT=lhsT, rhs=rhs, st=start, sp=stop, s=sem, k=k):
            i = e.matmul(out, lhsT, rhs, start=st, stop=sp)
            if s is not None:
                i.then_inc(s.h, k)
        self.emit("tensor", f)

    def tr(self, out, in_, iden, sem=None):
        if sem is not None:
            sem.n += 1
        def f(e, out=out, in_=in_, iden=iden, s=sem):
            i = e.transpose(out, in_, iden)
            if s is not None:
                i.then_inc(s.h, 1)
        self.emit("tensor", f)

    def act(self, eng, out, in_, func, bias=0.0, scale=1.0, sem=None):
        if sem is not None:
            sem.n += 1
        def f(e, out=out, in_=in_, func=func, b=bias, sc=scale, s=sem):
            i = e.activation(out, in_, func, bias=b, scale=sc)
            if s is not None:
                i.then_inc(s.h, 1)
        self.emit(eng, f)

    def ew(self, eng, name, *args, sem=None, **kw):
        if sem is not None:
            sem.n += 1
        def f(e, name=name, args=args, s=sem, kw=kw):
            i = getattr(e, name)(*args, **kw)
            if s is not None:
                i.then_inc(s.h, 1)
        self.emit(eng, f)

    def barrier(self, bsem, engines=ENGS):
        base = bsem.n
        for e in engines:
            self.emit(e, lambda en, h=bsem.h: en.nop().then_inc(h, 1))
        bsem.n = base + len(engines)
        for e in engines:
            self.wait(e, bsem)

    def replay(self):
        with self.nc.Block() as blk:
            mk = {}
            def bind(ename):
                ops = self.ops[ename]
                def runner(eng):
                    for f in ops:
                        f(eng)
                return runner
            blk.sync(bind("sync"))
            blk.tensor(bind("tensor"))
            blk.vector(bind("vector"))
            blk.scalar(bind("scalar"))
            blk.gpsimd(bind("gpsimd"))
        self.ops = {e: [] for e in ENGS}



def _rope_tables_a():
    inv = 1.0 / (10000.0 ** (np.arange(0, 128, 2, dtype=np.float64) / 128.0))
    pos = np.arange(S, dtype=np.float64)
    ang = pos[None, :] * inv[:, None]
    c = np.cos(ang); s = np.sin(ang)
    C = np.concatenate([c, c], axis=0).astype(np.float32)
    Sp = np.concatenate([-s, s], axis=0).astype(np.float32)
    return C, Sp


def _rope_tables_b():
    inv = 1.0 / (10000.0 ** (np.arange(0, 64, 2, dtype=np.float64) / 64.0))
    pos = np.arange(S, dtype=np.float64)
    ang = pos[None, :] * inv[:, None]
    c = np.cos(ang); s = np.sin(ang)
    C = np.concatenate([c, c, c, c], axis=0).astype(np.float32)
    Sp = np.concatenate([-s, s, -s, s], axis=0).astype(np.float32)
    return C, Sp


def _consts():
    d = {}
    C_A, S_A = _rope_tables_a()
    C_B, S_B = _rope_tables_b()
    d["cs_a"] = C_A; d["sn_a"] = S_A
    d["cs_b"] = C_B; d["sn_b"] = S_B
    d["csk_b"] = (C_B * 0.125).astype(np.float32)
    d["snk_b"] = (S_B * 0.125).astype(np.float32)
    ii = np.arange(128)
    d["p64"] = np.eye(128, dtype=np.float32)[ii ^ 64].T.astype(BFD)
    d["p32"] = np.eye(128, dtype=np.float32)[ii ^ 32].T.astype(BFD)
    ki = ii[:, None]; qi = ii[None, :]
    d["tri0"] = (ki <= qi).astype(np.float32).astype(BFD)
    d["tri2"] = (ki > qi).astype(np.float32).astype(BFD)
    d["ones"] = np.ones((128, 128), dtype=np.float32).astype(BFD)
    d["idb"] = np.eye(128, dtype=np.float32).astype(BFD)
    d["idf"] = np.eye(128, dtype=np.float32)
    d["h64"] = ((-1.0) ** np.array([[bin(a & b).count("1") for b in range(64)]
                                    for a in range(64)])).astype(np.float32).astype(BFD)
    d["h128"] = ((-1.0) ** np.array([[bin(a & b).count("1") for b in range(128)]
                                     for a in range(128)])).astype(np.float32).astype(np.float16)
    seld = np.zeros((128, 16 * 16), dtype=np.float32)
    for h in range(16):
        seld[:, h * 16 + h] = 1.0
    d["seld"] = seld.astype(BFD)
    expd = np.zeros((16, 16 * 128), dtype=np.float32)
    for h in range(16):
        expd[h, h * 128:(h + 1) * 128] = 1.0
    d["expd"] = expd
    return d


_CONSTS = _consts()



def build_program():
    nc = bass.Bass("TRN2", target_bir_lowering=False,
                   detect_race_conditions=False)

    ap = {}
    def din(name, shape, dt):
        ap[name] = nc.dram_tensor(name, list(shape), dt, kind="ExternalInput").ap()
    def dscr(name, shape, dt):
        ap[name] = nc.dram_tensor(name, list(shape), dt, kind="Internal").ap()

    din("x", (S, DM), F32)
    din("wqa", (2048, 2048), F32)
    din("wkva", (1024, 2048), F32)
    din("woa", (2048, 2048), F32)
    din("temp", (16, 1), F32)
    din("wqb", (3072, 2048), F32)
    din("wkb", (3072, 2048), F32)
    din("wvb", (2048, 2048), F32)
    din("wob", (2048, 2048), F32)
    din("w1", (8192, 2048), F32)
    din("w2", (2048, 8192), F32)
    din("w3", (8192, 2048), F32)
    for n, v in _CONSTS.items():
        dt = F32 if v.dtype == np.float32 else (BF16 if v.dtype == BFD else FP16)
        din(n, v.shape, dt)
    ap["out"] = nc.dram_tensor("out", [S, DM], F32, kind="ExternalOutput").ap()

    dscr("xscr", (DM, S), F32)
    dscr("wqa_bf", (2048, 2048), BF16)
    dscr("wkva_bf", (1024, 2048), BF16)
    dscr("woa_bf", (2048, 2048), BF16)
    dscr("wqb_bf", (3072, 2048), BF16)
    dscr("wkb_bf", (3072, 2048), BF16)
    dscr("wvb_bf", (2048, 2048), BF16)
    dscr("wob_bf", (2048, 2048), BF16)
    dscr("w1_bf", (8192, 2048), BF16)
    dscr("w3_bf", (8192, 2048), BF16)
    dscr("s1h", (64, 2048 * 128), FP16)
    dscr("w2m", (2048, 8192), BF16)
    dscr("t16", (2, 16), F32)
    dscr("kscr", (16, 128, 3072), BF16)
    dscr("vscr", (16, 128, 2048), BF16)
    dscr("qscr", (24, 128, S), BF16)
    dscr("dbg", (DM, S), F32)

    stack = contextlib.ExitStack()
    with stack:
        r = Rec(nc, stack)
        E = stack.enter_context

        pb = [E(nc.psum_tensor(f"pb{i}", [128, 512], F32)) for i in range(8)]
        c_ones = E(nc.sbuf_tensor("c_ones", [128, 128], BF16))
        c_idb = E(nc.sbuf_tensor("c_idb", [128, 128], BF16))
        c_idf = E(nc.sbuf_tensor("c_idf", [128, 128], F32))
        c_tri0 = E(nc.sbuf_tensor("c_tri0", [128, 128], BF16))
        c_tri2 = E(nc.sbuf_tensor("c_tri2", [128, 128], BF16))
        c_p64 = E(nc.sbuf_tensor("c_p64", [128, 128], BF16))
        c_p32 = E(nc.sbuf_tensor("c_p32", [128, 128], BF16))
        c_seld = E(nc.sbuf_tensor("c_seld", [128, 256], BF16))
        c_tempbc = E(nc.sbuf_tensor("c_tempbc", [128, 32], F32))
        c_eps = E(nc.sbuf_tensor("c_eps", [128, 2], F32))
        ws_a = E(nc.sbuf_tensor("ws_a", [128, 2 * 2048], BF16))
        ws_b = E(nc.sbuf_tensor("ws_b", [128, 2 * 2048], BF16))

        bar = r.sem("bar")

        with contextlib.ExitStack() as sc:
            g0(r, nc, ap, sc.enter_context, pb,
               dict(ones=c_ones, idb=c_idb, idf=c_idf, tri0=c_tri0,
                    tri2=c_tri2, p64=c_p64, p32=c_p32, seld=c_seld,
                    tempbc=c_tempbc, eps=c_eps))
            r.barrier(bar)
            r.replay()

        with contextlib.ExitStack() as sc:
            g1a(r, nc, ap, sc.enter_context, pb,
                dict(ones=c_ones, idf=c_idf, tri0=c_tri0, tri2=c_tri2,
                     p64=c_p64, tempbc=c_tempbc, ws_a=ws_a, ws_b=ws_b,
                     eps=c_eps))
            r.barrier(bar)
            r.replay()

        cdict = dict(ones=c_ones, eps=c_eps)
        with contextlib.ExitStack() as sc:
            ffn_group(r, nc, ap, sc.enter_context, pb, ws_a, ws_b, cdict, "2f_")
            r.barrier(bar)
            r.replay()

        with contextlib.ExitStack() as sc:
            g1c(r, nc, ap, sc.enter_context, pb, ws_a, ws_b,
                dict(ones=c_ones, eps=c_eps, p32=c_p32))
            r.barrier(bar)
            r.replay()

        with contextlib.ExitStack() as sc:
            g2a(r, nc, ap, sc.enter_context, pb, ws_a, c_seld, c_ones)
            r.barrier(bar)
            r.replay()

        with contextlib.ExitStack() as sc:
            ffn_group(r, nc, ap, sc.enter_context, pb, ws_a, ws_b, cdict, "4f_")
            g2b_exit(r, nc, ap, sc.enter_context, pb, c_idf)
            r.barrier(bar)
            r.replay()

    return nc



def g0(r, nc, ap, E, pb, C):
    ld = r.sem("g0ld")
    cast = r.sem("g0cast")
    pe = r.sem("g0pe")
    ve = r.sem("g0ve")
    se = r.sem("g0se")
    stv = r.sem("g0st")

    for name, buf in (("ones", C["ones"]), ("idb", C["idb"]), ("idf", C["idf"]),
                      ("tri0", C["tri0"]), ("tri2", C["tri2"]), ("p64", C["p64"]),
                      ("p32", C["p32"]), ("seld", C["seld"])):
        r.dma("sync", buf[:], ap[name], sem=ld)

    r.ew("vector", "memset", C["eps"][:, 0:1], 1e-5)
    r.ew("vector", "memset", C["eps"][:, 1:2], 1e-24, sem=ve)
    tb1 = E(nc.sbuf_tensor("g0_tb1", [16, 2], F32))
    tbr = E(nc.sbuf_tensor("g0_tbr", [1, 32], BF16))
    onesrow = E(nc.sbuf_tensor("g0_or", [1, 32], BF16))
    r.dma("sync", tb1[:, 0:1], ap["temp"], sem=ld)
    n_ld = ld.n
    r.wait("scalar", ld, n_ld)
    r.act("scalar", tb1[:, 1:2], tb1[:, 0:1], AF.Abs, scale=1.0, sem=se)
    r.ew("scalar", "mul", tb1[:, 1:2], tb1[:, 1:2], -1.0, sem=se)
    tbs = r.sem("g0_tbs")
    r.wait("gpsimd", se, se.n)
    r.dma("gpsimd", bass.AP(ap["t16"].tensor, 0, [[1, 16], [16, 2]]), tb1[:, 0:2], sem=tbs,
          allow_slow_non_contiguous=True)
    r.wait("gpsimd", tbs, tbs.n)
    r.dma("gpsimd", tbr[:], bass.AP(ap["t16"].tensor, 0, [[32, 1], [1, 32]]), sem=tbs)
    r.wait("vector", ld, n_ld)
    r.ew("vector", "memset", onesrow[:], 1.0, sem=ve)
    r.wait("tensor", ld, n_ld)
    r.wait("tensor", tbs, tbs.n)
    r.wait("tensor", ve, ve.n)
    r.mm(pb[0][:, 0:32], C["ones"][0:1, 0:128], tbr[:], True, True, sem=pe)
    r.wait("vector", pe, pe.n)
    r.ew("vector", "tensor_copy", C["tempbc"][:, 0:32], pb[0][:, 0:32], sem=ve)

    cwa = r.sem("cwa")
    for srcw, dst in (("wqa", "wqa_bf"), ("wkva", "wkva_bf"), ("woa", "woa_bf")):
        n = ap[srcw].tensor.shape[0]
        half = n // 2
        r.dma("gpsimd", ap[dst][0:half, :], ap[srcw][0:half, :], sem=cwa)
        r.dma("gpsimd", ap[dst][half:n, :], ap[srcw][half:n, :], sem=cwa)

    h64 = E(nc.sbuf_tensor("g0_h64", [64, 64], BF16))
    h128 = E(nc.sbuf_tensor("g0_h128", [128, 128], FP16))
    ga = E(nc.sbuf_tensor("g0_ga", [64, 2 * 1024], BF16))
    ea = E(nc.sbuf_tensor("g0_ea", [64, 2 * 1024], FP16))
    ld2 = r.sem("g0ld2")
    r.dma("sync", h64[:], ap["h64"], sem=ld2)
    r.dma("sync", h128[:], ap["h128"], sem=ld2)
    pa_mm = r.sem("g0_pamm")
    pa_ev = r.sem("g0_paev")
    gas = [r.sem("g0_ga0"), r.sem("g0_ga1")]
    sas = [r.sem("g0_sa0"), r.sem("g0_sa1")]
    r.wait("tensor", ld2, ld2.n)
    NT_A = 256
    for i in range(NT_A):
        d0 = i * 8
        slot = i % 2
        if i >= 2:
            r.wait("gpsimd", pa_ev, i - 1)
        r.dma("gpsimd",
              ga[:, slot * 1024:(slot + 1) * 1024],
              bass.AP(ap["w2"].tensor, d0 * 8192, [[128, 64], [8192, 8], [1, 128]]),
              sem=gas[slot])
        bk = 4 + 2 * slot
        r.wait("tensor", gas[slot], gas[slot].n)
        if i >= 2:
            r.wait("tensor", pa_ev, i - 1)
        r.mm(pb[bk][0:64, :], h64[:], ga[0:64, slot * 1024: slot * 1024 + 512],
             True, True)
        r.mm(pb[bk + 1][0:64, :], h64[:],
             ga[0:64, slot * 1024 + 512: slot * 1024 + 1024], True, True, sem=pa_mm)
        r.wait("vector", pa_mm, pa_mm.n)
        if i >= 2:
            r.wait("vector", sas[slot], sas[slot].n)
        r.ew("vector", "tensor_copy", ea[0:64, slot * 1024: slot * 1024 + 512],
             pb[bk][0:64, :])
        r.ew("vector", "tensor_copy", ea[0:64, slot * 1024 + 512: slot * 1024 + 1024],
             pb[bk + 1][0:64, :], sem=pa_ev)
        r.wait("sync", pa_ev, pa_ev.n)
        r.dma("sync",
              bass.AP(ap["s1h"].tensor, d0 * 128, [[2048 * 128, 64], [128, 8], [1, 128]]),
              ea[0:64, slot * 1024:(slot + 1) * 1024], sem=sas[slot])

    xb = E(nc.sbuf_tensor("g0_xb", [128, 2 * 512], FP16))
    eb = E(nc.sbuf_tensor("g0_eb", [128, 2 * 512], BF16))
    xbs = [r.sem("g0_xb0"), r.sem("g0_xb1")]
    pb_mm = r.sem("g0_pbmm")
    pb_ev = r.sem("g0_pbev")
    sbs = [r.sem("g0_sb0"), r.sem("g0_sb1")]
    ti = 0
    for dblk in range(4):
        for cp in range(64):
            slot = ti % 2
            if ti >= 2:
                r.wait("sync", pb_ev, ti - 1)
            if cp == 0:
                r.wait("sync", sas[0], 16 * 32 * (dblk + 1))
                r.wait("sync", sas[1], 16 * 32 * (dblk + 1))
            srcap = bass.AP(ap["s1h"].tensor, cp * (2048 * 128) + dblk * 512 * 128,
                            [[128, 512], [1, 128]])
            r.dmat(xb[:, slot * 512:(slot + 1) * 512], srcap, sem=xbs[slot])
            bank = pb[4 + slot]
            r.wait("tensor", xbs[slot], xbs[slot].n)
            if ti < 2:
                r.wait("tensor", pa_ev, NT_A)
            else:
                r.wait("tensor", pb_ev, ti - 1)
            for k in range(4):
                r.mm(bank[:, k * 128:(k + 1) * 128],
                     xb[:, slot * 512 + k * 128: slot * 512 + (k + 1) * 128],
                     h128[:], k == 0, k == 3, sem=(pb_mm if k == 3 else None))
            r.wait("vector", pb_mm, pb_mm.n)
            if ti >= 2:
                r.wait("vector", sbs[slot], sbs[slot].n)
            r.ew("vector", "tensor_scalar_mul", eb[:, slot * 512:(slot + 1) * 512],
                 bank[:], FWHT_SC, sem=pb_ev)
            r.wait("gpsimd", pb_ev, pb_ev.n)
            for k in range(4):
                d0 = dblk * 512 + k * 128
                r.dma("gpsimd", ap["w2m"][d0:d0 + 128, cp * 128:(cp + 1) * 128],
                      eb[:, slot * 512 + k * 128: slot * 512 + (k + 1) * 128],
                      sem=sbs[slot])
            ti += 1

    r.wait("gpsimd", sbs[0], sbs[0].n)
    r.wait("gpsimd", sbs[1], sbs[1].n)
    r.wait("sync", sas[0], sas[0].n)
    r.wait("sync", sas[1], sas[1].n)



def entry_block(r, nc, ap, pb, idf, xpan, xblk, b, sems):
    xin, tp, evd = sems
    for tc in range(TCH):
        g = b * TCH + tc
        slot = g % 2
        if g >= 2:
            r.wait("sync", tp, (g - 1) * NCD)
        r.dma("sync", xpan[:, slot * 2048:(slot + 1) * 2048],
              ap["x"][g * 128:(g + 1) * 128, :], sem=xin[slot])
        r.wait("tensor", xin[slot], xin[slot].n)
        for dc in range(NCD):
            bank = pb[dc % 2]
            j = g * NCD + dc
            if j >= 2:
                r.wait("tensor", evd, j - 1)
            r.tr(bank[:, 0:128],
                 xpan[:, slot * 2048 + dc * 128: slot * 2048 + (dc + 1) * 128],
                 idf[:], sem=tp)
            r.wait("vector", tp, tp.n)
            r.ew("vector", "tensor_copy",
                 xblk[:, dc * 512 + tc * 128: dc * 512 + tc * 128 + 128],
                 bank[:, 0:128], sem=evd)


def ln_block(r, bufs, banks, sems, b, ev):
    xblk, yblk = bufs["xblk"], bufs["yblk"]
    xb16, sq16 = bufs["xb16"], bufs["sq16"]
    rstd, negmr, lntmp = bufs["rstd"], bufs["negmr"], bufs["lntmp"]
    bm, bs = banks
    s_cast, s_sq, s_slot, s_lnmm, s_stat, s_rstd = (
        sems["ln_cast"], sems["ln_sq"], sems["ln_slot"], sems["ln_mm"],
        sems["ln_stat"], sems["ln_rstd"])
    ones = bufs["ones"]
    if ev.get("xblk_ready"):
        r.wait("scalar", *ev["xblk_ready"])
    for dc in range(NCD):
        slot = dc % 2
        if ev.get("ln_slot%d" % slot):
            r.wait("vector", *ev["ln_slot%d" % slot])
            r.wait("scalar", *ev["ln_slot%d" % slot])
        if dc == 0 and ev.get("yblk"):
            r.wait("vector", *ev["yblk"])
        r.ew("vector", "tensor_copy", xb16[:, slot * 512:(slot + 1) * 512],
             xblk[:, dc * 512:(dc + 1) * 512], sem=s_cast)
        r.act("scalar", sq16[:, slot * 512:(slot + 1) * 512],
              xblk[:, dc * 512:(dc + 1) * 512], AF.Square, sem=s_sq)
        r.wait("tensor", s_cast, s_cast.n)
        r.wait("tensor", s_sq, s_sq.n)
        if dc == 0 and ev.get("ln_banks"):
            r.wait("tensor", *ev["ln_banks"])
        r.mm(bm[:], ones[:], xb16[:, slot * 512:(slot + 1) * 512],
             dc == 0, dc == NCD - 1)
        r.mm(bs[:], ones[:], sq16[:, slot * 512:(slot + 1) * 512],
             dc == 0, dc == NCD - 1,
             sem=(s_lnmm if dc == NCD - 1 else s_slot))
        if dc != NCD - 1:
            ev["ln_slot%d" % slot] = (s_slot, s_slot.n)
    r.wait("vector", s_lnmm, s_lnmm.n)
    r.ew("vector", "tensor_scalar_mul", lntmp[:], bm[:], 1.0 / 2048.0)
    r.ew("vector", "tensor_mul", rstd[:], lntmp[:], lntmp[:])
    r.ew("vector", "scalar_tensor_tensor", negmr[:], bs[:], 1.0 / 2048.0,
         rstd[:], OP.mult, OP.subtract, sem=s_stat)
    ev["ln_banks"] = (s_stat, s_stat.n)
    r.wait("scalar", s_stat, s_stat.n)
    r.act("scalar", rstd[:], negmr[:], AF.Ln, bias=bufs["eps"][:, 0:1])
    r.act("scalar", rstd[:], rstd[:], AF.Exp, scale=-0.5, sem=s_rstd)
    r.wait("vector", s_rstd, s_rstd.n)
    r.ew("vector", "scalar_tensor_tensor", negmr[:], lntmp[:], -1.0, rstd[:],
         OP.mult, OP.mult, sem=s_stat)
    for dc in range(NCD):
        r.ew("vector", "tensor_mul", lntmp[:], xblk[:, dc * 512:(dc + 1) * 512], rstd[:])
        r.ew("vector", "tensor_add", yblk[:, dc * 512:(dc + 1) * 512], lntmp[:],
             negmr[:], sem=(sems["ln_y"] if dc == NCD - 1 else None))


SPANS_A = {-2: (0, 128), -1: (0, 256), 0: (0, 384), 1: (128, 384),
           2: (256, 256), 3: (384, 128)}

PH1 = {"kv": True, "q": True, "sc": True, "wo": True}
DBG_WO = False


def g1a(r, nc, ap, E, pb, C):
    xin = [r.sem("xld0"), r.sem("xld1")]
    tp = r.sem("tp")
    evd = r.sem("evd")
    xout = r.sem("xout")
    sems = {k: r.sem(k) for k in
            ("ln_cast", "ln_sq", "ln_slot", "ln_mm", "ln_stat", "ln_rstd", "ln_y",
             "pj", "ev", "rp", "sq", "nm", "f", "qk",
             "sc", "ex", "mk", "dn", "pv", "at", "wo", "res")}
    s = sems

    xpan = E(nc.sbuf_tensor("1a_xpan", [128, 2 * 2048], F32))
    xblk = E(nc.sbuf_tensor("1a_xblk", [128, NCD * 512], F32))
    yblk = E(nc.sbuf_tensor("1a_yblk", [128, NCD * 512], BF16))
    xb16 = E(nc.sbuf_tensor("1a_xb16", [128, 2 * 512], BF16))
    sq16 = E(nc.sbuf_tensor("1a_sq16", [128, 2 * 512], BF16))
    rstd = E(nc.sbuf_tensor("1a_rstd", [128, 512], F32))
    negmr = E(nc.sbuf_tensor("1a_negmr", [128, 512], F32))
    lntmp = E(nc.sbuf_tensor("1a_lntmp", [128, 512], F32))
    kbuf = E(nc.sbuf_tensor("1a_kbuf", [128, HKV * 2048], BF16))
    vtm = E(nc.sbuf_tensor("1a_vtm", [128, 16 * 512], BF16))
    qbuf = E(nc.sbuf_tensor("1a_qbuf", [128, 16 * 512], BF16))
    ebuf = E(nc.sbuf_tensor("1a_ebuf", [128, 8 * 512], BF16))
    rt1 = E(nc.sbuf_tensor("1a_rt1", [128, 512], F32))
    rt2 = E(nc.sbuf_tensor("1a_rt2", [128, 512], F32))
    rt3 = E(nc.sbuf_tensor("1a_rt3", [128, 512], F32))
    sqf = E(nc.sbuf_tensor("1a_sqf", [128, 512], BF16))
    tln = E(nc.sbuf_tensor("1a_tln", [128, 512], F32))
    fbuf = E(nc.sbuf_tensor("1a_fbuf", [128, 512], F32))
    exps = E(nc.sbuf_tensor("1a_expS", [128, 6 * 512], BF16))
    zbuf = E(nc.sbuf_tensor("1a_zbuf", [128, 128], BF16))
    atto = E(nc.sbuf_tensor("1a_atto", [128, 16 * 512], BF16))
    rcp = E(nc.sbuf_tensor("1a_rcp", [128, 512], F32))
    dcp = E(nc.sbuf_tensor("1a_dcp", [128, 512], F32))
    csa = E(nc.sbuf_tensor("1a_csa", [128, 2048], F32))
    sna = E(nc.sbuf_tensor("1a_sna", [128, 2048], F32))
    ws = C["ws_a"]
    ones, idf = C["ones"], C["idf"]
    p64, tempbc = C["p64"], C["tempbc"]
    tri0, tri2 = C["tri0"], C["tri2"]

    lnbufs = dict(xblk=xblk, yblk=yblk, xb16=xb16, sq16=sq16, rstd=rstd,
                  negmr=negmr, lntmp=lntmp, ones=ones, eps=C["eps"])
    ev = {}

    zrd = r.sem("zrd")
    r.ew("vector", "memset", zbuf[:], 0.0, sem=zrd)
    r.wait("tensor", zrd, zrd.n)
    cwa = r.sem("cwa")
    cw12 = r.sem("cw12")
    cwb = r.sem("cwb")
    for srcw, dst, sm in (("w1", "w1_bf", cw12), ("w3", "w3_bf", cw12),
                          ("wqb", "wqb_bf", cwb), ("wkb", "wkb_bf", cwb),
                          ("wvb", "wvb_bf", cwb), ("wob", "wob_bf", cwb)):
        n = ap[srcw].tensor.shape[0]
        half = n // 2
        r.dma("gpsimd", ap[dst][0:half, :], ap[srcw][0:half, :], sem=sm)
        r.dma("gpsimd", ap[dst][half:n, :], ap[srcw][half:n, :], sem=sm)
    r.wait("sync", cwa, cwa.n)
    tbl = r.sem("tbl")
    r.dma("sync", csa[:], ap["cs_a"], sem=tbl)
    r.dma("sync", sna[:], ap["sn_a"], sem=tbl)
    r.wait("vector", tbl, tbl.n)
    wl = [r.sem("wl0"), r.sem("wl1")]
    wtile = [0]

    def stream(src_rows, src_cols, nrows):
        w = wtile[0]; wtile[0] += 1
        slot = w % 2
        if ev.get("ws%d" % slot):
            r.wait("sync", *ev["ws%d" % slot])
        r.dmat(ws[:, slot * 2048: slot * 2048 + nrows],
               src_rows[0][src_rows[1]:src_rows[1] + nrows,
                           src_cols:src_cols + 128], sem=wl[slot])
        return slot, wl[slot], wl[slot].n

    def rope_norm(cchunk, bank, dst, dstcol, b, is_q, head, ready_ev):
        ecol = (cchunk % 8) * 512
        r.wait("vector", *ready_ev)
        if ev.get("eb%d" % (cchunk % 8)):
            r.wait("vector", *ev["eb%d" % (cchunk % 8)])
        r.ew("vector", "tensor_copy", ebuf[:, ecol:ecol + 512], bank[:], sem=s["ev"])
        r.wait("tensor", s["ev"], s["ev"].n)
        r.mm(bank[:], p64[:], ebuf[:, ecol:ecol + 512], True, True, sem=s["rp"])
        ev["eb%d" % (cchunk % 8)] = (s["rp"], s["rp"].n)
        r.wait("vector", s["rp"], s["rp"].n)
        r.ew("vector", "tensor_mul", rt1[:], ebuf[:, ecol:ecol + 512],
             csa[:, b * 512:(b + 1) * 512])
        r.ew("vector", "tensor_mul", rt2[:], bank[:], sna[:, b * 512:(b + 1) * 512])
        r.ew("vector", "tensor_add", rt3[:], rt1[:], rt2[:], sem=s["rp"])
        r.wait("scalar", s["rp"], s["rp"].n)
        r.act("scalar", sqf[:], rt3[:], AF.Square, sem=s["sq"])
        r.wait("tensor", s["sq"], s["sq"].n)
        r.mm(bank[:], ones[:], sqf[:], True, True, sem=s["nm"])
        r.wait("scalar", s["nm"], s["nm"].n)
        r.act("scalar", tln[:], bank[:], AF.Ln, bias=C["eps"][:, 1:2])
        r.act("scalar", fbuf[:], tln[:], AF.Exp, scale=-0.5, sem=s["f"])
        r.wait("vector", s["f"], s["f"].n)
        if is_q:
            r.ew("vector", "scalar_tensor_tensor", dst[:, dstcol:dstcol + 512],
                 rt3[:], tempbc[:, head:head + 1], fbuf[:], OP.mult, OP.mult,
                 sem=s["qk"])
        else:
            r.ew("vector", "tensor_mul", dst[:, dstcol:dstcol + 512], rt3[:],
                 fbuf[:], sem=s["qk"])

    for b in range(NBLK):
        if b > 0 and ev.get("xblk"):
            r.wait("vector", *ev["xblk"])
        entry_block(r, nc, ap, pb, idf, xpan, xblk, b, (xin, tp, evd))
        ev["xblk_ready"] = (evd, evd.n)
        ln_block(r, lnbufs, (pb[2][:], pb[3][:]), sems, b, ev)
        ev["yblk"] = None
        if not PH1["kv"]:
            continue
        r.wait("tensor", s["ln_y"], s["ln_y"].n)
        for i in range(NCD):
            slot, wsem, lv = stream((ap["wkva_bf"], 0), i * 128, 1024)
            r.wait("tensor", wsem, lv)
            if i == 0 and ev.get("pj_banks"):
                r.wait("tensor", *ev["pj_banks"])
            for ts in range(4):
                r.mm(pb[ts][:], yblk[:, i * 512 + ts * 128: i * 512 + ts * 128 + 128],
                     ws[:, slot * 2048 + 512: slot * 2048 + 1024],
                     i == 0, i == NCD - 1)
            for kc in range(4):
                last = (kc == 3)
                r.mm(pb[4 + kc][:],
                     ws[:, slot * 2048 + kc * 128: slot * 2048 + (kc + 1) * 128],
                     yblk[:, i * 512: (i + 1) * 512],
                     i == 0, i == NCD - 1, sem=(s["pj"] if last else None))
            ev["ws%d" % slot] = (s["pj"], s["pj"].n)
        r.wait("scalar", s["pj"], s["pj"].n)
        for ts in range(4):
            g = b * 4 + ts
            r.act("scalar", vtm[:, g * 512:(g + 1) * 512], pb[ts][:], AF.Copy,
                  sem=s["ev"])
        pj_ev = (s["pj"], s["pj"].n)
        for kc in range(4):
            rope_norm(kc, pb[4 + kc], kbuf, kc * 2048 + b * 512, b, False, 0, pj_ev)
        ev["pj_banks"] = (s["qk"], s["qk"].n)
        if not PH1["q"]:
            continue
        for half in range(2):
            for i in range(NCD):
                slot, wsem, lv = stream((ap["wqa_bf"], half * 1024), i * 128, 1024)
                r.wait("tensor", wsem, lv)
                if i == 0:
                    r.wait("tensor", *ev["pj_banks"])
                    if half == 0:
                        r.wait("tensor", s["ev"], s["ev"].n)
                for qc8 in range(8):
                    r.mm(pb[qc8][:],
                         ws[:, slot * 2048 + qc8 * 128: slot * 2048 + (qc8 + 1) * 128],
                         yblk[:, i * 512:(i + 1) * 512],
                         i == 0, i == NCD - 1,
                         sem=(s["pj"] if qc8 == 7 else None))
                ev["ws%d" % slot] = (s["pj"], s["pj"].n)
            pj_ev = (s["pj"], s["pj"].n)
            for qc8 in range(8):
                h = half * 8 + qc8
                rope_norm(qc8, pb[qc8], qbuf, h * 512, b, True, h, pj_ev)
            ev["pj_banks"] = (s["qk"], s["qk"].n)
        ev["yblk"] = (s["pj"], s["pj"].n)
        kcs = list(range(max(0, 4 * b - 2), 4 * b + 4))
        if not PH1["sc"]:
            continue
        r.wait("tensor", s["qk"], s["qk"].n)
        for h in range(16):
            kvh = h // 4
            par = h % 2
            bS = [pb[0 + 4 * par], pb[1 + 4 * par]]
            bD = pb[2 + 4 * par]
            bO = pb[3 + 4 * par]
            if ev.get("sc_banks%d" % par):
                r.wait("tensor", *ev["sc_banks%d" % par])
            r.wait("scalar", s["pv"], s["pv"].n)
            for i, kc in enumerate(kcs):
                off, span = SPANS_A[kc - 4 * b]
                bank = bS[i % 2]
                if i >= 2:
                    r.wait("tensor", s["ex"], s["ex"].n - 1)
                r.mm(bank[:, 0:span],
                     kbuf[:, kvh * 2048 + kc * 128: kvh * 2048 + (kc + 1) * 128],
                     qbuf[:, h * 512 + off: h * 512 + off + span],
                     True, True, sem=s["sc"])
                r.wait("scalar", s["sc"], s["sc"].n)
                r.act("scalar", exps[:, i * 512 + off: i * 512 + off + span],
                      bank[:, 0:span], AF.Exp,
                      bias=tempbc[:, 16 + h:17 + h], sem=s["ex"])
            r.wait("vector", s["ex"], s["ex"].n)
            mops = []
            for i, kc in enumerate(kcs):
                d = kc - 4 * b
                if 0 <= d <= 3:
                    mops.append((i, d, tri0))
                if 0 <= d + 2 <= 3:
                    mops.append((i, d + 2, tri2))
            for mi, (i, j, tri) in enumerate(mops):
                r.ew("vector", "tensor_mul",
                     exps[:, i * 512 + j * 128: i * 512 + (j + 1) * 128],
                     exps[:, i * 512 + j * 128: i * 512 + (j + 1) * 128],
                     tri[:], sem=(s["mk"] if mi == len(mops) - 1 else None))
            r.wait("tensor", s["mk"], s["mk"].n)
            nl = len(kcs)
            r.mm(bD[:, 0:512], zbuf[:], yblk[:, 0:512], True, False)
            for i, kc in enumerate(kcs):
                off, span = SPANS_A[kc - 4 * b]
                r.mm(bD[:, off:off + span], ones[:],
                     exps[:, i * 512 + off: i * 512 + off + span],
                     False, i == nl - 1, sem=(s["dn"] if i == nl - 1 else None))
            r.mm(bO[:, 0:512], zbuf[:], yblk[:, 0:512], True, False)
            for i, kc in enumerate(kcs):
                off, span = SPANS_A[kc - 4 * b]
                r.mm(bO[:, off:off + span],
                     vtm[:, kc * 512 + kvh * 128: kc * 512 + (kvh + 1) * 128],
                     exps[:, i * 512 + off: i * 512 + off + span],
                     False, i == nl - 1, sem=(s["pv"] if i == nl - 1 else None))
            r.wait("vector", s["dn"], s["dn"].n)
            r.ew("vector", "tensor_copy", dcp[:], bD[:])
            r.ew("vector", "reciprocal", rcp[:], dcp[:])
            r.wait("vector", s["pv"], s["pv"].n)
            r.ew("vector", "tensor_mul", atto[:, h * 512:(h + 1) * 512], bO[:],
                 rcp[:], sem=s["at"])
            ev["sc_banks%d" % par] = (s["at"], s["at"].n)
        if not PH1["wo"]:
            continue
        r.wait("tensor", s["at"], s["at"].n)
        for half in range(2):
            for vc in range(NCD):
                slot, wsem, lv = stream((ap["woa_bf"], half * 1024), vc * 128, 1024)
                r.wait("tensor", wsem, lv)
                if vc == 0:
                    r.wait("tensor", s["res"], s["res"].n)
                for oc8 in range(8):
                    r.mm(pb[oc8][:],
                         ws[:, slot * 2048 + oc8 * 128: slot * 2048 + (oc8 + 1) * 128],
                         atto[:, vc * 512:(vc + 1) * 512],
                         vc == 0, vc == NCD - 1,
                         sem=(s["wo"] if oc8 == 7 else None))
                ev["ws%d" % slot] = (s["wo"], s["wo"].n)
            r.wait("vector", s["wo"], s["wo"].n)
            for oc8 in range(8):
                oc = half * 8 + oc8
                r.ew("vector", "tensor_add", xblk[:, oc * 512:(oc + 1) * 512],
                     xblk[:, oc * 512:(oc + 1) * 512], pb[oc8][:],
                     sem=(s["res"] if oc8 == 7 else None))
            ev["wo_banks"] = (s["res"], s["res"].n)
        ev["xblk"] = None
        r.wait("gpsimd", s["res"], s["res"].n)
        for dc in range(NCD):
            r.dma("gpsimd", ap["xscr"][dc * 128:(dc + 1) * 128, b * 512:(b + 1) * 512],
                  xblk[:, dc * 512:(dc + 1) * 512], sem=xout)
        ev["xblk"] = (xout, xout.n)
        if b < NBLK - 1:
            r.wait("tensor", s["res"], s["res"].n)
    r.wait("gpsimd", xout, xout.n)
    r.wait("sync", xout, xout.n)


def ffn_group(r, nc, ap, E, pb, ws_a, ws_b, C, tag):
    s = {k: r.sem(k) for k in
         ("ln_cast", "ln_sq", "ln_slot", "ln_mm", "ln_stat", "ln_rstd", "ln_y",
          "pj1", "pj3", "hp", "sl", "gt", "w2", "res")}
    xld = [r.sem("xld0"), r.sem("xld1")]
    xout = r.sem("xout")

    xblk = E(nc.sbuf_tensor(tag + "xblk", [128, NCD * 512], F32))
    yblk = E(nc.sbuf_tensor(tag + "yblk", [128, NCD * 512], BF16))
    xb16 = E(nc.sbuf_tensor(tag + "xb16", [128, 2 * 512], BF16))
    sq16 = E(nc.sbuf_tensor(tag + "sq16", [128, 2 * 512], BF16))
    rstd = E(nc.sbuf_tensor(tag + "rstd", [128, 512], F32))
    negmr = E(nc.sbuf_tensor(tag + "negmr", [128, 512], F32))
    lntmp = E(nc.sbuf_tensor(tag + "lntmp", [128, 512], F32))
    hbuf = E(nc.sbuf_tensor(tag + "hbuf", [128, NHC * 512], BF16))
    sgb = E(nc.sbuf_tensor(tag + "sgb", [128, 4 * 512], BF16))

    lnbufs = dict(xblk=xblk, yblk=yblk, xb16=xb16, sq16=sq16, rstd=rstd,
                  negmr=negmr, lntmp=lntmp, ones=C["ones"], eps=C["eps"])
    ev = {}
    st1 = Streamer(r, ws_a, 4, 1024, "sync", "wsy")
    st3 = Streamer(r, ws_b, 4, 1024, "sync", "wsc")

    cw12 = r.sem("cw12")
    r.wait("sync", cw12, cw12.n)
    r.wait("scalar", cw12, cw12.n)

    for b in range(NBLK):
        slot_b = b % 2
        if ev.get("xblk"):
            r.wait("sync", *ev["xblk"])
        for dc in range(NCD):
            r.dma("sync", xblk[:, dc * 512:(dc + 1) * 512],
                  ap["xscr"][dc * 128:(dc + 1) * 128, b * 512:(b + 1) * 512],
                  sem=xld[slot_b])
        ev["xblk_ready"] = (xld[slot_b], xld[slot_b].n)
        r.wait("vector", xld[slot_b], xld[slot_b].n)
        ln_block(r, lnbufs, (pb[2][:], pb[3][:]), s, b, ev)
        r.wait("tensor", s["ln_y"], s["ln_y"].n)
        for hcg in range(16):
            for i in range(NCD):
                sl1, w1s, lv1 = st1.load(
                    ap["w1_bf"][hcg * 512: hcg * 512 + 512, i * 128:(i + 1) * 128], 512)
                sl3, w3s, lv3 = st3.load(
                    ap["w3_bf"][hcg * 512: hcg * 512 + 512, i * 128:(i + 1) * 128], 512)
                r.wait("tensor", w1s, lv1)
                r.wait("tensor", w3s, lv3)
                if i == 0:
                    r.wait("tensor", s["gt"], s["gt"].n)
                for k in range(4):
                    r.mm(pb[k][:], st1.view(sl1, k * 128, (k + 1) * 128),
                         yblk[:, i * 512:(i + 1) * 512],
                         i == 0, i == NCD - 1,
                         sem=(s["pj1"] if k == 3 else None))
                for k in range(4):
                    r.mm(pb[4 + k][:], st3.view(sl3, k * 128, (k + 1) * 128),
                         yblk[:, i * 512:(i + 1) * 512],
                         i == 0, i == NCD - 1,
                         sem=(s["pj3"] if k == 3 else None))
                st1.set_free(sl1, s["pj1"])
                st3.set_free(sl3, s["pj3"])
            r.wait("scalar", s["pj3"], s["pj3"].n)
            for k in range(4):
                r.act("scalar", sgb[:, k * 512:(k + 1) * 512], pb[k][:], AF.Silu,
                      sem=s["sl"])
            r.wait("vector", s["sl"], s["sl"].n)
            for k in range(4):
                hc = hcg * 4 + k
                r.ew("vector", "tensor_mul", hbuf[:, hc * 512:(hc + 1) * 512],
                     sgb[:, k * 512:(k + 1) * 512], pb[4 + k][:], sem=s["gt"])
        r.wait("tensor", s["gt"], s["gt"].n)
        for ohalf in range(2):
            if ohalf == 1:
                r.wait("tensor", s["res"], s["res"].n)
            for hc in range(NHC):
                stx = st1 if hc % 2 == 0 else st3
                sl2, w2s, lv2 = stx.load(
                    ap["w2m"][ohalf * 1024: ohalf * 1024 + 1024,
                              hc * 128:(hc + 1) * 128], 1024)
                r.wait("tensor", w2s, lv2)
                for oc8 in range(8):
                    r.mm(pb[oc8][:], stx.view(sl2, oc8 * 128, (oc8 + 1) * 128),
                         hbuf[:, hc * 512:(hc + 1) * 512],
                         hc == 0, hc == NHC - 1,
                         sem=(s["w2"] if oc8 == 7 else None))
                stx.set_free(sl2, s["w2"])
            r.wait("vector", s["w2"], s["w2"].n)
            for oc8 in range(8):
                oc = ohalf * 8 + oc8
                r.ew("vector", "tensor_add", xblk[:, oc * 512:(oc + 1) * 512],
                     xblk[:, oc * 512:(oc + 1) * 512], pb[oc8][:],
                     sem=(s["res"] if oc8 == 7 else None))
        r.wait("gpsimd", s["res"], s["res"].n)
        for dc in range(NCD):
            r.dma("gpsimd", ap["xscr"][dc * 128:(dc + 1) * 128, b * 512:(b + 1) * 512],
                  xblk[:, dc * 512:(dc + 1) * 512], sem=xout)
        ev["xblk"] = (xout, xout.n)
        if b < NBLK - 1:
            r.wait("tensor", s["res"], s["res"].n)
    r.wait("gpsimd", xout, xout.n)
    r.wait("sync", xout, xout.n)
    return xout


def g1c(r, nc, ap, E, pb, ws_a, ws_b, C):
    s = {k: r.sem(k) for k in
         ("ln_cast", "ln_sq", "ln_slot", "ln_mm", "ln_stat", "ln_rstd", "ln_y",
          "pj", "ev", "rp", "st")}
    xld = [r.sem("xld0"), r.sem("xld1")]
    sto = [r.sem("sto0"), r.sem("sto1")]

    xblk = E(nc.sbuf_tensor("1c_xblk", [128, NCD * 512], F32))
    yblk = E(nc.sbuf_tensor("1c_yblk", [128, NCD * 512], BF16))
    xb16 = E(nc.sbuf_tensor("1c_xb16", [128, 2 * 512], BF16))
    sq16 = E(nc.sbuf_tensor("1c_sq16", [128, 2 * 512], BF16))
    rstd = E(nc.sbuf_tensor("1c_rstd", [128, 512], F32))
    negmr = E(nc.sbuf_tensor("1c_negmr", [128, 512], F32))
    lntmp = E(nc.sbuf_tensor("1c_lntmp", [128, 512], F32))
    stg = E(nc.sbuf_tensor("1c_stg", [128, 2 * 8 * 512], BF16))
    ebuf = E(nc.sbuf_tensor("1c_ebuf", [128, 8 * 512], BF16))
    rt1 = E(nc.sbuf_tensor("1c_rt1", [128, 512], F32))
    rt2 = E(nc.sbuf_tensor("1c_rt2", [128, 512], F32))
    csb = E(nc.sbuf_tensor("1c_csb", [128, 2048], F32))
    snb = E(nc.sbuf_tensor("1c_snb", [128, 2048], F32))
    cskb = E(nc.sbuf_tensor("1c_cskb", [128, 2048], F32))
    snkb = E(nc.sbuf_tensor("1c_snkb", [128, 2048], F32))

    tbl = r.sem("tbl")
    for name, buf in (("cs_b", csb), ("sn_b", snb), ("csk_b", cskb), ("snk_b", snkb)):
        r.dma("sync", buf[:], ap[name], sem=tbl)
    r.wait("vector", tbl, tbl.n)
    cwb = r.sem("cwb")
    r.wait("sync", cwb, cwb.n)
    r.wait("scalar", cwb, cwb.n)

    lnbufs = dict(xblk=xblk, yblk=yblk, xb16=xb16, sq16=sq16, rstd=rstd,
                  negmr=negmr, lntmp=lntmp, ones=C["ones"], eps=C["eps"])
    ev = {}
    st1 = Streamer(r, ws_a, 4, 1024, "sync", "wsy")
    st3 = Streamer(r, ws_b, 4, 1024, "sync", "wsc")

    groups = [
        (ap["wqb_bf"], 0, "qn", 0), (ap["wqb_bf"], 1024, "qn", 8),
        (ap["wqb_bf"], 2048, "qr", 0),
        (ap["wkb_bf"], 0, "kn", 0), (ap["wkb_bf"], 1024, "kn", 8),
        (ap["wkb_bf"], 2048, "kr", 0),
        (ap["wvb_bf"], 0, "v", 0), (ap["wvb_bf"], 1024, "v", 1),
    ]

    for b in range(NBLK):
        slot_b = b % 2
        if ev.get("xblk"):
            r.wait("sync", *ev["xblk"])
        for dc in range(NCD):
            r.dma("sync", xblk[:, dc * 512:(dc + 1) * 512],
                  ap["xscr"][dc * 128:(dc + 1) * 128, b * 512:(b + 1) * 512],
                  sem=xld[slot_b])
        ev["xblk_ready"] = (xld[slot_b], xld[slot_b].n)
        r.wait("vector", xld[slot_b], xld[slot_b].n)
        ln_block(r, lnbufs, (pb[2][:], pb[3][:]), s, b, ev)
        r.wait("tensor", s["ln_y"], s["ln_y"].n)
        for gi, (wsrc, row0, mode, arg) in enumerate(groups):
            gslot = gi % 2
            stgo = gslot * 8 * 512
            stx = st1 if gi % 2 == 0 else st3
            if mode == "v":
                for i in range(NCD):
                    slw, wsem, lv = stx.load(
                        wsrc[row0:row0 + 1024, i * 128:(i + 1) * 128], 1024)
                    r.wait("tensor", wsem, lv)
                    if i == 0 and ev.get("pj_banks"):
                        r.wait("tensor", *ev["pj_banks"])
                    for tsub in range(4):
                        for vfb2 in range(2):
                            r.mm(pb[tsub * 2 + vfb2][:],
                                 yblk[:, i * 512 + tsub * 128: i * 512 + tsub * 128 + 128],
                                 stx.view(slw, vfb2 * 512, (vfb2 + 1) * 512),
                                 i == 0, i == NCD - 1,
                                 sem=(s["pj"] if (tsub, vfb2) == (3, 1) else None))
                    stx.set_free(slw, s["pj"])
            else:
                for i in range(NCD):
                    slw, wsem, lv = stx.load(
                        wsrc[row0:row0 + 1024, i * 128:(i + 1) * 128], 1024)
                    r.wait("tensor", wsem, lv)
                    if i == 0 and ev.get("pj_banks"):
                        r.wait("tensor", *ev["pj_banks"])
                    for c8 in range(8):
                        r.mm(pb[c8][:], stx.view(slw, c8 * 128, (c8 + 1) * 128),
                             yblk[:, i * 512:(i + 1) * 512],
                             i == 0, i == NCD - 1,
                             sem=(s["pj"] if c8 == 7 else None))
                    stx.set_free(slw, s["pj"])
            pj_ev = (s["pj"], s["pj"].n)
            if ev.get("stg%d" % gslot):
                r.wait("scalar", *ev["stg%d" % gslot])
                r.wait("vector", *ev["stg%d" % gslot])
            if mode in ("qn", "kn", "v"):
                r.wait("scalar", s["pj"], s["pj"].n)
                sc = RSQ_DN if mode == "kn" else 1.0
                for c8 in range(8):
                    r.act("scalar", stg[:, stgo + c8 * 512: stgo + (c8 + 1) * 512],
                          pb[c8][:], AF.Copy, scale=sc, sem=s["ev"])
                r.wait("gpsimd", s["ev"], s["ev"].n)
            else:
                for c8 in range(8):
                    r.wait("vector", *pj_ev)
                    if ev.get("eb%d" % c8):
                        r.wait("vector", *ev["eb%d" % c8])
                    r.ew("vector", "tensor_copy", ebuf[:, c8 * 512:(c8 + 1) * 512],
                         pb[c8][:], sem=s["ev"])
                    r.wait("tensor", s["ev"], s["ev"].n)
                    r.mm(pb[c8][:], C["p32"][:], ebuf[:, c8 * 512:(c8 + 1) * 512],
                         True, True, sem=s["rp"])
                    ev["eb%d" % c8] = (s["rp"], s["rp"].n)
                    ct = csb if mode == "qr" else cskb
                    st_ = snb if mode == "qr" else snkb
                    r.wait("vector", s["rp"], s["rp"].n)
                    r.ew("vector", "tensor_mul", rt1[:],
                         ebuf[:, c8 * 512:(c8 + 1) * 512],
                         ct[:, b * 512:(b + 1) * 512])
                    r.ew("vector", "tensor_mul", rt2[:], pb[c8][:],
                         st_[:, b * 512:(b + 1) * 512])
                    r.ew("vector", "tensor_add",
                         stg[:, stgo + c8 * 512: stgo + (c8 + 1) * 512],
                         rt1[:], rt2[:], sem=s["ev"])
                r.wait("gpsimd", s["ev"], s["ev"].n)
            ev["pj_banks"] = (s["ev"], s["ev"].n)
            for c8 in range(8):
                srcap = stg[:, stgo + c8 * 512: stgo + (c8 + 1) * 512]
                if mode == "qn":
                    ch = arg + c8
                    dst = bass.AP(ap["qscr"].tensor, ch * 128 * S + b * 512,
                                  [[S, 128], [1, 512]])
                elif mode == "qr":
                    ch = 16 + c8
                    dst = bass.AP(ap["qscr"].tensor, ch * 128 * S + b * 512,
                                  [[S, 128], [1, 512]])
                elif mode in ("kn", "kr"):
                    colb = (2048 + c8 * 128) if mode == "kr" else (arg + c8) * 128
                    dst = bass.AP(ap["kscr"].tensor, b * 4 * 128 * 3072 + colb,
                                  [[3072, 128], [128 * 3072, 4], [1, 128]])
                else:
                    tsub = c8 // 2; vfb2 = c8 % 2
                    kc = b * 4 + tsub
                    colb = arg * 1024 + vfb2 * 512
                    dst = bass.AP(ap["vscr"].tensor, kc * 128 * 2048 + colb,
                                  [[2048, 128], [1, 512]])
                r.dma("gpsimd", dst, srcap, sem=sto[gslot])
            ev["stg%d" % gslot] = (sto[gslot], sto[gslot].n)
        ev["xblk"] = None
        ev["yblk"] = (s["pj"], s["pj"].n)
    r.wait("gpsimd", sto[0], sto[0].n)
    r.wait("gpsimd", sto[1], sto[1].n)
    r.wait("sync", sto[0], sto[0].n)
    r.wait("sync", sto[1], sto[1].n)


def g2a(r, nc, ap, E, pb, ws_a, c_seld, c_ones):
    s = {k: r.sem(k) for k in
         ("ql", "sc", "ex", "pvc", "oc", "dn", "dloc", "rc", "exf", "at",
          "wo", "res", "zp")}
    xld = [r.sem("xld0"), r.sem("xld1")]
    kld = [r.sem("kld0"), r.sem("kld1")]
    wl = [r.sem("wl0"), r.sem("wl1")]
    xout = r.sem("xout")

    qbn = E(nc.sbuf_tensor("2a_qbn", [128, 16 * 512], BF16))
    qrp = E(nc.sbuf_tensor("2a_qrp", [128, 16 * 512], BF16))
    kbb = E(nc.sbuf_tensor("2a_kbb", [128, 8 * 3072], BF16))
    vbb = E(nc.sbuf_tensor("2a_vbb", [128, 8 * 2048], BF16))
    expt = E(nc.sbuf_tensor("2a_expt", [128, 2 * 512], BF16))
    obuf = E(nc.sbuf_tensor("2a_obuf", [128, 16 * 512], BF16))
    dloc = E(nc.sbuf_tensor("2a_dloc", [16, 512], F32))
    drec = E(nc.sbuf_tensor("2a_drec", [16, 512], F32))
    xblk = E(nc.sbuf_tensor("2a_xblk", [128, NCD * 512], F32))
    expd = E(nc.sbuf_tensor("2a_expd", [16, 2048], F32))
    dbgs = E(nc.sbuf_tensor("2a_dbgs", [128, 512], F32)) if DBG_WO else None
    dbsem = r.sem("dbsem") if DBG_WO else None
    dbsd = r.sem("dbsd") if DBG_WO else None

    tbl = r.sem("tbl")
    r.dma("sync", expd[:], ap["expd"], sem=tbl)
    r.wait("tensor", tbl, tbl.n)
    cwb = r.sem("cwb")
    r.wait("sync", cwb, cwb.n)
    for h in range(16):
        if h % 2 == 0:
            r.ew("vector", "memset", qrp[64:128, h * 512:(h + 1) * 512], 0.0,
                 sem=(s["zp"] if h == 15 else None))
        else:
            r.ew("vector", "memset", qrp[0:64, h * 512:(h + 1) * 512], 0.0,
                 sem=(s["zp"] if h == 15 else None))

    ev = {}
    wtile = [0]

    def stream(row0, col0):
        w = wtile[0]; wtile[0] += 1
        slot = w % 2
        if ev.get("ws%d" % slot):
            r.wait("sync", *ev["ws%d" % slot])
        r.dmat(ws_a[:, slot * 2048: slot * 2048 + 1024],
               ap["wob_bf"][row0:row0 + 1024, col0:col0 + 128], sem=wl[slot])
        return slot, wl[slot], wl[slot].n

    for qb in range(NBLK):
        qslot = qb % 2
        if ev.get("qbuf"):
            r.wait("sync", *ev["qbuf"])
        for ch in range(16):
            r.dma("sync", qbn[:, ch * 512:(ch + 1) * 512],
                  bass.AP(ap["qscr"].tensor, ch * 128 * S + qb * 512,
                          [[S, 128], [1, 512]]), sem=s["ql"])
        for c in range(8):
            r.dma("sync", qrp[0:64, (2 * c) * 512:(2 * c + 1) * 512],
                  bass.AP(ap["qscr"].tensor, (16 + c) * 128 * S + qb * 512,
                          [[S, 64], [1, 512]]), sem=s["ql"])
            r.dma("sync", qrp[64:128, (2 * c + 1) * 512:(2 * c + 2) * 512],
                  bass.AP(ap["qscr"].tensor, (16 + c) * 128 * S + 64 * S + qb * 512,
                          [[S, 64], [1, 512]]), sem=s["ql"])
        if ev.get("xblk"):
            r.wait("sync", *ev["xblk"])
        for dc in range(NCD):
            r.dma("sync", xblk[:, dc * 512:(dc + 1) * 512],
                  ap["xscr"][dc * 128:(dc + 1) * 128, qb * 512:(qb + 1) * 512],
                  sem=xld[qslot])
        ql_ev = (s["ql"], s["ql"].n)
        for kh in range(2):
            kslot = kh
            if ev.get("kvb"):
                r.wait("sync", *ev["kvb"])
            for kc8 in range(8):
                kc = kh * 8 + kc8
                r.dma("sync", kbb[:, kc8 * 3072:(kc8 + 1) * 3072],
                      bass.AP(ap["kscr"].tensor, kc * 128 * 3072,
                              [[3072, 128], [1, 3072]]), sem=kld[kslot])
                r.dma("sync", vbb[:, kc8 * 2048:(kc8 + 1) * 2048],
                      bass.AP(ap["vscr"].tensor, kc * 128 * 2048,
                              [[2048, 128], [1, 2048]]), sem=kld[kslot])
            r.wait("tensor", kld[kslot], kld[kslot].n)
            r.wait("tensor", *ql_ev)
            if kh == 0:
                r.wait("tensor", s["zp"], s["zp"].n)
            r.wait("tensor", s["dloc"], s["dloc"].n)
            bD = pb[4 + kh]
            ii = 0
            for h in range(16):
                bS = [pb[0], pb[1]]
                bO = pb[2 + (h % 2)]
                if s["oc"].n >= 1:
                    r.wait("tensor", s["oc"], s["oc"].n - 1)
                for kc8 in range(8):
                    bank = bS[ii % 2]
                    if ev.get("sb%d" % (ii % 2)):
                        r.wait("tensor", *ev["sb%d" % (ii % 2)])
                    r.mm(bank[:], kbb[:, kc8 * 3072 + h * 128: kc8 * 3072 + (h + 1) * 128],
                         qbn[:, h * 512:(h + 1) * 512], True, False)
                    r.mm(bank[:],
                         kbb[:, kc8 * 3072 + 2048 + (h // 2) * 128: kc8 * 3072 + 2048 + (h // 2 + 1) * 128],
                         qrp[:, h * 512:(h + 1) * 512], False, True, sem=s["sc"])
                    r.wait("scalar", s["sc"], s["sc"].n)
                    if ev.get("et%d" % (ii % 2)):
                        r.wait("scalar", *ev["et%d" % (ii % 2)])
                    r.act("scalar", expt[:, (ii % 2) * 512: (ii % 2 + 1) * 512],
                          bank[:], AF.Exp, sem=s["ex"])
                    ev["sb%d" % (ii % 2)] = (s["ex"], s["ex"].n)
                    r.wait("tensor", s["ex"], s["ex"].n)
                    r.mm(bD[0:16, :], c_seld[:, h * 16:(h + 1) * 16],
                         expt[:, (ii % 2) * 512: (ii % 2 + 1) * 512],
                         ii == 0, ii == 127, sem=(s["dn"] if ii == 127 else None))
                    r.mm(bO[:], vbb[:, kc8 * 2048 + h * 128: kc8 * 2048 + (h + 1) * 128],
                         expt[:, (ii % 2) * 512: (ii % 2 + 1) * 512],
                         kc8 == 0, kc8 == 7,
                         sem=(s["pvc"]))
                    ev["et%d" % (ii % 2)] = (s["pvc"], s["pvc"].n)
                    ii += 1
                r.wait("vector", s["pvc"], s["pvc"].n)
                if kh == 0:
                    r.ew("vector", "tensor_copy", obuf[:, h * 512:(h + 1) * 512],
                         bO[:], sem=s["oc"])
                else:
                    r.ew("vector", "tensor_add", obuf[:, h * 512:(h + 1) * 512],
                         obuf[:, h * 512:(h + 1) * 512], bO[:], sem=s["oc"])
                if h >= 2:
                    pass
                ev["ob%d" % (h % 2)] = (s["oc"], s["oc"].n)
            r.wait("vector", s["dn"], s["dn"].n)
            if kh == 0:
                r.ew("vector", "tensor_copy", dloc[:], bD[0:16, :], sem=s["dloc"])
            else:
                r.ew("vector", "tensor_add", dloc[:], dloc[:], bD[0:16, :],
                     sem=s["dloc"])
            ev["kvb"] = (s["oc"], s["oc"].n)
        r.wait("vector", s["dloc"], s["dloc"].n)
        r.ew("vector", "reciprocal", drec[:], dloc[:], sem=s["rc"])
        r.wait("tensor", s["rc"], s["rc"].n)
        for h in range(16):
            bE = pb[6 + (h % 2)]
            if ev.get("be%d" % (h % 2)):
                r.wait("tensor", *ev["be%d" % (h % 2)])
            r.mm(bE[:], expd[:, h * 128:(h + 1) * 128], drec[:], True, True,
                 sem=s["exf"])
            r.wait("vector", s["exf"], s["exf"].n)
            r.ew("vector", "tensor_mul", obuf[:, h * 512:(h + 1) * 512],
                 obuf[:, h * 512:(h + 1) * 512], bE[:], sem=s["at"])
            ev["be%d" % (h % 2)] = (s["at"], s["at"].n)
        ev["qbuf"] = (s["at"], s["at"].n)
        r.wait("tensor", s["at"], s["at"].n)
        r.wait("tensor", xld[qslot], xld[qslot].n)
        for half in range(2):
            for vc in range(NCD):
                slw, wsem, lv = stream(half * 1024, vc * 128)
                r.wait("tensor", wsem, lv)
                if vc == 0:
                    r.wait("tensor", s["res"], s["res"].n)
                for oc8 in range(8):
                    r.mm(pb[oc8][:],
                         ws_a[:, slw * 2048 + oc8 * 128: slw * 2048 + (oc8 + 1) * 128],
                         obuf[:, vc * 512:(vc + 1) * 512],
                         vc == 0, vc == NCD - 1,
                         sem=(s["wo"] if oc8 == 7 else None))
                ev["ws%d" % slw] = (s["wo"], s["wo"].n)
            r.wait("vector", s["wo"], s["wo"].n)
            r.wait("vector", xld[qslot], xld[qslot].n)
            for oc8 in range(8):
                oc = half * 8 + oc8
                if DBG_WO:
                    r.ew("vector", "tensor_copy", dbgs[:], pb[oc8][:], sem=dbsem)
                    r.wait("gpsimd", dbsem, dbsem.n)
                    r.dma("gpsimd", ap["dbg"][oc * 128:(oc + 1) * 128,
                                              qb * 512:(qb + 1) * 512],
                          dbgs[:], sem=dbsd)
                    r.wait("vector", dbsd, dbsd.n)
                r.ew("vector", "tensor_add", xblk[:, oc * 512:(oc + 1) * 512],
                     xblk[:, oc * 512:(oc + 1) * 512], pb[oc8][:],
                     sem=(s["res"] if oc8 == 7 else None))
        r.wait("gpsimd", s["res"], s["res"].n)
        for dc in range(NCD):
            r.dma("gpsimd", ap["xscr"][dc * 128:(dc + 1) * 128, qb * 512:(qb + 1) * 512],
                  xblk[:, dc * 512:(dc + 1) * 512], sem=xout)
        ev["xblk"] = (xout, xout.n)
        if qb < NBLK - 1:
            r.wait("tensor", s["res"], s["res"].n)
    r.wait("gpsimd", xout, xout.n)
    r.wait("sync", xout, xout.n)



def g2b_exit(r, nc, ap, E, pb, c_idf):
    xin = [r.sem("xld0"), r.sem("xld1")]
    tp = r.sem("tp")
    evd = r.sem("evd")
    ost = r.sem("xout")

    xcol = E(nc.sbuf_tensor("2b_xcol", [128, 2 * 2048], F32))
    opan = E(nc.sbuf_tensor("2b_opan", [128, 2 * 2048], F32))

    tp_base = tp.n
    evd_base = evd.n
    ost_base = ost.n
    for g in range(16):
        slot = g % 2
        if g >= 2:
            r.wait("sync", tp, tp_base + (g - 1) * NCD)
        for dc in range(NCD):
            r.dma("sync", xcol[:, slot * 2048 + dc * 128: slot * 2048 + (dc + 1) * 128],
                  ap["xscr"][dc * 128:(dc + 1) * 128, g * 128:(g + 1) * 128],
                  sem=xin[slot])
        r.wait("tensor", xin[slot], xin[slot].n)
        for dc in range(NCD):
            bank = pb[dc % 2]
            j = g * NCD + dc
            if j >= 2:
                r.wait("tensor", evd, evd_base + j - 1)
            r.tr(bank[:, 0:128],
                 xcol[:, slot * 2048 + dc * 128: slot * 2048 + (dc + 1) * 128],
                 c_idf[:], sem=tp)
            r.wait("vector", tp, tp.n)
            if g >= 2 and dc == 0:
                r.wait("vector", ost, ost_base + 16 * (g - 1))
            r.ew("vector", "tensor_copy",
                 opan[:, slot * 2048 + dc * 128: slot * 2048 + (dc + 1) * 128],
                 bank[:, 0:128], sem=evd)
        r.wait("gpsimd", evd, evd.n)
        r.dma("gpsimd", ap["out"][g * 128:(g + 1) * 128, :],
              opan[:, slot * 2048:(slot + 1) * 2048], sem=ost)
    r.wait("gpsimd", ost, ost.n)
    r.wait("sync", ost, ost.n)



_NC_CACHE = [None]

def kernel(L0_attn_a_x, L0_attn_a_Wq, L0_attn_a_Wkv, L0_attn_a_Wo,
           L0_attn_a_qk_temp, L0_attn_b_Wq, L0_attn_b_Wk, L0_attn_b_Wv,
           L0_attn_b_Wo, L0_ffn_W1, L0_ffn_W2, L0_ffn_W3):
    x = np.ascontiguousarray(np.asarray(L0_attn_a_x, dtype=np.float32))
    shared = {
        "wqa": np.ascontiguousarray(L0_attn_a_Wq, dtype=np.float32),
        "wkva": np.ascontiguousarray(L0_attn_a_Wkv, dtype=np.float32),
        "woa": np.ascontiguousarray(L0_attn_a_Wo, dtype=np.float32),
        "temp": np.ascontiguousarray(np.asarray(L0_attn_a_qk_temp, dtype=np.float32).reshape(16, 1)),
        "wqb": np.ascontiguousarray(L0_attn_b_Wq, dtype=np.float32),
        "wkb": np.ascontiguousarray(L0_attn_b_Wk, dtype=np.float32),
        "wvb": np.ascontiguousarray(L0_attn_b_Wv, dtype=np.float32),
        "wob": np.ascontiguousarray(L0_attn_b_Wo, dtype=np.float32),
        "w1": np.ascontiguousarray(L0_ffn_W1, dtype=np.float32),
        "w2": np.ascontiguousarray(L0_ffn_W2, dtype=np.float32),
        "w3": np.ascontiguousarray(L0_ffn_W3, dtype=np.float32),
    }
    shared.update(_CONSTS)
    if _NC_CACHE[0] is None:
        _NC_CACHE[0] = build_program()
    nc = _NC_CACHE[0]
    in_maps = [dict(shared, x=x[i]) for i in range(8)]
    res = run_bass_kernel_spmd(nc, in_maps, core_ids=list(range(8)))
    return np.stack([np.asarray(res.results[i]["out"]) for i in range(8)], axis=0)


# version 06
# speedup vs baseline: 1.3610x; 1.1348x over previous
import contextlib
import math
import numpy as np
import ml_dtypes
import concourse.bass as bass
import concourse.mybir as mybir
from concourse.bass_utils import run_bass_kernel_spmd

F32 = mybir.dt.float32
BF16 = mybir.dt.bfloat16
FP16 = mybir.dt.float16
AF = mybir.ActivationFunctionType
OP = mybir.AluOpType
BFD = ml_dtypes.bfloat16

ENGS = ("sync", "tensor", "vector", "scalar", "gpsimd")

S = 2048
DM = 2048
NCD = 16
TB = 512
NBLK = 4
TCH = 4
HA = 16
HKV = 4
WIN = 256
HBH = 16
DN = 128
DR = 64
DV = 128
HID = 8192
NHC = 64
RSQ_DN = 1.0 / math.sqrt(128.0)
FWHT_SC = 1.0 / math.sqrt(8192.0)


class Sem:
    __slots__ = ("h", "n")
    def __init__(self, h):
        self.h = h
        self.n = 0


class Streamer:
    def __init__(self, r, buf, nslots, span, eng, name):
        self.r = r
        self.buf = buf
        self.nslots = nslots
        self.span = span
        self.eng = eng
        self.sems = [r.sem(f"{name}{i}") for i in range(nslots)]
        self.free = [None] * nslots
        self.w = 0

    def load(self, src_ap, ncols):
        slot = self.w % self.nslots
        self.w += 1
        if self.free[slot] is not None:
            self.r.wait(self.eng, *self.free[slot])
        self.r.dmat(self.buf[:, slot * self.span: slot * self.span + ncols],
                    src_ap, sem=self.sems[slot], eng=self.eng)
        return slot, self.sems[slot], self.sems[slot].n

    def view(self, slot, c0, c1):
        return self.buf[:, slot * self.span + c0: slot * self.span + c1]

    def set_free(self, slot, sem):
        self.free[slot] = (sem, sem.n)


class Rec:
    def __init__(self, nc, stack):
        self.nc = nc
        self.ops = {e: [] for e in ENGS}
        self.stack = stack
        self._semid = 0

    def sem(self, name=None):
        self._semid += 1
        if name is None:
            name = f"s{self._semid}"
        if not hasattr(self, "_semcache"):
            self._semcache = {}
        if name not in self._semcache:
            h = self.stack.enter_context(self.nc.semaphore(name))
            self._semcache[name] = Sem(h)
        return self._semcache[name]

    def emit(self, eng, fn):
        self.ops[eng].append(fn)

    def wait(self, eng, sem, val=None):
        v = sem.n if val is None else val
        if v <= 0:
            return
        self.emit(eng, lambda e, h=sem.h, v=v: e.wait_ge(h, v))

    def inc(self, eng, sem, k=1):
        sem.n += k
        self.emit(eng, lambda e, h=sem.h, k=k: e.nop().then_inc(h, k))

    def _fin(self, inst, sem, k):
        if sem is not None:
            inst.then_inc(sem.h, k)

    def dma(self, eng, out, in_, sem=None, **kw):
        if sem is not None:
            sem.n += 16
        def f(e, out=out, in_=in_, s=sem, kw=kw):
            i = e.dma_start(out=out, in_=in_, **kw)
            if s is not None:
                i.then_inc(s.h, 16)
        self.emit(eng, f)

    def dmat(self, out, in_, sem=None, eng="sync"):
        if sem is not None:
            sem.n += 16
        def f(e, out=out, in_=in_, s=sem):
            i = e.dma_start_transpose(out, in_)
            if s is not None:
                i.then_inc(s.h, 16)
        self.emit(eng, f)

    def mm(self, out, lhsT, rhs, start, stop, sem=None, k=1):
        if sem is not None:
            sem.n += k
        def f(e, out=out, lhsT=lhsT, rhs=rhs, st=start, sp=stop, s=sem, k=k):
            i = e.matmul(out, lhsT, rhs, start=st, stop=sp)
            if s is not None:
                i.then_inc(s.h, k)
        self.emit("tensor", f)

    def tr(self, out, in_, iden, sem=None):
        if sem is not None:
            sem.n += 1
        def f(e, out=out, in_=in_, iden=iden, s=sem):
            i = e.transpose(out, in_, iden)
            if s is not None:
                i.then_inc(s.h, 1)
        self.emit("tensor", f)

    def act(self, eng, out, in_, func, bias=0.0, scale=1.0, sem=None):
        if sem is not None:
            sem.n += 1
        def f(e, out=out, in_=in_, func=func, b=bias, sc=scale, s=sem):
            i = e.activation(out, in_, func, bias=b, scale=sc)
            if s is not None:
                i.then_inc(s.h, 1)
        self.emit(eng, f)

    def ew(self, eng, name, *args, sem=None, **kw):
        if sem is not None:
            sem.n += 1
        def f(e, name=name, args=args, s=sem, kw=kw):
            i = getattr(e, name)(*args, **kw)
            if s is not None:
                i.then_inc(s.h, 1)
        self.emit(eng, f)

    def barrier(self, bsem, engines=ENGS):
        base = bsem.n
        for e in engines:
            self.emit(e, lambda en, h=bsem.h: en.nop().then_inc(h, 1))
        bsem.n = base + len(engines)
        for e in engines:
            self.wait(e, bsem)

    def replay(self):
        with self.nc.Block() as blk:
            mk = {}
            def bind(ename):
                ops = self.ops[ename]
                def runner(eng):
                    for f in ops:
                        f(eng)
                return runner
            blk.sync(bind("sync"))
            blk.tensor(bind("tensor"))
            blk.vector(bind("vector"))
            blk.scalar(bind("scalar"))
            blk.gpsimd(bind("gpsimd"))
        self.ops = {e: [] for e in ENGS}



def _rope_tables_a():
    inv = 1.0 / (10000.0 ** (np.arange(0, 128, 2, dtype=np.float64) / 128.0))
    pos = np.arange(S, dtype=np.float64)
    ang = pos[None, :] * inv[:, None]
    c = np.cos(ang); s = np.sin(ang)
    C = np.concatenate([c, c], axis=0).astype(np.float32)
    Sp = np.concatenate([-s, s], axis=0).astype(np.float32)
    return C, Sp


def _rope_tables_b():
    inv = 1.0 / (10000.0 ** (np.arange(0, 64, 2, dtype=np.float64) / 64.0))
    pos = np.arange(S, dtype=np.float64)
    ang = pos[None, :] * inv[:, None]
    c = np.cos(ang); s = np.sin(ang)
    C = np.concatenate([c, c, c, c], axis=0).astype(np.float32)
    Sp = np.concatenate([-s, s, -s, s], axis=0).astype(np.float32)
    return C, Sp


def _consts():
    d = {}
    C_A, S_A = _rope_tables_a()
    C_B, S_B = _rope_tables_b()
    d["cs_a"] = C_A; d["sn_a"] = S_A
    d["cs_b"] = C_B; d["sn_b"] = S_B
    d["csk_b"] = (C_B * 0.125).astype(np.float32)
    d["snk_b"] = (S_B * 0.125).astype(np.float32)
    ii = np.arange(128)
    d["p64"] = np.eye(128, dtype=np.float32)[ii ^ 64].T.astype(BFD)
    d["p32"] = np.eye(128, dtype=np.float32)[ii ^ 32].T.astype(BFD)
    ki = ii[:, None]; qi = ii[None, :]
    d["tri0"] = (ki <= qi).astype(np.float32).astype(BFD)
    d["tri2"] = (ki > qi).astype(np.float32).astype(BFD)
    d["ones"] = np.ones((128, 128), dtype=np.float32).astype(BFD)
    d["idb"] = np.eye(128, dtype=np.float32).astype(BFD)
    d["idf"] = np.eye(128, dtype=np.float32)
    d["h64"] = ((-1.0) ** np.array([[bin(a & b).count("1") for b in range(64)]
                                    for a in range(64)])).astype(np.float32).astype(BFD)
    d["h128"] = ((-1.0) ** np.array([[bin(a & b).count("1") for b in range(128)]
                                     for a in range(128)])).astype(np.float32).astype(np.float16)
    seld = np.zeros((128, 16 * 16), dtype=np.float32)
    for h in range(16):
        seld[:, h * 16 + h] = 1.0
    d["seld"] = seld.astype(BFD)
    expd = np.zeros((16, 16 * 128), dtype=np.float32)
    for h in range(16):
        expd[h, h * 128:(h + 1) * 128] = 1.0
    d["expd"] = expd
    return d


_CONSTS = _consts()



def build_program():
    nc = bass.Bass("TRN2", target_bir_lowering=False,
                   detect_race_conditions=False)

    ap = {}
    def din(name, shape, dt):
        ap[name] = nc.dram_tensor(name, list(shape), dt, kind="ExternalInput").ap()
    def dscr(name, shape, dt):
        ap[name] = nc.dram_tensor(name, list(shape), dt, kind="Internal").ap()

    din("x", (S, DM), F32)
    din("wqa", (2048, 2048), F32)
    din("wkva", (1024, 2048), F32)
    din("woa", (2048, 2048), F32)
    din("temp", (16, 1), F32)
    din("wqb", (3072, 2048), F32)
    din("wkb", (3072, 2048), F32)
    din("wvb", (2048, 2048), F32)
    din("wob", (2048, 2048), F32)
    din("w1", (8192, 2048), F32)
    din("w2", (2048, 8192), F32)
    din("w3", (8192, 2048), F32)
    for n, v in _CONSTS.items():
        dt = F32 if v.dtype == np.float32 else (BF16 if v.dtype == BFD else FP16)
        din(n, v.shape, dt)
    ap["out"] = nc.dram_tensor("out", [S, DM], F32, kind="ExternalOutput").ap()

    dscr("xscr", (DM, S), F32)
    dscr("wqa_bf", (2048, 2048), BF16)
    dscr("wkva_bf", (1024, 2048), BF16)
    dscr("woa_bf", (2048, 2048), BF16)
    dscr("wqb_bf", (3072, 2048), BF16)
    dscr("wkb_bf", (3072, 2048), BF16)
    dscr("wvb_bf", (2048, 2048), BF16)
    dscr("wob_bf", (2048, 2048), BF16)
    dscr("w13cb", (16, 16, 1024, 128), BF16)
    dscr("s1h", (64, 2048 * 128), FP16)
    dscr("w2cb", (64, 2048, 128), BF16)
    dscr("t16", (2, 16), F32)
    dscr("kscr", (16, 128, 3072), BF16)
    dscr("vscr", (16, 128, 2048), BF16)
    dscr("qscr", (24, 128, S), BF16)
    dscr("dbg", (DM, S), F32)

    stack = contextlib.ExitStack()
    with stack:
        r = Rec(nc, stack)
        E = stack.enter_context

        pb = [E(nc.psum_tensor(f"pb{i}", [128, 512], F32)) for i in range(8)]
        c_ones = E(nc.sbuf_tensor("c_ones", [128, 128], BF16))
        c_idb = E(nc.sbuf_tensor("c_idb", [128, 128], BF16))
        c_idf = E(nc.sbuf_tensor("c_idf", [128, 128], F32))
        c_tri0 = E(nc.sbuf_tensor("c_tri0", [128, 128], BF16))
        c_tri2 = E(nc.sbuf_tensor("c_tri2", [128, 128], BF16))
        c_p64 = E(nc.sbuf_tensor("c_p64", [128, 128], BF16))
        c_p32 = E(nc.sbuf_tensor("c_p32", [128, 128], BF16))
        c_seld = E(nc.sbuf_tensor("c_seld", [128, 256], BF16))
        c_tempbc = E(nc.sbuf_tensor("c_tempbc", [128, 32], F32))
        c_eps = E(nc.sbuf_tensor("c_eps", [128, 2], F32))
        ws_a = E(nc.sbuf_tensor("ws_a", [128, 2 * 2048], BF16))
        ws_b = E(nc.sbuf_tensor("ws_b", [128, 2 * 2048], BF16))

        bar = r.sem("bar")

        with contextlib.ExitStack() as sc:
            g0(r, nc, ap, sc.enter_context, pb,
               dict(ones=c_ones, idb=c_idb, idf=c_idf, tri0=c_tri0,
                    tri2=c_tri2, p64=c_p64, p32=c_p32, seld=c_seld,
                    tempbc=c_tempbc, eps=c_eps))
            r.barrier(bar)
            r.replay()

        with contextlib.ExitStack() as sc:
            g1a(r, nc, ap, sc.enter_context, pb,
                dict(ones=c_ones, idf=c_idf, tri0=c_tri0, tri2=c_tri2,
                     p64=c_p64, tempbc=c_tempbc, ws_a=ws_a, ws_b=ws_b,
                     eps=c_eps))
            r.barrier(bar)
            r.replay()

        cdict = dict(ones=c_ones, eps=c_eps)
        with contextlib.ExitStack() as sc:
            ffn_group(r, nc, ap, sc.enter_context, pb, ws_a, ws_b, cdict, "2f_")
            r.barrier(bar)
            r.replay()

        with contextlib.ExitStack() as sc:
            g1c(r, nc, ap, sc.enter_context, pb, ws_a, ws_b,
                dict(ones=c_ones, eps=c_eps, p32=c_p32))
            r.barrier(bar)
            r.replay()

        with contextlib.ExitStack() as sc:
            g2a(r, nc, ap, sc.enter_context, pb, ws_a, c_seld, c_ones)
            r.barrier(bar)
            r.replay()

        with contextlib.ExitStack() as sc:
            ffn_group(r, nc, ap, sc.enter_context, pb, ws_a, ws_b, cdict, "4f_")
            g2b_exit(r, nc, ap, sc.enter_context, pb, c_idf)
            r.barrier(bar)
            r.replay()

    return nc



def g0(r, nc, ap, E, pb, C):
    ld = r.sem("g0ld")
    cast = r.sem("g0cast")
    pe = r.sem("g0pe")
    ve = r.sem("g0ve")
    se = r.sem("g0se")
    stv = r.sem("g0st")

    for name, buf in (("ones", C["ones"]), ("idb", C["idb"]), ("idf", C["idf"]),
                      ("tri0", C["tri0"]), ("tri2", C["tri2"]), ("p64", C["p64"]),
                      ("p32", C["p32"]), ("seld", C["seld"])):
        r.dma("sync", buf[:], ap[name], sem=ld)

    r.ew("vector", "memset", C["eps"][:, 0:1], 1e-5)
    r.ew("vector", "memset", C["eps"][:, 1:2], 1e-24, sem=ve)
    tb1 = E(nc.sbuf_tensor("g0_tb1", [16, 2], F32))
    tbr = E(nc.sbuf_tensor("g0_tbr", [1, 32], BF16))
    onesrow = E(nc.sbuf_tensor("g0_or", [1, 32], BF16))
    r.dma("sync", tb1[:, 0:1], ap["temp"], sem=ld)
    n_ld = ld.n
    r.wait("scalar", ld, n_ld)
    r.act("scalar", tb1[:, 1:2], tb1[:, 0:1], AF.Abs, scale=1.0, sem=se)
    r.ew("scalar", "mul", tb1[:, 1:2], tb1[:, 1:2], -1.0, sem=se)
    tbs = r.sem("g0_tbs")
    r.wait("gpsimd", se, se.n)
    r.dma("gpsimd", bass.AP(ap["t16"].tensor, 0, [[1, 16], [16, 2]]), tb1[:, 0:2], sem=tbs,
          allow_slow_non_contiguous=True)
    r.wait("gpsimd", tbs, tbs.n)
    r.dma("gpsimd", tbr[:], bass.AP(ap["t16"].tensor, 0, [[32, 1], [1, 32]]), sem=tbs)
    r.wait("vector", ld, n_ld)
    r.ew("vector", "memset", onesrow[:], 1.0, sem=ve)
    r.wait("tensor", ld, n_ld)
    r.wait("tensor", tbs, tbs.n)
    r.wait("tensor", ve, ve.n)
    r.mm(pb[0][:, 0:32], C["ones"][0:1, 0:128], tbr[:], True, True, sem=pe)
    r.wait("vector", pe, pe.n)
    r.ew("vector", "tensor_copy", C["tempbc"][:, 0:32], pb[0][:, 0:32], sem=ve)

    cwa = r.sem("cwa")
    for srcw, dst in (("wqa", "wqa_bf"), ("wkva", "wkva_bf"), ("woa", "woa_bf")):
        n = ap[srcw].tensor.shape[0]
        half = n // 2
        r.dma("gpsimd", ap[dst][0:half, :], ap[srcw][0:half, :], sem=cwa)
        r.dma("gpsimd", ap[dst][half:n, :], ap[srcw][half:n, :], sem=cwa)

    h64 = E(nc.sbuf_tensor("g0_h64", [64, 64], BF16))
    h128 = E(nc.sbuf_tensor("g0_h128", [128, 128], FP16))
    ga = E(nc.sbuf_tensor("g0_ga", [64, 4 * 1024], BF16))
    ea = E(nc.sbuf_tensor("g0_ea", [64, 4 * 1024], FP16))
    ld2 = r.sem("g0ld2")
    r.dma("sync", h64[:], ap["h64"], sem=ld2)
    r.dma("sync", h128[:], ap["h128"], sem=ld2)
    pa_mm = r.sem("g0_pamm")
    pa_ev = r.sem("g0_paev")
    gas = [r.sem("g0_ga%d" % i) for i in range(4)]
    sas = [r.sem("g0_sa%d" % i) for i in range(4)]
    r.wait("tensor", ld2, ld2.n)
    NT_A = 256
    for i in range(NT_A):
        d0 = i * 8
        slot = i % 4
        if i >= 4:
            r.wait("gpsimd", pa_ev, i - 3)
        r.dma("gpsimd",
              ga[:, slot * 1024:(slot + 1) * 1024],
              bass.AP(ap["w2"].tensor, d0 * 8192, [[128, 64], [8192, 8], [1, 128]]),
              sem=gas[slot])
        bk = 2 * slot
        r.wait("tensor", gas[slot], gas[slot].n)
        if i >= 4:
            r.wait("tensor", pa_ev, i - 3)
        r.mm(pb[bk][0:64, :], h64[:], ga[0:64, slot * 1024: slot * 1024 + 512],
             True, True)
        r.mm(pb[bk + 1][0:64, :], h64[:],
             ga[0:64, slot * 1024 + 512: slot * 1024 + 1024], True, True, sem=pa_mm)
        r.wait("vector", pa_mm, pa_mm.n)
        if i >= 4:
            r.wait("vector", sas[slot], sas[slot].n)
        r.ew("vector", "tensor_copy", ea[0:64, slot * 1024: slot * 1024 + 512],
             pb[bk][0:64, :])
        r.ew("vector", "tensor_copy", ea[0:64, slot * 1024 + 512: slot * 1024 + 1024],
             pb[bk + 1][0:64, :], sem=pa_ev)
        r.wait("sync", pa_ev, pa_ev.n)
        r.dma("sync",
              bass.AP(ap["s1h"].tensor, d0 * 128, [[2048 * 128, 64], [128, 8], [1, 128]]),
              ea[0:64, slot * 1024:(slot + 1) * 1024], sem=sas[slot])

    xb = E(nc.sbuf_tensor("g0_xb", [128, 4 * 512], FP16))
    eb = E(nc.sbuf_tensor("g0_eb", [128, 4 * 512], BF16))
    xbs = [r.sem("g0_xb%d" % i) for i in range(4)]
    pb_mm = r.sem("g0_pbmm")
    pb_ev = r.sem("g0_pbev")
    sbs = [r.sem("g0_sb%d" % i) for i in range(4)]
    ti = 0
    for dblk in range(4):
        for cp in range(64):
            slot = ti % 4
            if ti >= 4:
                r.wait("sync", pb_ev, ti - 3)
            if cp == 0:
                for k in range(4):
                    r.wait("sync", sas[k], 16 * 16 * (dblk + 1))
            srcap = bass.AP(ap["s1h"].tensor, cp * (2048 * 128) + dblk * 512 * 128,
                            [[128, 512], [1, 128]])
            r.dmat(xb[:, slot * 512:(slot + 1) * 512], srcap, sem=xbs[slot])
            bank = pb[slot]
            r.wait("tensor", xbs[slot], xbs[slot].n)
            if ti < 4:
                r.wait("tensor", pa_ev, NT_A)
            else:
                r.wait("tensor", pb_ev, ti - 3)
            for k in range(4):
                r.mm(bank[:, k * 128:(k + 1) * 128],
                     xb[:, slot * 512 + k * 128: slot * 512 + (k + 1) * 128],
                     h128[:], k == 0, k == 3, sem=(pb_mm if k == 3 else None))
            r.wait("vector", pb_mm, pb_mm.n)
            if ti >= 4:
                r.wait("vector", sbs[slot], sbs[slot].n)
            r.ew("vector", "tensor_scalar_mul", eb[:, slot * 512:(slot + 1) * 512],
                 bank[:], FWHT_SC, sem=pb_ev)
            r.wait("gpsimd", pb_ev, pb_ev.n)
            for k in range(4):
                d0 = dblk * 512 + k * 128
                r.dma("gpsimd",
                      bass.AP(ap["w2cb"].tensor, cp * 262144 + d0 * 128,
                              [[128, 128], [1, 128]]),
                      eb[:, slot * 512 + k * 128: slot * 512 + (k + 1) * 128],
                      sem=sbs[slot])
            ti += 1

    for k in range(4):
        r.wait("gpsimd", sbs[k], sbs[k].n)
        r.wait("sync", sas[k], sas[k].n)



def entry_block(r, nc, ap, pb, idf, xpan, xblk, b, sems):
    xin, tp, evd = sems
    for tc in range(TCH):
        g = b * TCH + tc
        slot = g % 2
        if g >= 2:
            r.wait("sync", tp, (g - 1) * NCD)
        r.dma("sync", xpan[:, slot * 2048:(slot + 1) * 2048],
              ap["x"][g * 128:(g + 1) * 128, :], sem=xin[slot])
        r.wait("tensor", xin[slot], xin[slot].n)
        for dc in range(NCD):
            bank = pb[dc % 2]
            j = g * NCD + dc
            if j >= 2:
                r.wait("tensor", evd, j - 1)
            r.tr(bank[:, 0:128],
                 xpan[:, slot * 2048 + dc * 128: slot * 2048 + (dc + 1) * 128],
                 idf[:], sem=tp)
            r.wait("vector", tp, tp.n)
            r.ew("vector", "tensor_copy",
                 xblk[:, dc * 512 + tc * 128: dc * 512 + tc * 128 + 128],
                 bank[:, 0:128], sem=evd)


def ln_block(r, bufs, banks, sems, b, ev):
    xblk, yblk = bufs["xblk"], bufs["yblk"]
    xb16, sq16 = bufs["xb16"], bufs["sq16"]
    rstd, negmr, lntmp = bufs["rstd"], bufs["negmr"], bufs["lntmp"]
    bm, bs = banks
    s_cast, s_sq, s_slot, s_lnmm, s_stat, s_rstd = (
        sems["ln_cast"], sems["ln_sq"], sems["ln_slot"], sems["ln_mm"],
        sems["ln_stat"], sems["ln_rstd"])
    ones = bufs["ones"]
    if ev.get("xblk_ready"):
        r.wait("scalar", *ev["xblk_ready"])
    for dc in range(NCD):
        slot = dc % 2
        if ev.get("ln_slot%d" % slot):
            r.wait("vector", *ev["ln_slot%d" % slot])
            r.wait("scalar", *ev["ln_slot%d" % slot])
        if dc == 0 and ev.get("yblk"):
            r.wait("vector", *ev["yblk"])
        r.ew("vector", "tensor_copy", xb16[:, slot * 512:(slot + 1) * 512],
             xblk[:, dc * 512:(dc + 1) * 512], sem=s_cast)
        r.act("scalar", sq16[:, slot * 512:(slot + 1) * 512],
              xblk[:, dc * 512:(dc + 1) * 512], AF.Square, sem=s_sq)
        r.wait("tensor", s_cast, s_cast.n)
        r.wait("tensor", s_sq, s_sq.n)
        if dc == 0 and ev.get("ln_banks"):
            r.wait("tensor", *ev["ln_banks"])
        r.mm(bm[:], ones[:], xb16[:, slot * 512:(slot + 1) * 512],
             dc == 0, dc == NCD - 1)
        r.mm(bs[:], ones[:], sq16[:, slot * 512:(slot + 1) * 512],
             dc == 0, dc == NCD - 1,
             sem=(s_lnmm if dc == NCD - 1 else s_slot))
        if dc != NCD - 1:
            ev["ln_slot%d" % slot] = (s_slot, s_slot.n)
    r.wait("vector", s_lnmm, s_lnmm.n)
    r.ew("vector", "tensor_scalar_mul", lntmp[:], bm[:], 1.0 / 2048.0)
    r.ew("vector", "tensor_mul", rstd[:], lntmp[:], lntmp[:])
    r.ew("vector", "scalar_tensor_tensor", negmr[:], bs[:], 1.0 / 2048.0,
         rstd[:], OP.mult, OP.subtract, sem=s_stat)
    ev["ln_banks"] = (s_stat, s_stat.n)
    r.wait("scalar", s_stat, s_stat.n)
    r.act("scalar", rstd[:], negmr[:], AF.Ln, bias=bufs["eps"][:, 0:1])
    r.act("scalar", rstd[:], rstd[:], AF.Exp, scale=-0.5, sem=s_rstd)
    r.wait("vector", s_rstd, s_rstd.n)
    r.ew("vector", "scalar_tensor_tensor", negmr[:], lntmp[:], -1.0, rstd[:],
         OP.mult, OP.mult, sem=s_stat)
    for dc in range(NCD):
        r.ew("vector", "tensor_mul", lntmp[:], xblk[:, dc * 512:(dc + 1) * 512], rstd[:])
        r.ew("vector", "tensor_add", yblk[:, dc * 512:(dc + 1) * 512], lntmp[:],
             negmr[:], sem=(sems["ln_y"] if dc == NCD - 1 else None))


SPANS_A = {-2: (0, 128), -1: (0, 256), 0: (0, 384), 1: (128, 384),
           2: (256, 256), 3: (384, 128)}

PH1 = {"kv": True, "q": True, "sc": True, "wo": True}
DBG_WO = False


def g1a(r, nc, ap, E, pb, C):
    xin = [r.sem("xld0"), r.sem("xld1")]
    tp = r.sem("tp")
    evd = r.sem("evd")
    xout = r.sem("xouts")
    sems = {k: r.sem(k) for k in
            ("ln_cast", "ln_sq", "ln_slot", "ln_mm", "ln_stat", "ln_rstd", "ln_y",
             "pj", "ev", "rp", "sq", "nm", "f", "qk",
             "sc", "ex", "mk", "dn", "pv", "at", "wo", "res")}
    s = sems

    xpan = E(nc.sbuf_tensor("1a_xpan", [128, 2 * 2048], F32))
    xblk = E(nc.sbuf_tensor("1a_xblk", [128, NCD * 512], F32))
    yblk = E(nc.sbuf_tensor("1a_yblk", [128, NCD * 512], BF16))
    xb16 = E(nc.sbuf_tensor("1a_xb16", [128, 2 * 512], BF16))
    sq16 = E(nc.sbuf_tensor("1a_sq16", [128, 2 * 512], BF16))
    rstd = E(nc.sbuf_tensor("1a_rstd", [128, 512], F32))
    negmr = E(nc.sbuf_tensor("1a_negmr", [128, 512], F32))
    lntmp = E(nc.sbuf_tensor("1a_lntmp", [128, 512], F32))
    kbuf = E(nc.sbuf_tensor("1a_kbuf", [128, HKV * 2048], BF16))
    vtm = E(nc.sbuf_tensor("1a_vtm", [128, 16 * 512], BF16))
    qbuf = E(nc.sbuf_tensor("1a_qbuf", [128, 16 * 512], BF16))
    ebuf = E(nc.sbuf_tensor("1a_ebuf", [128, 8 * 512], BF16))
    rt1 = E(nc.sbuf_tensor("1a_rt1", [128, 512], F32))
    rt2 = E(nc.sbuf_tensor("1a_rt2", [128, 512], F32))
    rt3 = E(nc.sbuf_tensor("1a_rt3", [128, 2 * 512], F32))
    sqf = E(nc.sbuf_tensor("1a_sqf", [128, 2 * 512], BF16))
    tln = E(nc.sbuf_tensor("1a_tln", [128, 512], F32))
    fbuf = E(nc.sbuf_tensor("1a_fbuf", [128, 2 * 512], F32))
    exps = E(nc.sbuf_tensor("1a_expS", [128, 6 * 512], BF16))
    zbuf = E(nc.sbuf_tensor("1a_zbuf", [128, 128], BF16))
    atto = E(nc.sbuf_tensor("1a_atto", [128, 16 * 512], BF16))
    rcp = E(nc.sbuf_tensor("1a_rcp", [128, 512], F32))
    dcp = E(nc.sbuf_tensor("1a_dcp", [128, 512], F32))
    csa = E(nc.sbuf_tensor("1a_csa", [128, 2048], F32))
    sna = E(nc.sbuf_tensor("1a_sna", [128, 2048], F32))
    ones, idf = C["ones"], C["idf"]
    p64, tempbc = C["p64"], C["tempbc"]
    tri0, tri2 = C["tri0"], C["tri2"]

    lnbufs = dict(xblk=xblk, yblk=yblk, xb16=xb16, sq16=sq16, rstd=rstd,
                  negmr=negmr, lntmp=lntmp, ones=ones, eps=C["eps"])
    ev = {}

    zrd = r.sem("zrd")
    r.ew("vector", "memset", zbuf[:], 0.0, sem=zrd)
    r.wait("tensor", zrd, zrd.n)
    st1 = Streamer(r, C["ws_a"], 4, 1024, "sync", "wsy")
    st3 = Streamer(r, C["ws_b"], 4, 1024, "sync", "wsc")
    stc = [0]
    def stream(srcap, row0, col0, nrows):
        stx = st1 if stc[0] % 2 == 0 else st3
        stc[0] += 1
        slot, sem2, lv = stx.load(srcap[row0:row0 + nrows, col0:col0 + 128], nrows)
        return stx, slot, sem2, lv
    cwa = r.sem("cwa")
    cw12 = r.sem("cw12")
    cwb = r.sem("cwb")
    for ic in range(16):
        for q in range(4):
            r.dma("gpsimd",
                  bass.AP(ap["w13cb"].tensor,
                          ic * 2097152 + q * 4 * 131072,
                          [[131072, 4], [128, 512], [1, 128]]),
                  ap["w1"][q * 2048:(q + 1) * 2048, ic * 128:(ic + 1) * 128],
                  sem=cw12)
            r.dma("gpsimd",
                  bass.AP(ap["w13cb"].tensor,
                          ic * 2097152 + q * 4 * 131072 + 512 * 128,
                          [[131072, 4], [128, 512], [1, 128]]),
                  ap["w3"][q * 2048:(q + 1) * 2048, ic * 128:(ic + 1) * 128],
                  sem=cw12)
    for srcw, dst, sm in (("wqb", "wqb_bf", cwb), ("wkb", "wkb_bf", cwb),
                          ("wvb", "wvb_bf", cwb), ("wob", "wob_bf", cwb)):
        n = ap[srcw].tensor.shape[0]
        half = n // 2
        r.dma("gpsimd", ap[dst][0:half, :], ap[srcw][0:half, :], sem=sm)
        r.dma("gpsimd", ap[dst][half:n, :], ap[srcw][half:n, :], sem=sm)
    r.wait("sync", cwa, cwa.n)
    tbl = r.sem("tbl")
    r.dma("sync", csa[:], ap["cs_a"], sem=tbl)
    r.dma("sync", sna[:], ap["sn_a"], sem=tbl)
    r.wait("vector", tbl, tbl.n)

    def rope_a(cchunk, bank, b, ready_ev):
        par = cchunk % 2
        ecol = (cchunk % 8) * 512
        r.wait("vector", *ready_ev)
        if ev.get("eb%d" % (cchunk % 8)):
            r.wait("vector", *ev["eb%d" % (cchunk % 8)])
        r.ew("vector", "tensor_copy", ebuf[:, ecol:ecol + 512], bank[:], sem=s["ev"])
        r.wait("tensor", s["ev"], s["ev"].n)
        r.mm(bank[:], p64[:], ebuf[:, ecol:ecol + 512], True, True, sem=s["rp"])
        ev["eb%d" % (cchunk % 8)] = (s["rp"], s["rp"].n)
        r.wait("vector", s["rp"], s["rp"].n)
        r.ew("vector", "tensor_mul", rt1[:], ebuf[:, ecol:ecol + 512],
             csa[:, b * 512:(b + 1) * 512])
        r.ew("vector", "tensor_mul", rt2[:], bank[:], sna[:, b * 512:(b + 1) * 512])
        if ev.get("rt%d" % par):
            r.wait("vector", *ev["rt%d" % par])
        r.ew("vector", "tensor_add", rt3[:, par * 512:(par + 1) * 512],
             rt1[:], rt2[:], sem=s["rp"])
        r.wait("scalar", s["rp"], s["rp"].n)
        if ev.get("sq%d" % par):
            r.wait("scalar", *ev["sq%d" % par])
        r.act("scalar", sqf[:, par * 512:(par + 1) * 512],
              rt3[:, par * 512:(par + 1) * 512], AF.Square, sem=s["sq"])
        ev["rt%d" % par] = (s["sq"], s["sq"].n)
        r.wait("tensor", s["sq"], s["sq"].n)
        r.mm(bank[:], ones[:], sqf[:, par * 512:(par + 1) * 512], True, True,
             sem=s["nm"])
        ev["sq%d" % par] = (s["nm"], s["nm"].n)

    def rope_b(cchunk, bank, dst, dstcol, is_q, head):
        par = cchunk % 2
        r.wait("scalar", s["nm"], s["nm"].n - (1 if False else 0), )
        r.act("scalar", tln[:], bank[:], AF.Ln, bias=C["eps"][:, 1:2])
        if ev.get("fb%d" % par):
            r.wait("scalar", *ev["fb%d" % par])
        r.act("scalar", fbuf[:, par * 512:(par + 1) * 512], tln[:], AF.Exp,
              scale=-0.5, sem=s["f"])
        r.wait("vector", s["f"], s["f"].n)
        if is_q:
            r.ew("vector", "scalar_tensor_tensor", dst[:, dstcol:dstcol + 512],
                 rt3[:, par * 512:(par + 1) * 512], tempbc[:, head:head + 1],
                 fbuf[:, par * 512:(par + 1) * 512], OP.mult, OP.mult, sem=s["qk"])
        else:
            r.ew("vector", "tensor_mul", dst[:, dstcol:dstcol + 512],
                 rt3[:, par * 512:(par + 1) * 512],
                 fbuf[:, par * 512:(par + 1) * 512], sem=s["qk"])
        ev["fb%d" % par] = (s["qk"], s["qk"].n)

    for b in range(NBLK):
        if b > 0 and ev.get("xblk"):
            r.wait("vector", *ev["xblk"])
        entry_block(r, nc, ap, pb, idf, xpan, xblk, b, (xin, tp, evd))
        ev["xblk_ready"] = (evd, evd.n)
        ln_block(r, lnbufs, (pb[2][:], pb[3][:]), sems, b, ev)
        ev["yblk"] = None
        if not PH1["kv"]:
            continue
        r.wait("tensor", s["ln_y"], s["ln_y"].n)
        for i in range(NCD):
            stx, slot, wsem, lv = stream(ap["wkva_bf"], 0, i * 128, 1024)
            r.wait("tensor", wsem, lv)
            if i == 0 and ev.get("pj_banks"):
                r.wait("tensor", *ev["pj_banks"])
            for ts in range(4):
                r.mm(pb[ts][:], yblk[:, i * 512 + ts * 128: i * 512 + ts * 128 + 128],
                     stx.view(slot, 512, 1024),
                     i == 0, i == NCD - 1)
            for kc in range(4):
                last = (kc == 3)
                r.mm(pb[4 + kc][:], stx.view(slot, kc * 128, (kc + 1) * 128),
                     yblk[:, i * 512: (i + 1) * 512],
                     i == 0, i == NCD - 1, sem=(s["pj"] if last else None))
            stx.set_free(slot, s["pj"])
        r.wait("scalar", s["pj"], s["pj"].n)
        for ts in range(4):
            g = b * 4 + ts
            r.act("scalar", vtm[:, g * 512:(g + 1) * 512], pb[ts][:], AF.Copy,
                  sem=s["ev"])
        pj_ev = (s["pj"], s["pj"].n)
        stages = []
        for kc in range(4):
            stages.append(("A", kc))
        rope_a(0, pb[4], b, pj_ev)
        rope_a(1, pb[5], b, pj_ev)
        for kc in range(2, 4):
            rope_b(kc - 2, pb[4 + kc - 2], kbuf, (kc - 2) * 2048 + b * 512, False, 0)
            rope_a(kc, pb[4 + kc], b, pj_ev)
        rope_b(2, pb[6], kbuf, 2 * 2048 + b * 512, False, 0)
        rope_b(3, pb[7], kbuf, 3 * 2048 + b * 512, False, 0)
        ev["pj_banks"] = (s["qk"], s["qk"].n)
        if not PH1["q"]:
            continue
        for half in range(2):
            for i in range(NCD):
                stx, slot, wsem, lv = stream(ap["wqa_bf"], half * 1024, i * 128, 1024)
                r.wait("tensor", wsem, lv)
                if i == 0:
                    r.wait("tensor", *ev["pj_banks"])
                    if half == 0:
                        r.wait("tensor", s["ev"], s["ev"].n)
                for qc8 in range(8):
                    r.mm(pb[qc8][:], stx.view(slot, qc8 * 128, (qc8 + 1) * 128),
                         yblk[:, i * 512:(i + 1) * 512],
                         i == 0, i == NCD - 1,
                         sem=(s["pj"] if qc8 == 7 else None))
                stx.set_free(slot, s["pj"])
            pj_ev = (s["pj"], s["pj"].n)
            rope_a(0, pb[0], b, pj_ev)
            rope_a(1, pb[1], b, pj_ev)
            for qc8 in range(2, 8):
                hprev = half * 8 + qc8 - 2
                rope_b(qc8 - 2, pb[qc8 - 2], qbuf, hprev * 512, True, hprev)
                rope_a(qc8, pb[qc8], b, pj_ev)
            for qc8 in (6, 7):
                h = half * 8 + qc8
                rope_b(qc8, pb[qc8], qbuf, h * 512, True, h)
            ev["pj_banks"] = (s["qk"], s["qk"].n)
        ev["yblk"] = (s["pj"], s["pj"].n)
        kcs = list(range(max(0, 4 * b - 2), 4 * b + 4))
        if not PH1["sc"]:
            continue
        r.wait("tensor", s["qk"], s["qk"].n)
        for h in range(16):
            kvh = h // 4
            par = h % 2
            bS = [pb[0 + 4 * par], pb[1 + 4 * par]]
            bD = pb[2 + 4 * par]
            bO = pb[3 + 4 * par]
            if ev.get("sc_banks%d" % par):
                r.wait("tensor", *ev["sc_banks%d" % par])
            r.wait("scalar", s["pv"], s["pv"].n)
            for i, kc in enumerate(kcs):
                off, span = SPANS_A[kc - 4 * b]
                bank = bS[i % 2]
                if i >= 2:
                    r.wait("tensor", s["ex"], s["ex"].n - 1)
                r.mm(bank[:, 0:span],
                     kbuf[:, kvh * 2048 + kc * 128: kvh * 2048 + (kc + 1) * 128],
                     qbuf[:, h * 512 + off: h * 512 + off + span],
                     True, True, sem=s["sc"])
                r.wait("scalar", s["sc"], s["sc"].n)
                r.act("scalar", exps[:, i * 512 + off: i * 512 + off + span],
                      bank[:, 0:span], AF.Exp,
                      bias=tempbc[:, 16 + h:17 + h], sem=s["ex"])
            r.wait("vector", s["ex"], s["ex"].n)
            mops = []
            for i, kc in enumerate(kcs):
                d = kc - 4 * b
                if 0 <= d <= 3:
                    mops.append((i, d, tri0))
                if 0 <= d + 2 <= 3:
                    mops.append((i, d + 2, tri2))
            for mi, (i, j, tri) in enumerate(mops):
                r.ew("vector", "tensor_mul",
                     exps[:, i * 512 + j * 128: i * 512 + (j + 1) * 128],
                     exps[:, i * 512 + j * 128: i * 512 + (j + 1) * 128],
                     tri[:], sem=(s["mk"] if mi == len(mops) - 1 else None))
            r.wait("tensor", s["mk"], s["mk"].n)
            nl = len(kcs)
            r.mm(bD[:, 0:512], zbuf[:], yblk[:, 0:512], True, False)
            for i, kc in enumerate(kcs):
                off, span = SPANS_A[kc - 4 * b]
                r.mm(bD[:, off:off + span], ones[:],
                     exps[:, i * 512 + off: i * 512 + off + span],
                     False, i == nl - 1, sem=(s["dn"] if i == nl - 1 else None))
            r.mm(bO[:, 0:512], zbuf[:], yblk[:, 0:512], True, False)
            for i, kc in enumerate(kcs):
                off, span = SPANS_A[kc - 4 * b]
                r.mm(bO[:, off:off + span],
                     vtm[:, kc * 512 + kvh * 128: kc * 512 + (kvh + 1) * 128],
                     exps[:, i * 512 + off: i * 512 + off + span],
                     False, i == nl - 1, sem=(s["pv"] if i == nl - 1 else None))
            r.wait("vector", s["dn"], s["dn"].n)
            r.ew("vector", "tensor_copy", dcp[:], bD[:])
            r.ew("vector", "reciprocal", rcp[:], dcp[:])
            r.wait("vector", s["pv"], s["pv"].n)
            r.ew("vector", "tensor_mul", atto[:, h * 512:(h + 1) * 512], bO[:],
                 rcp[:], sem=s["at"])
            ev["sc_banks%d" % par] = (s["at"], s["at"].n)
        if not PH1["wo"]:
            continue
        r.wait("tensor", s["at"], s["at"].n)
        for half in range(2):
            for vc in range(NCD):
                stx, slot, wsem, lv = stream(ap["woa_bf"], half * 1024, vc * 128, 1024)
                r.wait("tensor", wsem, lv)
                if vc == 0:
                    r.wait("tensor", s["res"], s["res"].n)
                for oc8 in range(8):
                    r.mm(pb[oc8][:], stx.view(slot, oc8 * 128, (oc8 + 1) * 128),
                         atto[:, vc * 512:(vc + 1) * 512],
                         vc == 0, vc == NCD - 1,
                         sem=(s["wo"] if oc8 == 7 else None))
                stx.set_free(slot, s["wo"])
            r.wait("vector", s["wo"], s["wo"].n)
            for oc8 in range(8):
                oc = half * 8 + oc8
                r.ew("vector", "tensor_add", xblk[:, oc * 512:(oc + 1) * 512],
                     xblk[:, oc * 512:(oc + 1) * 512], pb[oc8][:],
                     sem=(s["res"] if oc8 == 7 else None))
            ev["wo_banks"] = (s["res"], s["res"].n)
        ev["xblk"] = None
        r.wait("sync", s["res"], s["res"].n)
        for dc in range(NCD):
            r.dma("sync", ap["xscr"][dc * 128:(dc + 1) * 128, b * 512:(b + 1) * 512],
                  xblk[:, dc * 512:(dc + 1) * 512], sem=xout)
        ev["xblk"] = (xout, xout.n)
        if b < NBLK - 1:
            r.wait("tensor", s["res"], s["res"].n)
    r.wait("sync", xout, xout.n)


def ffn_group(r, nc, ap, E, pb, ws_a, ws_b, C, tag):
    s = {k: r.sem(k) for k in
         ("ln_cast", "ln_sq", "ln_slot", "ln_mm", "ln_stat", "ln_rstd", "ln_y",
          "pj1", "pj3", "hp", "sl", "gt", "w2", "res")}
    xld = [r.sem("xld0"), r.sem("xld1")]
    xout = r.sem("xout")

    xblk = E(nc.sbuf_tensor(tag + "xblk", [128, NCD * 512], F32))
    yblk = E(nc.sbuf_tensor(tag + "yblk", [128, NCD * 512], BF16))
    xb16 = E(nc.sbuf_tensor(tag + "xb16", [128, 2 * 512], BF16))
    sq16 = E(nc.sbuf_tensor(tag + "sq16", [128, 2 * 512], BF16))
    rstd = E(nc.sbuf_tensor(tag + "rstd", [128, 512], F32))
    negmr = E(nc.sbuf_tensor(tag + "negmr", [128, 512], F32))
    lntmp = E(nc.sbuf_tensor(tag + "lntmp", [128, 512], F32))
    hbuf = E(nc.sbuf_tensor(tag + "hbuf", [128, NHC * 512], BF16))
    sgb = E(nc.sbuf_tensor(tag + "sgb", [128, 4 * 512], BF16))

    lnbufs = dict(xblk=xblk, yblk=yblk, xb16=xb16, sq16=sq16, rstd=rstd,
                  negmr=negmr, lntmp=lntmp, ones=C["ones"], eps=C["eps"])
    ev = {}
    st1 = Streamer(r, ws_a, 4, 1024, "sync", "wsy")
    st3 = Streamer(r, ws_b, 4, 1024, "sync", "wsc")

    cw12 = r.sem("cw12")
    r.wait("sync", cw12, cw12.n)
    r.wait("scalar", cw12, cw12.n)

    for b in range(NBLK):
        slot_b = b % 2
        if ev.get("xblk"):
            r.wait("sync", *ev["xblk"])
        for dc in range(NCD):
            r.dma("sync", xblk[:, dc * 512:(dc + 1) * 512],
                  ap["xscr"][dc * 128:(dc + 1) * 128, b * 512:(b + 1) * 512],
                  sem=xld[slot_b])
        ev["xblk_ready"] = (xld[slot_b], xld[slot_b].n)
        r.wait("vector", xld[slot_b], xld[slot_b].n)
        ln_block(r, lnbufs, (pb[2][:], pb[3][:]), s, b, ev)
        r.wait("tensor", s["ln_y"], s["ln_y"].n)
        for hcg in range(16):
            for i in range(NCD):
                stx = st1 if i % 2 == 0 else st3
                slw, wws, lvw = stx.load(
                    bass.AP(ap["w13cb"].tensor, i * 2097152 + hcg * 131072,
                            [[128, 1024], [1, 128]]), 1024)
                r.wait("tensor", wws, lvw)
                if i == 0:
                    r.wait("tensor", s["gt"], s["gt"].n)
                for k in range(4):
                    r.mm(pb[k][:], stx.view(slw, k * 128, (k + 1) * 128),
                         yblk[:, i * 512:(i + 1) * 512],
                         i == 0, i == NCD - 1,
                         sem=(s["pj1"] if k == 3 else None))
                for k in range(4):
                    r.mm(pb[4 + k][:], stx.view(slw, 512 + k * 128, 512 + (k + 1) * 128),
                         yblk[:, i * 512:(i + 1) * 512],
                         i == 0, i == NCD - 1,
                         sem=(s["pj3"] if k == 3 else None))
                stx.set_free(slw, s["pj3"])
            r.wait("scalar", s["pj3"], s["pj3"].n)
            for k in range(4):
                r.act("scalar", sgb[:, k * 512:(k + 1) * 512], pb[k][:], AF.Silu,
                      sem=s["sl"])
            r.wait("vector", s["sl"], s["sl"].n)
            for k in range(4):
                hc = hcg * 4 + k
                r.ew("vector", "tensor_mul", hbuf[:, hc * 512:(hc + 1) * 512],
                     sgb[:, k * 512:(k + 1) * 512], pb[4 + k][:], sem=s["gt"])
        r.wait("tensor", s["gt"], s["gt"].n)
        for ohalf in range(2):
            if ohalf == 1:
                r.wait("tensor", s["res"], s["res"].n)
            for hc in range(NHC):
                stx = st1 if hc % 2 == 0 else st3
                sl2, w2s, lv2 = stx.load(
                    bass.AP(ap["w2cb"].tensor, hc * 262144 + ohalf * 1024 * 128,
                            [[128, 1024], [1, 128]]), 1024)
                r.wait("tensor", w2s, lv2)
                for oc8 in range(8):
                    r.mm(pb[oc8][:], stx.view(sl2, oc8 * 128, (oc8 + 1) * 128),
                         hbuf[:, hc * 512:(hc + 1) * 512],
                         hc == 0, hc == NHC - 1,
                         sem=(s["w2"] if oc8 == 7 else None))
                stx.set_free(sl2, s["w2"])
            r.wait("vector", s["w2"], s["w2"].n)
            for oc8 in range(8):
                oc = ohalf * 8 + oc8
                r.ew("vector", "tensor_add", xblk[:, oc * 512:(oc + 1) * 512],
                     xblk[:, oc * 512:(oc + 1) * 512], pb[oc8][:],
                     sem=(s["res"] if oc8 == 7 else None))
        r.wait("gpsimd", s["res"], s["res"].n)
        for dc in range(NCD):
            r.dma("gpsimd", ap["xscr"][dc * 128:(dc + 1) * 128, b * 512:(b + 1) * 512],
                  xblk[:, dc * 512:(dc + 1) * 512], sem=xout)
        ev["xblk"] = (xout, xout.n)
        if b < NBLK - 1:
            r.wait("tensor", s["res"], s["res"].n)
    r.wait("gpsimd", xout, xout.n)
    r.wait("sync", xout, xout.n)
    return xout


def g1c(r, nc, ap, E, pb, ws_a, ws_b, C):
    s = {k: r.sem(k) for k in
         ("ln_cast", "ln_sq", "ln_slot", "ln_mm", "ln_stat", "ln_rstd", "ln_y",
          "pj", "ev", "rp", "st")}
    xld = [r.sem("xld0"), r.sem("xld1")]
    sto = [r.sem("sto0"), r.sem("sto1")]

    xblk = E(nc.sbuf_tensor("1c_xblk", [128, NCD * 512], F32))
    yblk = E(nc.sbuf_tensor("1c_yblk", [128, NCD * 512], BF16))
    xb16 = E(nc.sbuf_tensor("1c_xb16", [128, 2 * 512], BF16))
    sq16 = E(nc.sbuf_tensor("1c_sq16", [128, 2 * 512], BF16))
    rstd = E(nc.sbuf_tensor("1c_rstd", [128, 512], F32))
    negmr = E(nc.sbuf_tensor("1c_negmr", [128, 512], F32))
    lntmp = E(nc.sbuf_tensor("1c_lntmp", [128, 512], F32))
    stg = E(nc.sbuf_tensor("1c_stg", [128, 2 * 8 * 512], BF16))
    ebuf = E(nc.sbuf_tensor("1c_ebuf", [128, 8 * 512], BF16))
    rt1 = E(nc.sbuf_tensor("1c_rt1", [128, 512], F32))
    rt2 = E(nc.sbuf_tensor("1c_rt2", [128, 512], F32))
    csb = E(nc.sbuf_tensor("1c_csb", [128, 2048], F32))
    snb = E(nc.sbuf_tensor("1c_snb", [128, 2048], F32))
    cskb = E(nc.sbuf_tensor("1c_cskb", [128, 2048], F32))
    snkb = E(nc.sbuf_tensor("1c_snkb", [128, 2048], F32))

    tbl = r.sem("tbl")
    for name, buf in (("cs_b", csb), ("sn_b", snb), ("csk_b", cskb), ("snk_b", snkb)):
        r.dma("sync", buf[:], ap[name], sem=tbl)
    r.wait("vector", tbl, tbl.n)
    cwb = r.sem("cwb")
    r.wait("sync", cwb, cwb.n)
    r.wait("scalar", cwb, cwb.n)

    lnbufs = dict(xblk=xblk, yblk=yblk, xb16=xb16, sq16=sq16, rstd=rstd,
                  negmr=negmr, lntmp=lntmp, ones=C["ones"], eps=C["eps"])
    ev = {}
    st1 = Streamer(r, ws_a, 4, 1024, "sync", "wsy")
    st3 = Streamer(r, ws_b, 4, 1024, "sync", "wsc")

    groups = [
        (ap["wqb_bf"], 0, "qn", 0), (ap["wqb_bf"], 1024, "qn", 8),
        (ap["wqb_bf"], 2048, "qr", 0),
        (ap["wkb_bf"], 0, "kn", 0), (ap["wkb_bf"], 1024, "kn", 8),
        (ap["wkb_bf"], 2048, "kr", 0),
        (ap["wvb_bf"], 0, "v", 0), (ap["wvb_bf"], 1024, "v", 1),
    ]

    for b in range(NBLK):
        slot_b = b % 2
        if ev.get("xblk"):
            r.wait("sync", *ev["xblk"])
        for dc in range(NCD):
            r.dma("sync", xblk[:, dc * 512:(dc + 1) * 512],
                  ap["xscr"][dc * 128:(dc + 1) * 128, b * 512:(b + 1) * 512],
                  sem=xld[slot_b])
        ev["xblk_ready"] = (xld[slot_b], xld[slot_b].n)
        r.wait("vector", xld[slot_b], xld[slot_b].n)
        ln_block(r, lnbufs, (pb[2][:], pb[3][:]), s, b, ev)
        r.wait("tensor", s["ln_y"], s["ln_y"].n)
        for gi, (wsrc, row0, mode, arg) in enumerate(groups):
            gslot = gi % 2
            stgo = gslot * 8 * 512
            stx = st1 if gi % 2 == 0 else st3
            if mode == "v":
                for i in range(NCD):
                    slw, wsem, lv = stx.load(
                        wsrc[row0:row0 + 1024, i * 128:(i + 1) * 128], 1024)
                    r.wait("tensor", wsem, lv)
                    if i == 0 and ev.get("pj_banks"):
                        r.wait("tensor", *ev["pj_banks"])
                    for tsub in range(4):
                        for vfb2 in range(2):
                            r.mm(pb[tsub * 2 + vfb2][:],
                                 yblk[:, i * 512 + tsub * 128: i * 512 + tsub * 128 + 128],
                                 stx.view(slw, vfb2 * 512, (vfb2 + 1) * 512),
                                 i == 0, i == NCD - 1,
                                 sem=(s["pj"] if (tsub, vfb2) == (3, 1) else None))
                    stx.set_free(slw, s["pj"])
            else:
                for i in range(NCD):
                    slw, wsem, lv = stx.load(
                        wsrc[row0:row0 + 1024, i * 128:(i + 1) * 128], 1024)
                    r.wait("tensor", wsem, lv)
                    if i == 0 and ev.get("pj_banks"):
                        r.wait("tensor", *ev["pj_banks"])
                    for c8 in range(8):
                        r.mm(pb[c8][:], stx.view(slw, c8 * 128, (c8 + 1) * 128),
                             yblk[:, i * 512:(i + 1) * 512],
                             i == 0, i == NCD - 1,
                             sem=(s["pj"] if c8 == 7 else None))
                    stx.set_free(slw, s["pj"])
            pj_ev = (s["pj"], s["pj"].n)
            if ev.get("stg%d" % gslot):
                r.wait("scalar", *ev["stg%d" % gslot])
                r.wait("vector", *ev["stg%d" % gslot])
            if mode in ("qn", "kn", "v"):
                r.wait("scalar", s["pj"], s["pj"].n)
                sc = RSQ_DN if mode == "kn" else 1.0
                for c8 in range(8):
                    r.act("scalar", stg[:, stgo + c8 * 512: stgo + (c8 + 1) * 512],
                          pb[c8][:], AF.Copy, scale=sc, sem=s["ev"])
                r.wait("gpsimd", s["ev"], s["ev"].n)
            else:
                for c8 in range(8):
                    r.wait("vector", *pj_ev)
                    if ev.get("eb%d" % c8):
                        r.wait("vector", *ev["eb%d" % c8])
                    r.ew("vector", "tensor_copy", ebuf[:, c8 * 512:(c8 + 1) * 512],
                         pb[c8][:], sem=s["ev"])
                    r.wait("tensor", s["ev"], s["ev"].n)
                    r.mm(pb[c8][:], C["p32"][:], ebuf[:, c8 * 512:(c8 + 1) * 512],
                         True, True, sem=s["rp"])
                    ev["eb%d" % c8] = (s["rp"], s["rp"].n)
                    ct = csb if mode == "qr" else cskb
                    st_ = snb if mode == "qr" else snkb
                    r.wait("vector", s["rp"], s["rp"].n)
                    r.ew("vector", "tensor_mul", rt1[:],
                         ebuf[:, c8 * 512:(c8 + 1) * 512],
                         ct[:, b * 512:(b + 1) * 512])
                    r.ew("vector", "tensor_mul", rt2[:], pb[c8][:],
                         st_[:, b * 512:(b + 1) * 512])
                    r.ew("vector", "tensor_add",
                         stg[:, stgo + c8 * 512: stgo + (c8 + 1) * 512],
                         rt1[:], rt2[:], sem=s["ev"])
                r.wait("gpsimd", s["ev"], s["ev"].n)
            ev["pj_banks"] = (s["ev"], s["ev"].n)
            for c8 in range(8):
                srcap = stg[:, stgo + c8 * 512: stgo + (c8 + 1) * 512]
                if mode == "qn":
                    ch = arg + c8
                    dst = bass.AP(ap["qscr"].tensor, ch * 128 * S + b * 512,
                                  [[S, 128], [1, 512]])
                elif mode == "qr":
                    ch = 16 + c8
                    dst = bass.AP(ap["qscr"].tensor, ch * 128 * S + b * 512,
                                  [[S, 128], [1, 512]])
                elif mode in ("kn", "kr"):
                    colb = (2048 + c8 * 128) if mode == "kr" else (arg + c8) * 128
                    dst = bass.AP(ap["kscr"].tensor, b * 4 * 128 * 3072 + colb,
                                  [[3072, 128], [128 * 3072, 4], [1, 128]])
                else:
                    tsub = c8 // 2; vfb2 = c8 % 2
                    kc = b * 4 + tsub
                    colb = arg * 1024 + vfb2 * 512
                    dst = bass.AP(ap["vscr"].tensor, kc * 128 * 2048 + colb,
                                  [[2048, 128], [1, 512]])
                r.dma("gpsimd", dst, srcap, sem=sto[gslot])
            ev["stg%d" % gslot] = (sto[gslot], sto[gslot].n)
        ev["xblk"] = None
        ev["yblk"] = (s["pj"], s["pj"].n)
    r.wait("gpsimd", sto[0], sto[0].n)
    r.wait("gpsimd", sto[1], sto[1].n)
    r.wait("sync", sto[0], sto[0].n)
    r.wait("sync", sto[1], sto[1].n)


def g2a(r, nc, ap, E, pb, ws_a, c_seld, c_ones):
    s = {k: r.sem(k) for k in
         ("ql", "sc", "ex", "pvc", "oc", "dn", "dloc", "rc", "exf", "at",
          "wo", "res", "zp")}
    xld = [r.sem("xld0"), r.sem("xld1")]
    kld = [r.sem("kld0"), r.sem("kld1")]
    wl = [r.sem("wl0"), r.sem("wl1")]
    xout = r.sem("xout")

    qbn = E(nc.sbuf_tensor("2a_qbn", [128, 16 * 512], BF16))
    qrp = E(nc.sbuf_tensor("2a_qrp", [128, 16 * 512], BF16))
    kbb = E(nc.sbuf_tensor("2a_kbb", [128, 8 * 3072], BF16))
    vbb = E(nc.sbuf_tensor("2a_vbb", [128, 8 * 2048], BF16))
    expt = E(nc.sbuf_tensor("2a_expt", [128, 2 * 512], BF16))
    obuf = E(nc.sbuf_tensor("2a_obuf", [128, 16 * 512], BF16))
    dloc = E(nc.sbuf_tensor("2a_dloc", [16, 512], F32))
    drec = E(nc.sbuf_tensor("2a_drec", [16, 512], F32))
    xblk = E(nc.sbuf_tensor("2a_xblk", [128, NCD * 512], F32))
    expd = E(nc.sbuf_tensor("2a_expd", [16, 2048], F32))
    dbgs = E(nc.sbuf_tensor("2a_dbgs", [128, 512], F32)) if DBG_WO else None
    dbsem = r.sem("dbsem") if DBG_WO else None
    dbsd = r.sem("dbsd") if DBG_WO else None

    tbl = r.sem("tbl")
    r.dma("sync", expd[:], ap["expd"], sem=tbl)
    r.wait("tensor", tbl, tbl.n)
    cwb = r.sem("cwb")
    r.wait("sync", cwb, cwb.n)
    for h in range(16):
        if h % 2 == 0:
            r.ew("vector", "memset", qrp[64:128, h * 512:(h + 1) * 512], 0.0,
                 sem=(s["zp"] if h == 15 else None))
        else:
            r.ew("vector", "memset", qrp[0:64, h * 512:(h + 1) * 512], 0.0,
                 sem=(s["zp"] if h == 15 else None))

    ev = {}
    wtile = [0]

    def stream(row0, col0):
        w = wtile[0]; wtile[0] += 1
        slot = w % 2
        if ev.get("ws%d" % slot):
            r.wait("sync", *ev["ws%d" % slot])
        r.dmat(ws_a[:, slot * 2048: slot * 2048 + 1024],
               ap["wob_bf"][row0:row0 + 1024, col0:col0 + 128], sem=wl[slot])
        return slot, wl[slot], wl[slot].n

    for qb in range(NBLK):
        qslot = qb % 2
        if ev.get("qbuf"):
            r.wait("sync", *ev["qbuf"])
        for ch in range(16):
            r.dma("sync", qbn[:, ch * 512:(ch + 1) * 512],
                  bass.AP(ap["qscr"].tensor, ch * 128 * S + qb * 512,
                          [[S, 128], [1, 512]]), sem=s["ql"])
        for c in range(8):
            r.dma("sync", qrp[0:64, (2 * c) * 512:(2 * c + 1) * 512],
                  bass.AP(ap["qscr"].tensor, (16 + c) * 128 * S + qb * 512,
                          [[S, 64], [1, 512]]), sem=s["ql"])
            r.dma("sync", qrp[64:128, (2 * c + 1) * 512:(2 * c + 2) * 512],
                  bass.AP(ap["qscr"].tensor, (16 + c) * 128 * S + 64 * S + qb * 512,
                          [[S, 64], [1, 512]]), sem=s["ql"])
        if ev.get("xblk"):
            r.wait("sync", *ev["xblk"])
        for dc in range(NCD):
            r.dma("sync", xblk[:, dc * 512:(dc + 1) * 512],
                  ap["xscr"][dc * 128:(dc + 1) * 128, qb * 512:(qb + 1) * 512],
                  sem=xld[qslot])
        ql_ev = (s["ql"], s["ql"].n)
        for kh in range(2):
            kslot = kh
            if ev.get("kvb"):
                r.wait("sync", *ev["kvb"])
            for kc8 in range(8):
                kc = kh * 8 + kc8
                r.dma("sync", kbb[:, kc8 * 3072:(kc8 + 1) * 3072],
                      bass.AP(ap["kscr"].tensor, kc * 128 * 3072,
                              [[3072, 128], [1, 3072]]), sem=kld[kslot])
                r.dma("sync", vbb[:, kc8 * 2048:(kc8 + 1) * 2048],
                      bass.AP(ap["vscr"].tensor, kc * 128 * 2048,
                              [[2048, 128], [1, 2048]]), sem=kld[kslot])
            r.wait("tensor", kld[kslot], kld[kslot].n)
            r.wait("tensor", *ql_ev)
            if kh == 0:
                r.wait("tensor", s["zp"], s["zp"].n)
            r.wait("tensor", s["dloc"], s["dloc"].n)
            bD = pb[4 + kh]
            def sc_emit(ii):
                h, kc8 = divmod(ii, 8)
                bank = pb[ii % 2]
                if ev.get("sb%d" % (ii % 2)):
                    r.wait("tensor", *ev["sb%d" % (ii % 2)])
                r.mm(bank[:], kbb[:, kc8 * 3072 + h * 128: kc8 * 3072 + (h + 1) * 128],
                     qbn[:, h * 512:(h + 1) * 512], True, False)
                r.mm(bank[:],
                     kbb[:, kc8 * 3072 + 2048 + (h // 2) * 128: kc8 * 3072 + 2048 + (h // 2 + 1) * 128],
                     qrp[:, h * 512:(h + 1) * 512], False, True, sem=s["sc"])
                r.wait("scalar", s["sc"], s["sc"].n)
                if ev.get("et%d" % (ii % 2)):
                    r.wait("scalar", *ev["et%d" % (ii % 2)])
                r.act("scalar", expt[:, (ii % 2) * 512: (ii % 2 + 1) * 512],
                      bank[:], AF.Exp, sem=s["ex"])
                ev["sb%d" % (ii % 2)] = (s["ex"], s["ex"].n)
                ev["exv%d" % (ii % 2)] = (s["ex"], s["ex"].n)

            def dnpv_emit(ii):
                h, kc8 = divmod(ii, 8)
                bO = pb[2 + (h % 2)]
                if kc8 == 0 and s["oc"].n >= 1:
                    r.wait("tensor", s["oc"], s["oc"].n - 1)
                r.wait("tensor", *ev["exv%d" % (ii % 2)])
                r.mm(bD[0:16, :], c_seld[:, h * 16:(h + 1) * 16],
                     expt[:, (ii % 2) * 512: (ii % 2 + 1) * 512],
                     ii == 0, ii == 127, sem=(s["dn"] if ii == 127 else None))
                r.mm(bO[:], vbb[:, kc8 * 2048 + h * 128: kc8 * 2048 + (h + 1) * 128],
                     expt[:, (ii % 2) * 512: (ii % 2 + 1) * 512],
                     kc8 == 0, kc8 == 7, sem=s["pvc"])
                ev["et%d" % (ii % 2)] = (s["pvc"], s["pvc"].n)
                if kc8 == 7:
                    r.wait("vector", s["pvc"], s["pvc"].n)
                    if kh == 0:
                        r.ew("vector", "tensor_copy", obuf[:, h * 512:(h + 1) * 512],
                             bO[:], sem=s["oc"])
                    else:
                        r.ew("vector", "tensor_add", obuf[:, h * 512:(h + 1) * 512],
                             obuf[:, h * 512:(h + 1) * 512], bO[:], sem=s["oc"])

            for ii in range(128):
                sc_emit(ii)
                if ii > 0:
                    dnpv_emit(ii - 1)
            dnpv_emit(127)
            r.wait("vector", s["dn"], s["dn"].n)
            if kh == 0:
                r.ew("vector", "tensor_copy", dloc[:], bD[0:16, :], sem=s["dloc"])
            else:
                r.ew("vector", "tensor_add", dloc[:], dloc[:], bD[0:16, :],
                     sem=s["dloc"])
            ev["kvb"] = (s["oc"], s["oc"].n)
        r.wait("vector", s["dloc"], s["dloc"].n)
        r.ew("vector", "reciprocal", drec[:], dloc[:], sem=s["rc"])
        r.wait("tensor", s["rc"], s["rc"].n)
        for h in range(16):
            bE = pb[6 + (h % 2)]
            if ev.get("be%d" % (h % 2)):
                r.wait("tensor", *ev["be%d" % (h % 2)])
            r.mm(bE[:], expd[:, h * 128:(h + 1) * 128], drec[:], True, True,
                 sem=s["exf"])
            r.wait("vector", s["exf"], s["exf"].n)
            r.ew("vector", "tensor_mul", obuf[:, h * 512:(h + 1) * 512],
                 obuf[:, h * 512:(h + 1) * 512], bE[:], sem=s["at"])
            ev["be%d" % (h % 2)] = (s["at"], s["at"].n)
        ev["qbuf"] = (s["at"], s["at"].n)
        r.wait("tensor", s["at"], s["at"].n)
        r.wait("tensor", xld[qslot], xld[qslot].n)
        for half in range(2):
            for vc in range(NCD):
                slw, wsem, lv = stream(half * 1024, vc * 128)
                r.wait("tensor", wsem, lv)
                if vc == 0:
                    r.wait("tensor", s["res"], s["res"].n)
                for oc8 in range(8):
                    r.mm(pb[oc8][:],
                         ws_a[:, slw * 2048 + oc8 * 128: slw * 2048 + (oc8 + 1) * 128],
                         obuf[:, vc * 512:(vc + 1) * 512],
                         vc == 0, vc == NCD - 1,
                         sem=(s["wo"] if oc8 == 7 else None))
                ev["ws%d" % slw] = (s["wo"], s["wo"].n)
            r.wait("vector", s["wo"], s["wo"].n)
            r.wait("vector", xld[qslot], xld[qslot].n)
            for oc8 in range(8):
                oc = half * 8 + oc8
                if DBG_WO:
                    r.ew("vector", "tensor_copy", dbgs[:], pb[oc8][:], sem=dbsem)
                    r.wait("gpsimd", dbsem, dbsem.n)
                    r.dma("gpsimd", ap["dbg"][oc * 128:(oc + 1) * 128,
                                              qb * 512:(qb + 1) * 512],
                          dbgs[:], sem=dbsd)
                    r.wait("vector", dbsd, dbsd.n)
                r.ew("vector", "tensor_add", xblk[:, oc * 512:(oc + 1) * 512],
                     xblk[:, oc * 512:(oc + 1) * 512], pb[oc8][:],
                     sem=(s["res"] if oc8 == 7 else None))
        r.wait("gpsimd", s["res"], s["res"].n)
        for dc in range(NCD):
            r.dma("gpsimd", ap["xscr"][dc * 128:(dc + 1) * 128, qb * 512:(qb + 1) * 512],
                  xblk[:, dc * 512:(dc + 1) * 512], sem=xout)
        ev["xblk"] = (xout, xout.n)
        if qb < NBLK - 1:
            r.wait("tensor", s["res"], s["res"].n)
    r.wait("gpsimd", xout, xout.n)
    r.wait("sync", xout, xout.n)



def g2b_exit(r, nc, ap, E, pb, c_idf):
    xin = [r.sem("xld0"), r.sem("xld1")]
    tp = r.sem("tp")
    evd = r.sem("evd")
    ost = r.sem("xout")

    xcol = E(nc.sbuf_tensor("2b_xcol", [128, 2 * 2048], F32))
    opan = E(nc.sbuf_tensor("2b_opan", [128, 2 * 2048], F32))

    tp_base = tp.n
    evd_base = evd.n
    ost_base = ost.n
    for g in range(16):
        slot = g % 2
        if g >= 2:
            r.wait("sync", tp, tp_base + (g - 1) * NCD)
        for dc in range(NCD):
            r.dma("sync", xcol[:, slot * 2048 + dc * 128: slot * 2048 + (dc + 1) * 128],
                  ap["xscr"][dc * 128:(dc + 1) * 128, g * 128:(g + 1) * 128],
                  sem=xin[slot])
        r.wait("tensor", xin[slot], xin[slot].n)
        for dc in range(NCD):
            bank = pb[dc % 2]
            j = g * NCD + dc
            if j >= 2:
                r.wait("tensor", evd, evd_base + j - 1)
            r.tr(bank[:, 0:128],
                 xcol[:, slot * 2048 + dc * 128: slot * 2048 + (dc + 1) * 128],
                 c_idf[:], sem=tp)
            r.wait("vector", tp, tp.n)
            if g >= 2 and dc == 0:
                r.wait("vector", ost, ost_base + 16 * (g - 1))
            r.ew("vector", "tensor_copy",
                 opan[:, slot * 2048 + dc * 128: slot * 2048 + (dc + 1) * 128],
                 bank[:, 0:128], sem=evd)
        r.wait("gpsimd", evd, evd.n)
        r.dma("gpsimd", ap["out"][g * 128:(g + 1) * 128, :],
              opan[:, slot * 2048:(slot + 1) * 2048], sem=ost)
    r.wait("gpsimd", ost, ost.n)
    r.wait("sync", ost, ost.n)



_NC_CACHE = [None]

def kernel(L0_attn_a_x, L0_attn_a_Wq, L0_attn_a_Wkv, L0_attn_a_Wo,
           L0_attn_a_qk_temp, L0_attn_b_Wq, L0_attn_b_Wk, L0_attn_b_Wv,
           L0_attn_b_Wo, L0_ffn_W1, L0_ffn_W2, L0_ffn_W3):
    x = np.ascontiguousarray(np.asarray(L0_attn_a_x, dtype=np.float32))
    shared = {
        "wqa": np.ascontiguousarray(L0_attn_a_Wq, dtype=np.float32),
        "wkva": np.ascontiguousarray(L0_attn_a_Wkv, dtype=np.float32),
        "woa": np.ascontiguousarray(L0_attn_a_Wo, dtype=np.float32),
        "temp": np.ascontiguousarray(np.asarray(L0_attn_a_qk_temp, dtype=np.float32).reshape(16, 1)),
        "wqb": np.ascontiguousarray(L0_attn_b_Wq, dtype=np.float32),
        "wkb": np.ascontiguousarray(L0_attn_b_Wk, dtype=np.float32),
        "wvb": np.ascontiguousarray(L0_attn_b_Wv, dtype=np.float32),
        "wob": np.ascontiguousarray(L0_attn_b_Wo, dtype=np.float32),
        "w1": np.ascontiguousarray(L0_ffn_W1, dtype=np.float32),
        "w2": np.ascontiguousarray(L0_ffn_W2, dtype=np.float32),
        "w3": np.ascontiguousarray(L0_ffn_W3, dtype=np.float32),
    }
    shared.update(_CONSTS)
    if _NC_CACHE[0] is None:
        _NC_CACHE[0] = build_program()
    nc = _NC_CACHE[0]
    in_maps = [dict(shared, x=x[i]) for i in range(8)]
    res = run_bass_kernel_spmd(nc, in_maps, core_ids=list(range(8)))
    return np.stack([np.asarray(res.results[i]["out"]) for i in range(8)], axis=0)


# version 08
# speedup vs baseline: 1.4083x; 1.0348x over previous
import contextlib
import math
import numpy as np
import ml_dtypes
import concourse.bass as bass
import concourse.mybir as mybir
from concourse.bass_utils import run_bass_kernel_spmd

F32 = mybir.dt.float32
BF16 = mybir.dt.bfloat16
FP16 = mybir.dt.float16
AF = mybir.ActivationFunctionType
OP = mybir.AluOpType
BFD = ml_dtypes.bfloat16

ENGS = ("sync", "tensor", "vector", "scalar", "gpsimd")

S = 2048
DM = 2048
NCD = 16
TB = 512
NBLK = 4
TCH = 4
HA = 16
HKV = 4
WIN = 256
HBH = 16
DN = 128
DR = 64
DV = 128
HID = 8192
NHC = 64
RSQ_DN = 1.0 / math.sqrt(128.0)
FWHT_SC = 1.0 / math.sqrt(8192.0)


class Sem:
    __slots__ = ("h", "n")
    def __init__(self, h):
        self.h = h
        self.n = 0


class Streamer:
    def __init__(self, r, buf, nslots, span, eng, name):
        self.r = r
        self.buf = buf
        self.nslots = nslots
        self.span = span
        self.eng = eng
        self.sems = [r.sem(f"{name}{i}") for i in range(nslots)]
        self.free = [None] * nslots
        self.w = 0

    def load(self, src_ap, ncols):
        slot = self.w % self.nslots
        self.w += 1
        if self.free[slot] is not None:
            self.r.wait(self.eng, *self.free[slot])
        self.r.dmat(self.buf[:, slot * self.span: slot * self.span + ncols],
                    src_ap, sem=self.sems[slot], eng=self.eng)
        return slot, self.sems[slot], self.sems[slot].n

    def view(self, slot, c0, c1):
        return self.buf[:, slot * self.span + c0: slot * self.span + c1]

    def set_free(self, slot, sem):
        self.free[slot] = (sem, sem.n)


class Rec:
    def __init__(self, nc, stack):
        self.nc = nc
        self.ops = {e: [] for e in ENGS}
        self.stack = stack
        self._semid = 0

    def sem(self, name=None):
        self._semid += 1
        if name is None:
            name = f"s{self._semid}"
        if not hasattr(self, "_semcache"):
            self._semcache = {}
        if name not in self._semcache:
            h = self.stack.enter_context(self.nc.semaphore(name))
            self._semcache[name] = Sem(h)
        return self._semcache[name]

    def emit(self, eng, fn):
        self.ops[eng].append(fn)

    def wait(self, eng, sem, val=None):
        v = sem.n if val is None else val
        if v <= 0:
            return
        self.emit(eng, lambda e, h=sem.h, v=v: e.wait_ge(h, v))

    def inc(self, eng, sem, k=1):
        sem.n += k
        self.emit(eng, lambda e, h=sem.h, k=k: e.nop().then_inc(h, k))

    def _fin(self, inst, sem, k):
        if sem is not None:
            inst.then_inc(sem.h, k)

    def dma(self, eng, out, in_, sem=None, **kw):
        if sem is not None:
            sem.n += 16
        def f(e, out=out, in_=in_, s=sem, kw=kw):
            i = e.dma_start(out=out, in_=in_, **kw)
            if s is not None:
                i.then_inc(s.h, 16)
        self.emit(eng, f)

    def dmat(self, out, in_, sem=None, eng="sync"):
        if sem is not None:
            sem.n += 16
        def f(e, out=out, in_=in_, s=sem):
            i = e.dma_start_transpose(out, in_)
            if s is not None:
                i.then_inc(s.h, 16)
        self.emit(eng, f)

    def mm(self, out, lhsT, rhs, start, stop, sem=None, k=1):
        if sem is not None:
            sem.n += k
        def f(e, out=out, lhsT=lhsT, rhs=rhs, st=start, sp=stop, s=sem, k=k):
            i = e.matmul(out, lhsT, rhs, start=st, stop=sp)
            if s is not None:
                i.then_inc(s.h, k)
        self.emit("tensor", f)

    def tr(self, out, in_, iden, sem=None):
        if sem is not None:
            sem.n += 1
        def f(e, out=out, in_=in_, iden=iden, s=sem):
            i = e.transpose(out, in_, iden)
            if s is not None:
                i.then_inc(s.h, 1)
        self.emit("tensor", f)

    def act(self, eng, out, in_, func, bias=0.0, scale=1.0, sem=None):
        if sem is not None:
            sem.n += 1
        def f(e, out=out, in_=in_, func=func, b=bias, sc=scale, s=sem):
            i = e.activation(out, in_, func, bias=b, scale=sc)
            if s is not None:
                i.then_inc(s.h, 1)
        self.emit(eng, f)

    def ew(self, eng, name, *args, sem=None, **kw):
        if sem is not None:
            sem.n += 1
        def f(e, name=name, args=args, s=sem, kw=kw):
            i = getattr(e, name)(*args, **kw)
            if s is not None:
                i.then_inc(s.h, 1)
        self.emit(eng, f)

    def barrier(self, bsem, engines=ENGS):
        base = bsem.n
        for e in engines:
            self.emit(e, lambda en, h=bsem.h: en.nop().then_inc(h, 1))
        bsem.n = base + len(engines)
        for e in engines:
            self.wait(e, bsem)

    def replay(self):
        with self.nc.Block() as blk:
            mk = {}
            def bind(ename):
                ops = self.ops[ename]
                def runner(eng):
                    for f in ops:
                        f(eng)
                return runner
            blk.sync(bind("sync"))
            blk.tensor(bind("tensor"))
            blk.vector(bind("vector"))
            blk.scalar(bind("scalar"))
            blk.gpsimd(bind("gpsimd"))
        self.ops = {e: [] for e in ENGS}



def _rope_tables_a():
    inv = 1.0 / (10000.0 ** (np.arange(0, 128, 2, dtype=np.float64) / 128.0))
    pos = np.arange(S, dtype=np.float64)
    ang = pos[None, :] * inv[:, None]
    c = np.cos(ang); s = np.sin(ang)
    C = np.concatenate([c, c], axis=0).astype(np.float32)
    Sp = np.concatenate([-s, s], axis=0).astype(np.float32)
    return C, Sp


def _rope_tables_b():
    inv = 1.0 / (10000.0 ** (np.arange(0, 64, 2, dtype=np.float64) / 64.0))
    pos = np.arange(S, dtype=np.float64)
    ang = pos[None, :] * inv[:, None]
    c = np.cos(ang); s = np.sin(ang)
    C = np.concatenate([c, c, c, c], axis=0).astype(np.float32)
    Sp = np.concatenate([-s, s, -s, s], axis=0).astype(np.float32)
    return C, Sp


def _consts():
    d = {}
    C_A, S_A = _rope_tables_a()
    C_B, S_B = _rope_tables_b()
    d["cs_a"] = C_A.astype(BFD); d["sn_a"] = S_A.astype(BFD)
    d["cs_b"] = C_B; d["sn_b"] = S_B
    d["csk_b"] = (C_B * 0.125).astype(np.float32)
    d["snk_b"] = (S_B * 0.125).astype(np.float32)
    ii = np.arange(128)
    d["p64"] = np.eye(128, dtype=np.float32)[ii ^ 64].T.astype(BFD)
    d["p32"] = np.eye(128, dtype=np.float32)[ii ^ 32].T.astype(BFD)
    ki = ii[:, None]; qi = ii[None, :]
    d["tri0"] = (ki <= qi).astype(np.float32).astype(BFD)
    d["tri2"] = (ki > qi).astype(np.float32).astype(BFD)
    d["ones"] = np.ones((128, 128), dtype=np.float32).astype(BFD)
    d["idb"] = np.eye(128, dtype=np.float32).astype(BFD)
    d["idf"] = np.eye(128, dtype=np.float32)
    d["h64"] = ((-1.0) ** np.array([[bin(a & b).count("1") for b in range(64)]
                                    for a in range(64)])).astype(np.float32).astype(BFD)
    d["h128"] = ((-1.0) ** np.array([[bin(a & b).count("1") for b in range(128)]
                                     for a in range(128)])).astype(np.float32).astype(np.float16)
    seld = np.zeros((128, 16 * 16), dtype=np.float32)
    for h in range(16):
        seld[:, h * 16 + h] = 1.0
    d["seld"] = seld.astype(BFD)
    expd = np.zeros((16, 16 * 128), dtype=np.float32)
    for h in range(16):
        expd[h, h * 128:(h + 1) * 128] = 1.0
    d["expd"] = expd
    return d


_CONSTS = _consts()



def build_program():
    nc = bass.Bass("TRN2", target_bir_lowering=False,
                   detect_race_conditions=False)

    ap = {}
    def din(name, shape, dt):
        ap[name] = nc.dram_tensor(name, list(shape), dt, kind="ExternalInput").ap()
    def dscr(name, shape, dt):
        ap[name] = nc.dram_tensor(name, list(shape), dt, kind="Internal").ap()

    din("x", (S, DM), F32)
    din("wqa", (2048, 2048), F32)
    din("wkva", (1024, 2048), F32)
    din("woa", (2048, 2048), F32)
    din("temp", (16, 1), F32)
    din("wqb", (3072, 2048), F32)
    din("wkb", (3072, 2048), F32)
    din("wvb", (2048, 2048), F32)
    din("wob", (2048, 2048), F32)
    din("w1", (8192, 2048), F32)
    din("w2", (2048, 8192), F32)
    din("w3", (8192, 2048), F32)
    for n, v in _CONSTS.items():
        dt = F32 if v.dtype == np.float32 else (BF16 if v.dtype == BFD else FP16)
        din(n, v.shape, dt)
    ap["out"] = nc.dram_tensor("out", [S, DM], F32, kind="ExternalOutput").ap()

    dscr("xscr", (DM, S), F32)
    dscr("wqa_bf", (2048, 2048), BF16)
    dscr("wkva_bf", (1024, 2048), BF16)
    dscr("woa_bf", (2048, 2048), BF16)
    dscr("wqb_bf", (3072, 2048), BF16)
    dscr("wkb_bf", (3072, 2048), BF16)
    dscr("wvb_bf", (2048, 2048), BF16)
    dscr("wob_bf", (2048, 2048), BF16)
    dscr("w13cb", (16, 16, 1024, 128), BF16)
    dscr("s1h", (64, 2048 * 128), FP16)
    dscr("w2cb", (64, 2048, 128), BF16)
    dscr("t16", (2, 16), F32)
    dscr("kscr", (16, 128, 3072), BF16)
    dscr("vscr", (16, 128, 2048), BF16)
    dscr("qscr", (24, 128, S), BF16)
    dscr("dbg", (DM, S), F32)

    stack = contextlib.ExitStack()
    with stack:
        r = Rec(nc, stack)
        E = stack.enter_context

        pb = [E(nc.psum_tensor(f"pb{i}", [128, 512], F32)) for i in range(8)]
        c_ones = E(nc.sbuf_tensor("c_ones", [128, 128], BF16))
        c_idb = E(nc.sbuf_tensor("c_idb", [128, 128], BF16))
        c_idf = E(nc.sbuf_tensor("c_idf", [128, 128], F32))
        c_tri0 = E(nc.sbuf_tensor("c_tri0", [128, 128], BF16))
        c_tri2 = E(nc.sbuf_tensor("c_tri2", [128, 128], BF16))
        c_p64 = E(nc.sbuf_tensor("c_p64", [128, 128], BF16))
        c_p32 = E(nc.sbuf_tensor("c_p32", [128, 128], BF16))
        c_seld = E(nc.sbuf_tensor("c_seld", [128, 256], BF16))
        c_tempbc = E(nc.sbuf_tensor("c_tempbc", [128, 32], F32))
        c_eps = E(nc.sbuf_tensor("c_eps", [128, 2], F32))
        ws_a = E(nc.sbuf_tensor("ws_a", [128, 2 * 2048], BF16))
        ws_b = E(nc.sbuf_tensor("ws_b", [128, 2 * 2048], BF16))

        bar = r.sem("bar")

        with contextlib.ExitStack() as sc:
            g0(r, nc, ap, sc.enter_context, pb,
               dict(ones=c_ones, idb=c_idb, idf=c_idf, tri0=c_tri0,
                    tri2=c_tri2, p64=c_p64, p32=c_p32, seld=c_seld,
                    tempbc=c_tempbc, eps=c_eps))
            r.barrier(bar)
            r.replay()

        with contextlib.ExitStack() as sc:
            g1a(r, nc, ap, sc.enter_context, pb,
                dict(ones=c_ones, idf=c_idf, tri0=c_tri0, tri2=c_tri2,
                     p64=c_p64, tempbc=c_tempbc, ws_a=ws_a, ws_b=ws_b,
                     eps=c_eps))
            r.barrier(bar)
            r.replay()

        cdict = dict(ones=c_ones, eps=c_eps)
        with contextlib.ExitStack() as sc:
            ffn_group(r, nc, ap, sc.enter_context, pb, ws_a, ws_b, cdict, "2f_")
            r.barrier(bar)
            r.replay()

        with contextlib.ExitStack() as sc:
            g1c(r, nc, ap, sc.enter_context, pb, ws_a, ws_b,
                dict(ones=c_ones, eps=c_eps, p32=c_p32))
            r.barrier(bar)
            r.replay()

        with contextlib.ExitStack() as sc:
            g2a(r, nc, ap, sc.enter_context, pb, ws_a, c_seld, c_ones)
            r.barrier(bar)
            r.replay()

        cdict4 = dict(cdict)
        cdict4["idf"] = c_idf
        with contextlib.ExitStack() as sc:
            ffn_group(r, nc, ap, sc.enter_context, pb, ws_a, ws_b, cdict4, "4f_")
            r.barrier(bar)
            r.replay()

    return nc



def g0(r, nc, ap, E, pb, C):
    ld = r.sem("g0ld")
    cast = r.sem("g0cast")
    pe = r.sem("g0pe")
    ve = r.sem("g0ve")
    se = r.sem("g0se")
    stv = r.sem("g0st")

    for name, buf in (("ones", C["ones"]), ("idb", C["idb"]), ("idf", C["idf"]),
                      ("tri0", C["tri0"]), ("tri2", C["tri2"]), ("p64", C["p64"]),
                      ("p32", C["p32"]), ("seld", C["seld"])):
        r.dma("sync", buf[:], ap[name], sem=ld)

    r.ew("vector", "memset", C["eps"][:, 0:1], 1e-5)
    r.ew("vector", "memset", C["eps"][:, 1:2], 1e-24, sem=ve)
    tb1 = E(nc.sbuf_tensor("g0_tb1", [16, 2], F32))
    tbr = E(nc.sbuf_tensor("g0_tbr", [1, 32], BF16))
    onesrow = E(nc.sbuf_tensor("g0_or", [1, 32], BF16))
    r.dma("sync", tb1[:, 0:1], ap["temp"], sem=ld)
    n_ld = ld.n
    r.wait("scalar", ld, n_ld)
    r.act("scalar", tb1[:, 1:2], tb1[:, 0:1], AF.Abs, scale=1.0, sem=se)
    r.ew("scalar", "mul", tb1[:, 1:2], tb1[:, 1:2], -1.0, sem=se)
    tbs = r.sem("g0_tbs")
    r.wait("gpsimd", se, se.n)
    r.dma("gpsimd", bass.AP(ap["t16"].tensor, 0, [[1, 16], [16, 2]]), tb1[:, 0:2], sem=tbs,
          allow_slow_non_contiguous=True)
    r.wait("gpsimd", tbs, tbs.n)
    r.dma("gpsimd", tbr[:], bass.AP(ap["t16"].tensor, 0, [[32, 1], [1, 32]]), sem=tbs)
    r.wait("vector", ld, n_ld)
    r.ew("vector", "memset", onesrow[:], 1.0, sem=ve)
    r.wait("tensor", ld, n_ld)
    r.wait("tensor", tbs, tbs.n)
    r.wait("tensor", ve, ve.n)
    r.mm(pb[0][:, 0:32], C["ones"][0:1, 0:128], tbr[:], True, True, sem=pe)
    r.wait("vector", pe, pe.n)
    r.ew("vector", "tensor_copy", C["tempbc"][:, 0:32], pb[0][:, 0:32], sem=ve)

    cwa = r.sem("cwa")
    for srcw, dst in (("wqa", "wqa_bf"), ("wkva", "wkva_bf"), ("woa", "woa_bf")):
        n = ap[srcw].tensor.shape[0]
        half = n // 2
        r.dma("gpsimd", ap[dst][0:half, :], ap[srcw][0:half, :], sem=cwa)
        r.dma("gpsimd", ap[dst][half:n, :], ap[srcw][half:n, :], sem=cwa)

    h64 = E(nc.sbuf_tensor("g0_h64", [64, 64], BF16))
    h128 = E(nc.sbuf_tensor("g0_h128", [128, 128], FP16))
    ga = E(nc.sbuf_tensor("g0_ga", [64, 4 * 1024], BF16))
    ea = E(nc.sbuf_tensor("g0_ea", [64, 4 * 1024], FP16))
    ld2 = r.sem("g0ld2")
    r.dma("sync", h64[:], ap["h64"], sem=ld2)
    r.dma("sync", h128[:], ap["h128"], sem=ld2)
    pa_mm = r.sem("g0_pamm")
    pa_ev = r.sem("g0_paev")
    gas = [r.sem("g0_ga%d" % i) for i in range(4)]
    sas = [r.sem("g0_sa%d" % i) for i in range(4)]
    r.wait("tensor", ld2, ld2.n)
    NT_A = 256
    for i in range(NT_A):
        d0 = i * 8
        slot = i % 4
        if i >= 4:
            r.wait("gpsimd", pa_ev, i - 3)
        r.dma("gpsimd",
              ga[:, slot * 1024:(slot + 1) * 1024],
              bass.AP(ap["w2"].tensor, d0 * 8192, [[128, 64], [8192, 8], [1, 128]]),
              sem=gas[slot])
        bk = 2 * slot
        r.wait("tensor", gas[slot], gas[slot].n)
        if i >= 4:
            r.wait("tensor", pa_ev, i - 3)
        r.mm(pb[bk][0:64, :], h64[:], ga[0:64, slot * 1024: slot * 1024 + 512],
             True, True)
        r.mm(pb[bk + 1][0:64, :], h64[:],
             ga[0:64, slot * 1024 + 512: slot * 1024 + 1024], True, True, sem=pa_mm)
        r.wait("vector", pa_mm, pa_mm.n)
        if i >= 4:
            r.wait("vector", sas[slot], sas[slot].n)
        r.ew("vector", "tensor_copy", ea[0:64, slot * 1024: slot * 1024 + 512],
             pb[bk][0:64, :])
        r.ew("vector", "tensor_copy", ea[0:64, slot * 1024 + 512: slot * 1024 + 1024],
             pb[bk + 1][0:64, :], sem=pa_ev)
        r.wait("sync", pa_ev, pa_ev.n)
        r.dma("sync",
              bass.AP(ap["s1h"].tensor, d0 * 128, [[2048 * 128, 64], [128, 8], [1, 128]]),
              ea[0:64, slot * 1024:(slot + 1) * 1024], sem=sas[slot])

    xb = E(nc.sbuf_tensor("g0_xb", [128, 4 * 512], FP16))
    eb = E(nc.sbuf_tensor("g0_eb", [128, 4 * 512], BF16))
    xbs = [r.sem("g0_xb%d" % i) for i in range(4)]
    pb_mm = r.sem("g0_pbmm")
    pb_ev = r.sem("g0_pbev")
    sbs = [r.sem("g0_sb%d" % i) for i in range(4)]
    ti = 0
    for dblk in range(4):
        for cp in range(64):
            slot = ti % 4
            if ti >= 4:
                r.wait("sync", pb_ev, ti - 3)
            if cp == 0:
                for k in range(4):
                    r.wait("sync", sas[k], 16 * 16 * (dblk + 1))
            srcap = bass.AP(ap["s1h"].tensor, cp * (2048 * 128) + dblk * 512 * 128,
                            [[128, 512], [1, 128]])
            r.dmat(xb[:, slot * 512:(slot + 1) * 512], srcap, sem=xbs[slot])
            bank = pb[slot]
            r.wait("tensor", xbs[slot], xbs[slot].n)
            if ti < 4:
                r.wait("tensor", pa_ev, NT_A)
            else:
                r.wait("tensor", pb_ev, ti - 3)
            for k in range(4):
                r.mm(bank[:, k * 128:(k + 1) * 128],
                     xb[:, slot * 512 + k * 128: slot * 512 + (k + 1) * 128],
                     h128[:], k == 0, k == 3, sem=(pb_mm if k == 3 else None))
            r.wait("vector", pb_mm, pb_mm.n)
            if ti >= 4:
                r.wait("vector", sbs[slot], sbs[slot].n)
            r.ew("vector", "tensor_scalar_mul", eb[:, slot * 512:(slot + 1) * 512],
                 bank[:], FWHT_SC, sem=pb_ev)
            r.wait("gpsimd", pb_ev, pb_ev.n)
            for k in range(4):
                d0 = dblk * 512 + k * 128
                r.dma("gpsimd",
                      bass.AP(ap["w2cb"].tensor, cp * 262144 + d0 * 128,
                              [[128, 128], [1, 128]]),
                      eb[:, slot * 512 + k * 128: slot * 512 + (k + 1) * 128],
                      sem=sbs[slot])
            ti += 1

    for k in range(4):
        r.wait("gpsimd", sbs[k], sbs[k].n)
        r.wait("sync", sas[k], sas[k].n)



def entry_block(r, nc, ap, pb, idf, xpan, xblk, b, sems):
    xin, tp, evd = sems
    for tc in range(TCH):
        g = b * TCH + tc
        slot = g % 2
        if g >= 2:
            r.wait("sync", tp, (g - 1) * NCD)
        r.dma("sync", xpan[:, slot * 2048:(slot + 1) * 2048],
              ap["x"][g * 128:(g + 1) * 128, :], sem=xin[slot])
        r.wait("tensor", xin[slot], xin[slot].n)
        for dc in range(NCD):
            bank = pb[dc % 2]
            j = g * NCD + dc
            if j >= 2:
                r.wait("tensor", evd, j - 1)
            r.tr(bank[:, 0:128],
                 xpan[:, slot * 2048 + dc * 128: slot * 2048 + (dc + 1) * 128],
                 idf[:], sem=tp)
            r.wait("vector", tp, tp.n)
            r.ew("vector", "tensor_copy",
                 xblk[:, dc * 512 + tc * 128: dc * 512 + tc * 128 + 128],
                 bank[:, 0:128], sem=evd)


def ln_block(r, bufs, banks, sems, b, ev):
    xblk, yblk = bufs["xblk"], bufs["yblk"]
    xb16, sq16 = bufs["xb16"], bufs["sq16"]
    rstd, negmr, lntmp = bufs["rstd"], bufs["negmr"], bufs["lntmp"]
    bm, bs = banks
    s_cast, s_sq, s_slot, s_lnmm, s_stat, s_rstd = (
        sems["ln_cast"], sems["ln_sq"], sems["ln_slot"], sems["ln_mm"],
        sems["ln_stat"], sems["ln_rstd"])
    ones = bufs["ones"]
    if ev.get("xblk_ready"):
        r.wait("scalar", *ev["xblk_ready"])
    for dc in range(NCD):
        slot = dc % 2
        if ev.get("ln_slot%d" % slot):
            r.wait("vector", *ev["ln_slot%d" % slot])
            r.wait("scalar", *ev["ln_slot%d" % slot])
        if dc == 0 and ev.get("yblk"):
            r.wait("vector", *ev["yblk"])
        r.ew("vector", "tensor_copy", xb16[:, slot * 512:(slot + 1) * 512],
             xblk[:, dc * 512:(dc + 1) * 512], sem=s_cast)
        r.act("scalar", sq16[:, slot * 512:(slot + 1) * 512],
              xblk[:, dc * 512:(dc + 1) * 512], AF.Square, sem=s_sq)
        r.wait("tensor", s_cast, s_cast.n)
        r.wait("tensor", s_sq, s_sq.n)
        if dc == 0 and ev.get("ln_banks"):
            r.wait("tensor", *ev["ln_banks"])
        r.mm(bm[:], ones[:], xb16[:, slot * 512:(slot + 1) * 512],
             dc == 0, dc == NCD - 1)
        r.mm(bs[:], ones[:], sq16[:, slot * 512:(slot + 1) * 512],
             dc == 0, dc == NCD - 1,
             sem=(s_lnmm if dc == NCD - 1 else s_slot))
        if dc != NCD - 1:
            ev["ln_slot%d" % slot] = (s_slot, s_slot.n)
    r.wait("vector", s_lnmm, s_lnmm.n)
    r.ew("vector", "tensor_scalar_mul", lntmp[:], bm[:], 1.0 / 2048.0)
    r.ew("vector", "tensor_mul", rstd[:], lntmp[:], lntmp[:])
    r.ew("vector", "scalar_tensor_tensor", negmr[:], bs[:], 1.0 / 2048.0,
         rstd[:], OP.mult, OP.subtract, sem=s_stat)
    ev["ln_banks"] = (s_stat, s_stat.n)
    r.wait("scalar", s_stat, s_stat.n)
    r.act("scalar", rstd[:], negmr[:], AF.Ln, bias=bufs["eps"][:, 0:1])
    r.act("scalar", rstd[:], rstd[:], AF.Exp, scale=-0.5, sem=s_rstd)
    r.wait("vector", s_rstd, s_rstd.n)
    r.ew("vector", "scalar_tensor_tensor", negmr[:], lntmp[:], -1.0, rstd[:],
         OP.mult, OP.mult, sem=s_stat)
    for dc in range(NCD):
        r.ew("vector", "tensor_mul", lntmp[:], xblk[:, dc * 512:(dc + 1) * 512], rstd[:])
        r.ew("vector", "tensor_add", yblk[:, dc * 512:(dc + 1) * 512], lntmp[:],
             negmr[:], sem=(sems["ln_y"] if dc == NCD - 1 else None))


SPANS_A = {-2: (0, 128), -1: (0, 256), 0: (0, 384), 1: (128, 384),
           2: (256, 256), 3: (384, 128)}

PH1 = {"kv": True, "q": True, "sc": True, "wo": True}
DBG_WO = False


def g1a(r, nc, ap, E, pb, C):
    xin = [r.sem("xld0"), r.sem("xld1")]
    tp = r.sem("tp")
    evd = r.sem("evd")
    xout = r.sem("xouts")
    sems = {k: r.sem(k) for k in
            ("ln_cast", "ln_sq", "ln_slot", "ln_mm", "ln_stat", "ln_rstd", "ln_y",
             "pj", "ev", "rp", "sq", "nm", "f", "qk",
             "sc", "ex", "mk", "dn", "pv", "at", "wo", "res")}
    s = sems

    xpan = E(nc.sbuf_tensor("1a_xpan", [128, 2 * 2048], F32))
    xblk = E(nc.sbuf_tensor("1a_xblk", [128, NCD * 512], F32))
    yblk = E(nc.sbuf_tensor("1a_yblk", [128, NCD * 512], BF16))
    xb16 = E(nc.sbuf_tensor("1a_xb16", [128, 2 * 512], BF16))
    sq16 = E(nc.sbuf_tensor("1a_sq16", [128, 2 * 512], BF16))
    rstd = E(nc.sbuf_tensor("1a_rstd", [128, 512], F32))
    negmr = E(nc.sbuf_tensor("1a_negmr", [128, 512], F32))
    lntmp = E(nc.sbuf_tensor("1a_lntmp", [128, 512], F32))
    kbuf = E(nc.sbuf_tensor("1a_kbuf", [128, HKV * 2048], BF16))
    vtm = E(nc.sbuf_tensor("1a_vtm", [128, 16 * 512], BF16))
    qbuf = E(nc.sbuf_tensor("1a_qbuf", [128, 16 * 512], BF16))
    ebuf = E(nc.sbuf_tensor("1a_ebuf", [128, 4 * 512], BF16))
    rt1 = E(nc.sbuf_tensor("1a_rt1", [128, 512], F32))
    rt2 = E(nc.sbuf_tensor("1a_rt2", [128, 512], F32))
    rt3 = E(nc.sbuf_tensor("1a_rt3", [128, 2 * 512], F32))
    sqf = E(nc.sbuf_tensor("1a_sqf", [128, 2 * 512], BF16))
    tln = E(nc.sbuf_tensor("1a_tln", [128, 512], F32))
    fbuf = E(nc.sbuf_tensor("1a_fbuf", [128, 2 * 512], F32))
    exps = E(nc.sbuf_tensor("1a_expS", [128, 2 * 6 * 512], BF16))
    zbuf = E(nc.sbuf_tensor("1a_zbuf", [128, 128], BF16))
    atto = E(nc.sbuf_tensor("1a_atto", [128, 16 * 512], BF16))
    rcp = E(nc.sbuf_tensor("1a_rcp", [128, 2 * 512], F32))
    dcp = E(nc.sbuf_tensor("1a_dcp", [128, 2 * 512], F32))
    csa = E(nc.sbuf_tensor("1a_csa", [128, 2048], BF16))
    sna = E(nc.sbuf_tensor("1a_sna", [128, 2048], BF16))
    ones, idf = C["ones"], C["idf"]
    p64, tempbc = C["p64"], C["tempbc"]
    tri0, tri2 = C["tri0"], C["tri2"]

    lnbufs = dict(xblk=xblk, yblk=yblk, xb16=xb16, sq16=sq16, rstd=rstd,
                  negmr=negmr, lntmp=lntmp, ones=ones, eps=C["eps"])
    ev = {}

    zrd = r.sem("zrd")
    r.ew("vector", "memset", zbuf[:], 0.0, sem=zrd)
    r.wait("tensor", zrd, zrd.n)
    st1 = Streamer(r, C["ws_a"], 4, 1024, "sync", "wsy")
    st3 = Streamer(r, C["ws_b"], 4, 1024, "sync", "wsc")
    stc = [0]
    def stream(srcap, row0, col0, nrows):
        stx = st1 if stc[0] % 2 == 0 else st3
        stc[0] += 1
        slot, sem2, lv = stx.load(srcap[row0:row0 + nrows, col0:col0 + 128], nrows)
        return stx, slot, sem2, lv
    cwa = r.sem("cwa")
    cw12 = r.sem("cw12")
    cwb = r.sem("cwb")
    for ic in range(16):
        for q in range(4):
            r.dma("gpsimd",
                  bass.AP(ap["w13cb"].tensor,
                          ic * 2097152 + q * 4 * 131072,
                          [[131072, 4], [128, 512], [1, 128]]),
                  ap["w1"][q * 2048:(q + 1) * 2048, ic * 128:(ic + 1) * 128],
                  sem=cw12)
            r.dma("gpsimd",
                  bass.AP(ap["w13cb"].tensor,
                          ic * 2097152 + q * 4 * 131072 + 512 * 128,
                          [[131072, 4], [128, 512], [1, 128]]),
                  ap["w3"][q * 2048:(q + 1) * 2048, ic * 128:(ic + 1) * 128],
                  sem=cw12)
    r.wait("sync", cwa, cwa.n)
    tbl = r.sem("tbl")
    r.dma("sync", csa[:], ap["cs_a"], sem=tbl)
    r.dma("sync", sna[:], ap["sn_a"], sem=tbl)
    r.wait("vector", tbl, tbl.n)

    def rope_a(cchunk, bank, b, ready_ev):
        par = cchunk % 2
        ecol = (cchunk % 4) * 512
        r.wait("vector", *ready_ev)
        if ev.get("eb%d" % (cchunk % 4)):
            r.wait("vector", *ev["eb%d" % (cchunk % 4)])
        r.ew("vector", "tensor_copy", ebuf[:, ecol:ecol + 512], bank[:], sem=s["ev"])
        r.wait("tensor", s["ev"], s["ev"].n)
        r.mm(bank[:], p64[:], ebuf[:, ecol:ecol + 512], True, True, sem=s["rp"])
        ev["eb%d" % (cchunk % 4)] = (s["rp"], s["rp"].n)
        r.wait("vector", s["rp"], s["rp"].n)
        r.ew("vector", "tensor_mul", rt1[:], ebuf[:, ecol:ecol + 512],
             csa[:, b * 512:(b + 1) * 512])
        r.ew("vector", "tensor_mul", rt2[:], bank[:], sna[:, b * 512:(b + 1) * 512])
        if ev.get("rt%d" % par):
            r.wait("vector", *ev["rt%d" % par])
        r.ew("vector", "tensor_add", rt3[:, par * 512:(par + 1) * 512],
             rt1[:], rt2[:], sem=s["rp"])
        r.wait("scalar", s["rp"], s["rp"].n)
        if ev.get("sq%d" % par):
            r.wait("scalar", *ev["sq%d" % par])
        r.act("scalar", sqf[:, par * 512:(par + 1) * 512],
              rt3[:, par * 512:(par + 1) * 512], AF.Square, sem=s["sq"])
        ev["rt%d" % par] = (s["sq"], s["sq"].n)
        r.wait("tensor", s["sq"], s["sq"].n)
        r.mm(bank[:], ones[:], sqf[:, par * 512:(par + 1) * 512], True, True,
             sem=s["nm"])
        ev["sq%d" % par] = (s["nm"], s["nm"].n)

    def rope_b(cchunk, bank, dst, dstcol, is_q, head):
        par = cchunk % 2
        r.wait("scalar", s["nm"], s["nm"].n - (1 if False else 0), )
        r.act("scalar", tln[:], bank[:], AF.Ln, bias=C["eps"][:, 1:2])
        if ev.get("fb%d" % par):
            r.wait("scalar", *ev["fb%d" % par])
        r.act("scalar", fbuf[:, par * 512:(par + 1) * 512], tln[:], AF.Exp,
              scale=-0.5, sem=s["f"])
        r.wait("vector", s["f"], s["f"].n)
        if is_q:
            r.ew("vector", "scalar_tensor_tensor", dst[:, dstcol:dstcol + 512],
                 rt3[:, par * 512:(par + 1) * 512], tempbc[:, head:head + 1],
                 fbuf[:, par * 512:(par + 1) * 512], OP.mult, OP.mult, sem=s["qk"])
        else:
            r.ew("vector", "tensor_mul", dst[:, dstcol:dstcol + 512],
                 rt3[:, par * 512:(par + 1) * 512],
                 fbuf[:, par * 512:(par + 1) * 512], sem=s["qk"])
        ev["fb%d" % par] = (s["qk"], s["qk"].n)

    for b in range(NBLK):
        if b > 0 and ev.get("xblk"):
            r.wait("vector", *ev["xblk"])
        entry_block(r, nc, ap, pb, idf, xpan, xblk, b, (xin, tp, evd))
        ev["xblk_ready"] = (evd, evd.n)
        ln_block(r, lnbufs, (pb[2][:], pb[3][:]), sems, b, ev)
        ev["yblk"] = None
        if not PH1["kv"]:
            continue
        r.wait("tensor", s["ln_y"], s["ln_y"].n)
        for i in range(NCD):
            stx, slot, wsem, lv = stream(ap["wkva_bf"], 0, i * 128, 1024)
            r.wait("tensor", wsem, lv)
            if i == 0 and ev.get("pj_banks"):
                r.wait("tensor", *ev["pj_banks"])
            for ts in range(4):
                r.mm(pb[ts][:], yblk[:, i * 512 + ts * 128: i * 512 + ts * 128 + 128],
                     stx.view(slot, 512, 1024),
                     i == 0, i == NCD - 1)
            for kc in range(4):
                last = (kc == 3)
                r.mm(pb[4 + kc][:], stx.view(slot, kc * 128, (kc + 1) * 128),
                     yblk[:, i * 512: (i + 1) * 512],
                     i == 0, i == NCD - 1, sem=(s["pj"] if last else None))
            stx.set_free(slot, s["pj"])
        r.wait("scalar", s["pj"], s["pj"].n)
        for ts in range(4):
            g = b * 4 + ts
            r.act("scalar", vtm[:, g * 512:(g + 1) * 512], pb[ts][:], AF.Copy,
                  sem=s["ev"])
        pj_ev = (s["pj"], s["pj"].n)
        stages = []
        for kc in range(4):
            stages.append(("A", kc))
        rope_a(0, pb[4], b, pj_ev)
        rope_a(1, pb[5], b, pj_ev)
        for kc in range(2, 4):
            rope_b(kc - 2, pb[4 + kc - 2], kbuf, (kc - 2) * 2048 + b * 512, False, 0)
            rope_a(kc, pb[4 + kc], b, pj_ev)
        rope_b(2, pb[6], kbuf, 2 * 2048 + b * 512, False, 0)
        rope_b(3, pb[7], kbuf, 3 * 2048 + b * 512, False, 0)
        ev["pj_banks"] = (s["qk"], s["qk"].n)
        if not PH1["q"]:
            continue
        for half in range(2):
            for i in range(NCD):
                stx, slot, wsem, lv = stream(ap["wqa_bf"], half * 1024, i * 128, 1024)
                r.wait("tensor", wsem, lv)
                if i == 0:
                    r.wait("tensor", *ev["pj_banks"])
                    if half == 0:
                        r.wait("tensor", s["ev"], s["ev"].n)
                for qc8 in range(8):
                    r.mm(pb[qc8][:], stx.view(slot, qc8 * 128, (qc8 + 1) * 128),
                         yblk[:, i * 512:(i + 1) * 512],
                         i == 0, i == NCD - 1,
                         sem=(s["pj"] if qc8 == 7 else None))
                stx.set_free(slot, s["pj"])
            pj_ev = (s["pj"], s["pj"].n)
            rope_a(0, pb[0], b, pj_ev)
            rope_a(1, pb[1], b, pj_ev)
            for qc8 in range(2, 8):
                hprev = half * 8 + qc8 - 2
                rope_b(qc8 - 2, pb[qc8 - 2], qbuf, hprev * 512, True, hprev)
                rope_a(qc8, pb[qc8], b, pj_ev)
            for qc8 in (6, 7):
                h = half * 8 + qc8
                rope_b(qc8, pb[qc8], qbuf, h * 512, True, h)
            ev["pj_banks"] = (s["qk"], s["qk"].n)
        ev["yblk"] = (s["pj"], s["pj"].n)
        kcs = list(range(max(0, 4 * b - 2), 4 * b + 4))
        if not PH1["sc"]:
            continue
        r.wait("tensor", s["qk"], s["qk"].n)
        nl = len(kcs)

        def sc_part(h):
            kvh = h // 4
            par = h % 2
            bS = [pb[0 + 4 * par], pb[1 + 4 * par]]
            ebase = par * 6 * 512
            if ev.get("sc_banks%d" % par):
                r.wait("tensor", *ev["sc_banks%d" % par])
            if ev.get("exps%d" % par):
                r.wait("scalar", *ev["exps%d" % par])
                r.wait("vector", *ev["exps%d" % par])
            for i, kc in enumerate(kcs):
                off, span = SPANS_A[kc - 4 * b]
                bank = bS[i % 2]
                if i >= 2:
                    r.wait("tensor", s["ex"], s["ex"].n - 1)
                r.mm(bank[:, 0:span],
                     kbuf[:, kvh * 2048 + kc * 128: kvh * 2048 + (kc + 1) * 128],
                     qbuf[:, h * 512 + off: h * 512 + off + span],
                     True, True, sem=s["sc"])
                r.wait("scalar", s["sc"], s["sc"].n)
                r.act("scalar", exps[:, ebase + i * 512 + off: ebase + i * 512 + off + span],
                      bank[:, 0:span], AF.Exp,
                      bias=tempbc[:, 16 + h:17 + h], sem=s["ex"])
            r.wait("vector", s["ex"], s["ex"].n)
            mops = []
            for i, kc in enumerate(kcs):
                d = kc - 4 * b
                if 0 <= d <= 3:
                    mops.append((i, d, tri0))
                if 0 <= d + 2 <= 3:
                    mops.append((i, d + 2, tri2))
            for mi, (i, j, tri) in enumerate(mops):
                r.ew("vector", "tensor_mul",
                     exps[:, ebase + i * 512 + j * 128: ebase + i * 512 + (j + 1) * 128],
                     exps[:, ebase + i * 512 + j * 128: ebase + i * 512 + (j + 1) * 128],
                     tri[:], sem=(s["mk"] if mi == len(mops) - 1 else None))
            ev["mk%d" % par] = (s["mk"], s["mk"].n)

        def dnpv_part(h):
            kvh = h // 4
            par = h % 2
            bD = pb[2 + 4 * par]
            bO = pb[3 + 4 * par]
            ebase = par * 6 * 512
            r.wait("tensor", *ev["mk%d" % par])
            r.mm(bD[:, 0:512], zbuf[:], yblk[:, 0:512], True, False)
            for i, kc in enumerate(kcs):
                off, span = SPANS_A[kc - 4 * b]
                r.mm(bD[:, off:off + span], ones[:],
                     exps[:, ebase + i * 512 + off: ebase + i * 512 + off + span],
                     False, i == nl - 1, sem=(s["dn"] if i == nl - 1 else None))
            r.mm(bO[:, 0:512], zbuf[:], yblk[:, 0:512], True, False)
            for i, kc in enumerate(kcs):
                off, span = SPANS_A[kc - 4 * b]
                r.mm(bO[:, off:off + span],
                     vtm[:, kc * 512 + kvh * 128: kc * 512 + (kvh + 1) * 128],
                     exps[:, ebase + i * 512 + off: ebase + i * 512 + off + span],
                     False, i == nl - 1, sem=(s["pv"] if i == nl - 1 else None))
            ev["exps%d" % par] = (s["pv"], s["pv"].n)
            r.wait("vector", s["dn"], s["dn"].n)
            r.ew("vector", "tensor_copy", dcp[:, par * 512:(par + 1) * 512], bD[:])
            r.ew("vector", "reciprocal", rcp[:, par * 512:(par + 1) * 512],
                 dcp[:, par * 512:(par + 1) * 512])
            r.wait("vector", s["pv"], s["pv"].n)
            r.ew("vector", "tensor_mul", atto[:, h * 512:(h + 1) * 512], bO[:],
                 rcp[:, par * 512:(par + 1) * 512], sem=s["at"])
            ev["sc_banks%d" % par] = (s["at"], s["at"].n)

        sc_part(0)
        for h in range(1, 16):
            sc_part(h)
            dnpv_part(h - 1)
        dnpv_part(15)
        if not PH1["wo"]:
            continue
        r.wait("tensor", s["at"], s["at"].n)
        for half in range(2):
            for vc in range(NCD):
                stx, slot, wsem, lv = stream(ap["woa_bf"], half * 1024, vc * 128, 1024)
                r.wait("tensor", wsem, lv)
                if vc == 0:
                    r.wait("tensor", s["res"], s["res"].n)
                for oc8 in range(8):
                    r.mm(pb[oc8][:], stx.view(slot, oc8 * 128, (oc8 + 1) * 128),
                         atto[:, vc * 512:(vc + 1) * 512],
                         vc == 0, vc == NCD - 1,
                         sem=(s["wo"] if oc8 == 7 else None))
                stx.set_free(slot, s["wo"])
            r.wait("vector", s["wo"], s["wo"].n)
            for oc8 in range(8):
                oc = half * 8 + oc8
                r.ew("vector", "tensor_add", xblk[:, oc * 512:(oc + 1) * 512],
                     xblk[:, oc * 512:(oc + 1) * 512], pb[oc8][:],
                     sem=(s["res"] if oc8 == 7 else None))
            ev["wo_banks"] = (s["res"], s["res"].n)
        ev["xblk"] = None
        r.wait("sync", s["res"], s["res"].n)
        for dc in range(NCD):
            r.dma("sync", ap["xscr"][dc * 128:(dc + 1) * 128, b * 512:(b + 1) * 512],
                  xblk[:, dc * 512:(dc + 1) * 512], sem=xout)
        ev["xblk"] = (xout, xout.n)
        if b < NBLK - 1:
            r.wait("tensor", s["res"], s["res"].n)
    r.wait("sync", xout, xout.n)


def ffn_group(r, nc, ap, E, pb, ws_a, ws_b, C, tag):
    s = {k: r.sem(k) for k in
         ("ln_cast", "ln_sq", "ln_slot", "ln_mm", "ln_stat", "ln_rstd", "ln_y",
          "pj1", "pj3", "hp", "sl", "gt", "w2", "res", "xtp", "xev")}
    xld = [r.sem("xld0"), r.sem("xld1")]
    xout = r.sem("xout")
    exit_mode = (tag == "4f_")
    if tag == "2f_":
        cwb = r.sem("cwb")
        for srcw, dst in (("wqb", "wqb_bf"), ("wkb", "wkb_bf"),
                          ("wvb", "wvb_bf"), ("wob", "wob_bf")):
            n = ap[srcw].tensor.shape[0]
            half = n // 2
            r.dma("gpsimd", ap[dst][0:half, :], ap[srcw][0:half, :], sem=cwb)
            r.dma("gpsimd", ap[dst][half:n, :], ap[srcw][half:n, :], sem=cwb)

    xblk = E(nc.sbuf_tensor(tag + "xblk", [128, NCD * 512], F32))
    yblk = E(nc.sbuf_tensor(tag + "yblk", [128, NCD * 512], BF16))
    xb16 = E(nc.sbuf_tensor(tag + "xb16", [128, 2 * 512], BF16))
    sq16 = E(nc.sbuf_tensor(tag + "sq16", [128, 2 * 512], BF16))
    rstd = E(nc.sbuf_tensor(tag + "rstd", [128, 512], F32))
    negmr = E(nc.sbuf_tensor(tag + "negmr", [128, 512], F32))
    lntmp = E(nc.sbuf_tensor(tag + "lntmp", [128, 512], F32))
    hbuf = E(nc.sbuf_tensor(tag + "hbuf", [128, NHC * 512], BF16))
    sgb = E(nc.sbuf_tensor(tag + "sgb", [128, 4 * 512], BF16))
    opan = E(nc.sbuf_tensor(tag + "opan", [128, 2048], F32)) if exit_mode else None

    lnbufs = dict(xblk=xblk, yblk=yblk, xb16=xb16, sq16=sq16, rstd=rstd,
                  negmr=negmr, lntmp=lntmp, ones=C["ones"], eps=C["eps"])
    ev = {}
    st1 = Streamer(r, ws_a, 4, 1024, "sync", "wsy")
    st3 = Streamer(r, ws_b, 4, 1024, "sync", "wsc")

    cw12 = r.sem("cw12")
    r.wait("sync", cw12, cw12.n)
    r.wait("scalar", cw12, cw12.n)

    for b in range(NBLK):
        slot_b = b % 2
        if ev.get("xblk"):
            r.wait("sync", *ev["xblk"])
        for dc in range(NCD):
            r.dma("sync", xblk[:, dc * 512:(dc + 1) * 512],
                  ap["xscr"][dc * 128:(dc + 1) * 128, b * 512:(b + 1) * 512],
                  sem=xld[slot_b])
        ev["xblk_ready"] = (xld[slot_b], xld[slot_b].n)
        r.wait("vector", xld[slot_b], xld[slot_b].n)
        ln_block(r, lnbufs, (pb[2][:], pb[3][:]), s, b, ev)
        r.wait("tensor", s["ln_y"], s["ln_y"].n)
        for hcg in range(16):
            for i in range(NCD):
                stx = st1 if i % 2 == 0 else st3
                slw, wws, lvw = stx.load(
                    bass.AP(ap["w13cb"].tensor, i * 2097152 + hcg * 131072,
                            [[128, 1024], [1, 128]]), 1024)
                r.wait("tensor", wws, lvw)
                if i == 0:
                    r.wait("tensor", s["gt"], s["gt"].n)
                for k in range(4):
                    r.mm(pb[k][:], stx.view(slw, k * 128, (k + 1) * 128),
                         yblk[:, i * 512:(i + 1) * 512],
                         i == 0, i == NCD - 1,
                         sem=(s["pj1"] if k == 3 else None))
                for k in range(4):
                    r.mm(pb[4 + k][:], stx.view(slw, 512 + k * 128, 512 + (k + 1) * 128),
                         yblk[:, i * 512:(i + 1) * 512],
                         i == 0, i == NCD - 1,
                         sem=(s["pj3"] if k == 3 else None))
                stx.set_free(slw, s["pj3"])
            r.wait("scalar", s["pj3"], s["pj3"].n)
            for k in range(4):
                r.act("scalar", sgb[:, k * 512:(k + 1) * 512], pb[k][:], AF.Silu,
                      sem=s["sl"])
            r.wait("vector", s["sl"], s["sl"].n)
            for k in range(4):
                hc = hcg * 4 + k
                r.ew("vector", "tensor_mul", hbuf[:, hc * 512:(hc + 1) * 512],
                     sgb[:, k * 512:(k + 1) * 512], pb[4 + k][:], sem=s["gt"])
        r.wait("tensor", s["gt"], s["gt"].n)
        for ohalf in range(2):
            if ohalf == 1:
                r.wait("tensor", s["res"], s["res"].n)
            for hc in range(NHC):
                stx = st1 if hc % 2 == 0 else st3
                sl2, w2s, lv2 = stx.load(
                    bass.AP(ap["w2cb"].tensor, hc * 262144 + ohalf * 1024 * 128,
                            [[128, 1024], [1, 128]]), 1024)
                r.wait("tensor", w2s, lv2)
                for oc8 in range(8):
                    r.mm(pb[oc8][:], stx.view(sl2, oc8 * 128, (oc8 + 1) * 128),
                         hbuf[:, hc * 512:(hc + 1) * 512],
                         hc == 0, hc == NHC - 1,
                         sem=(s["w2"] if oc8 == 7 else None))
                stx.set_free(sl2, s["w2"])
            r.wait("vector", s["w2"], s["w2"].n)
            for oc8 in range(8):
                oc = ohalf * 8 + oc8
                r.ew("vector", "tensor_add", xblk[:, oc * 512:(oc + 1) * 512],
                     xblk[:, oc * 512:(oc + 1) * 512], pb[oc8][:],
                     sem=(s["res"] if oc8 == 7 else None))
        if not exit_mode:
            r.wait("gpsimd", s["res"], s["res"].n)
            for dc in range(NCD):
                r.dma("gpsimd",
                      ap["xscr"][dc * 128:(dc + 1) * 128, b * 512:(b + 1) * 512],
                      xblk[:, dc * 512:(dc + 1) * 512], sem=xout)
            ev["xblk"] = (xout, xout.n)
        else:
            r.wait("tensor", s["res"], s["res"].n)
            for ts in range(4):
                g = b * 4 + ts
                for dc in range(NCD):
                    bank = pb[dc % 2]
                    j = b * 64 + ts * NCD + dc
                    if j >= 2:
                        r.wait("tensor", s["xev"], s["xev"].n - 1)
                    r.tr(bank[:, 0:128],
                         xblk[:, dc * 512 + ts * 128: dc * 512 + ts * 128 + 128],
                         C["idf"][:], sem=s["xtp"])
                    r.wait("vector", s["xtp"], s["xtp"].n)
                    if ts > 0 or b > 0:
                        if dc == 0:
                            r.wait("vector", xout, xout.n)
                    r.ew("vector", "tensor_copy",
                         opan[:, dc * 128:(dc + 1) * 128], bank[:, 0:128],
                         sem=s["xev"])
                r.wait("gpsimd", s["xev"], s["xev"].n)
                r.dma("gpsimd", ap["out"][g * 128:(g + 1) * 128, :], opan[:],
                      sem=xout)
            ev["xblk"] = (s["xtp"], s["xtp"].n)
        if b < NBLK - 1:
            r.wait("tensor", s["res"], s["res"].n)
    r.wait("gpsimd", xout, xout.n)
    r.wait("sync", xout, xout.n)
    return xout


def g1c(r, nc, ap, E, pb, ws_a, ws_b, C):
    s = {k: r.sem(k) for k in
         ("ln_cast", "ln_sq", "ln_slot", "ln_mm", "ln_stat", "ln_rstd", "ln_y",
          "pj", "ev", "rp", "st")}
    xld = [r.sem("xld0"), r.sem("xld1")]
    sto = [r.sem("sto0"), r.sem("sto1")]

    xblk = E(nc.sbuf_tensor("1c_xblk", [128, NCD * 512], F32))
    yblk = E(nc.sbuf_tensor("1c_yblk", [128, NCD * 512], BF16))
    xb16 = E(nc.sbuf_tensor("1c_xb16", [128, 2 * 512], BF16))
    sq16 = E(nc.sbuf_tensor("1c_sq16", [128, 2 * 512], BF16))
    rstd = E(nc.sbuf_tensor("1c_rstd", [128, 512], F32))
    negmr = E(nc.sbuf_tensor("1c_negmr", [128, 512], F32))
    lntmp = E(nc.sbuf_tensor("1c_lntmp", [128, 512], F32))
    stg = E(nc.sbuf_tensor("1c_stg", [128, 2 * 8 * 512], BF16))
    ebuf = E(nc.sbuf_tensor("1c_ebuf", [128, 8 * 512], BF16))
    rt1 = E(nc.sbuf_tensor("1c_rt1", [128, 512], F32))
    rt2 = E(nc.sbuf_tensor("1c_rt2", [128, 512], F32))
    csb = E(nc.sbuf_tensor("1c_csb", [128, 2048], F32))
    snb = E(nc.sbuf_tensor("1c_snb", [128, 2048], F32))
    cskb = E(nc.sbuf_tensor("1c_cskb", [128, 2048], F32))
    snkb = E(nc.sbuf_tensor("1c_snkb", [128, 2048], F32))

    tbl = r.sem("tbl")
    for name, buf in (("cs_b", csb), ("sn_b", snb), ("csk_b", cskb), ("snk_b", snkb)):
        r.dma("sync", buf[:], ap[name], sem=tbl)
    r.wait("vector", tbl, tbl.n)
    cwb = r.sem("cwb")
    r.wait("sync", cwb, cwb.n)
    r.wait("scalar", cwb, cwb.n)

    lnbufs = dict(xblk=xblk, yblk=yblk, xb16=xb16, sq16=sq16, rstd=rstd,
                  negmr=negmr, lntmp=lntmp, ones=C["ones"], eps=C["eps"])
    ev = {}
    st1 = Streamer(r, ws_a, 4, 1024, "sync", "wsy")
    st3 = Streamer(r, ws_b, 4, 1024, "sync", "wsc")

    groups = [
        (ap["wqb_bf"], 0, "qn", 0), (ap["wqb_bf"], 1024, "qn", 8),
        (ap["wqb_bf"], 2048, "qr", 0),
        (ap["wkb_bf"], 0, "kn", 0), (ap["wkb_bf"], 1024, "kn", 8),
        (ap["wkb_bf"], 2048, "kr", 0),
        (ap["wvb_bf"], 0, "v", 0), (ap["wvb_bf"], 1024, "v", 1),
    ]

    for b in range(NBLK):
        slot_b = b % 2
        if ev.get("xblk"):
            r.wait("sync", *ev["xblk"])
        for dc in range(NCD):
            r.dma("sync", xblk[:, dc * 512:(dc + 1) * 512],
                  ap["xscr"][dc * 128:(dc + 1) * 128, b * 512:(b + 1) * 512],
                  sem=xld[slot_b])
        ev["xblk_ready"] = (xld[slot_b], xld[slot_b].n)
        r.wait("vector", xld[slot_b], xld[slot_b].n)
        ln_block(r, lnbufs, (pb[2][:], pb[3][:]), s, b, ev)
        r.wait("tensor", s["ln_y"], s["ln_y"].n)
        for gi, (wsrc, row0, mode, arg) in enumerate(groups):
            gslot = gi % 2
            stgo = gslot * 8 * 512
            stx = st1 if gi % 2 == 0 else st3
            if mode == "v":
                for i in range(NCD):
                    slw, wsem, lv = stx.load(
                        wsrc[row0:row0 + 1024, i * 128:(i + 1) * 128], 1024)
                    r.wait("tensor", wsem, lv)
                    if i == 0 and ev.get("pj_banks"):
                        r.wait("tensor", *ev["pj_banks"])
                    for tsub in range(4):
                        for vfb2 in range(2):
                            r.mm(pb[tsub * 2 + vfb2][:],
                                 yblk[:, i * 512 + tsub * 128: i * 512 + tsub * 128 + 128],
                                 stx.view(slw, vfb2 * 512, (vfb2 + 1) * 512),
                                 i == 0, i == NCD - 1,
                                 sem=(s["pj"] if (tsub, vfb2) == (3, 1) else None))
                    stx.set_free(slw, s["pj"])
            else:
                for i in range(NCD):
                    slw, wsem, lv = stx.load(
                        wsrc[row0:row0 + 1024, i * 128:(i + 1) * 128], 1024)
                    r.wait("tensor", wsem, lv)
                    if i == 0 and ev.get("pj_banks"):
                        r.wait("tensor", *ev["pj_banks"])
                    for c8 in range(8):
                        r.mm(pb[c8][:], stx.view(slw, c8 * 128, (c8 + 1) * 128),
                             yblk[:, i * 512:(i + 1) * 512],
                             i == 0, i == NCD - 1,
                             sem=(s["pj"] if c8 == 7 else None))
                    stx.set_free(slw, s["pj"])
            pj_ev = (s["pj"], s["pj"].n)
            if ev.get("stg%d" % gslot):
                r.wait("scalar", *ev["stg%d" % gslot])
                r.wait("vector", *ev["stg%d" % gslot])
            if mode in ("qn", "kn", "v"):
                r.wait("scalar", s["pj"], s["pj"].n)
                sc = RSQ_DN if mode == "kn" else 1.0
                for c8 in range(8):
                    r.act("scalar", stg[:, stgo + c8 * 512: stgo + (c8 + 1) * 512],
                          pb[c8][:], AF.Copy, scale=sc, sem=s["ev"])
                r.wait("gpsimd", s["ev"], s["ev"].n)
            else:
                for c8 in range(8):
                    r.wait("vector", *pj_ev)
                    if ev.get("eb%d" % c8):
                        r.wait("vector", *ev["eb%d" % c8])
                    r.ew("vector", "tensor_copy", ebuf[:, c8 * 512:(c8 + 1) * 512],
                         pb[c8][:], sem=s["ev"])
                    r.wait("tensor", s["ev"], s["ev"].n)
                    r.mm(pb[c8][:], C["p32"][:], ebuf[:, c8 * 512:(c8 + 1) * 512],
                         True, True, sem=s["rp"])
                    ev["eb%d" % c8] = (s["rp"], s["rp"].n)
                    ct = csb if mode == "qr" else cskb
                    st_ = snb if mode == "qr" else snkb
                    r.wait("vector", s["rp"], s["rp"].n)
                    r.ew("vector", "tensor_mul", rt1[:],
                         ebuf[:, c8 * 512:(c8 + 1) * 512],
                         ct[:, b * 512:(b + 1) * 512])
                    r.ew("vector", "tensor_mul", rt2[:], pb[c8][:],
                         st_[:, b * 512:(b + 1) * 512])
                    r.ew("vector", "tensor_add",
                         stg[:, stgo + c8 * 512: stgo + (c8 + 1) * 512],
                         rt1[:], rt2[:], sem=s["ev"])
                r.wait("gpsimd", s["ev"], s["ev"].n)
            ev["pj_banks"] = (s["ev"], s["ev"].n)
            for c8 in range(8):
                srcap = stg[:, stgo + c8 * 512: stgo + (c8 + 1) * 512]
                if mode == "qn":
                    ch = arg + c8
                    dst = bass.AP(ap["qscr"].tensor, ch * 128 * S + b * 512,
                                  [[S, 128], [1, 512]])
                elif mode == "qr":
                    ch = 16 + c8
                    dst = bass.AP(ap["qscr"].tensor, ch * 128 * S + b * 512,
                                  [[S, 128], [1, 512]])
                elif mode in ("kn", "kr"):
                    colb = (2048 + c8 * 128) if mode == "kr" else (arg + c8) * 128
                    dst = bass.AP(ap["kscr"].tensor, b * 4 * 128 * 3072 + colb,
                                  [[3072, 128], [128 * 3072, 4], [1, 128]])
                else:
                    tsub = c8 // 2; vfb2 = c8 % 2
                    kc = b * 4 + tsub
                    colb = arg * 1024 + vfb2 * 512
                    dst = bass.AP(ap["vscr"].tensor, kc * 128 * 2048 + colb,
                                  [[2048, 128], [1, 512]])
                r.dma("gpsimd", dst, srcap, sem=sto[gslot])
            ev["stg%d" % gslot] = (sto[gslot], sto[gslot].n)
        ev["xblk"] = None
        ev["yblk"] = (s["pj"], s["pj"].n)
    r.wait("gpsimd", sto[0], sto[0].n)
    r.wait("gpsimd", sto[1], sto[1].n)
    r.wait("sync", sto[0], sto[0].n)
    r.wait("sync", sto[1], sto[1].n)


def g2a(r, nc, ap, E, pb, ws_a, c_seld, c_ones):
    s = {k: r.sem(k) for k in
         ("ql", "sc", "ex", "pvc", "oc", "dn", "dloc", "rc", "exf", "at",
          "wo", "res", "zp")}
    xld = [r.sem("xld0"), r.sem("xld1")]
    kld = [r.sem("kld0"), r.sem("kld1")]
    wl = [r.sem("wl0"), r.sem("wl1")]
    xout = r.sem("xout")

    qbn = E(nc.sbuf_tensor("2a_qbn", [128, 16 * 512], BF16))
    qrp = E(nc.sbuf_tensor("2a_qrp", [128, 16 * 512], BF16))
    kbb = E(nc.sbuf_tensor("2a_kbb", [128, 8 * 3072], BF16))
    vbb = E(nc.sbuf_tensor("2a_vbb", [128, 8 * 2048], BF16))
    expt = E(nc.sbuf_tensor("2a_expt", [128, 2 * 512], BF16))
    obuf = E(nc.sbuf_tensor("2a_obuf", [128, 16 * 512], BF16))
    dloc = E(nc.sbuf_tensor("2a_dloc", [16, 512], F32))
    drec = E(nc.sbuf_tensor("2a_drec", [16, 512], F32))
    xblk = E(nc.sbuf_tensor("2a_xblk", [128, NCD * 512], F32))
    expd = E(nc.sbuf_tensor("2a_expd", [16, 2048], F32))
    dbgs = E(nc.sbuf_tensor("2a_dbgs", [128, 512], F32)) if DBG_WO else None
    dbsem = r.sem("dbsem") if DBG_WO else None
    dbsd = r.sem("dbsd") if DBG_WO else None

    tbl = r.sem("tbl")
    r.dma("sync", expd[:], ap["expd"], sem=tbl)
    r.wait("tensor", tbl, tbl.n)
    cwb = r.sem("cwb")
    r.wait("sync", cwb, cwb.n)
    for h in range(16):
        if h % 2 == 0:
            r.ew("vector", "memset", qrp[64:128, h * 512:(h + 1) * 512], 0.0,
                 sem=(s["zp"] if h == 15 else None))
        else:
            r.ew("vector", "memset", qrp[0:64, h * 512:(h + 1) * 512], 0.0,
                 sem=(s["zp"] if h == 15 else None))

    ev = {}
    wtile = [0]

    def stream(row0, col0):
        w = wtile[0]; wtile[0] += 1
        slot = w % 2
        if ev.get("ws%d" % slot):
            r.wait("sync", *ev["ws%d" % slot])
        r.dmat(ws_a[:, slot * 2048: slot * 2048 + 1024],
               ap["wob_bf"][row0:row0 + 1024, col0:col0 + 128], sem=wl[slot])
        return slot, wl[slot], wl[slot].n

    for qb in range(NBLK):
        qslot = qb % 2
        if ev.get("qbuf"):
            r.wait("sync", *ev["qbuf"])
        for ch in range(16):
            r.dma("sync", qbn[:, ch * 512:(ch + 1) * 512],
                  bass.AP(ap["qscr"].tensor, ch * 128 * S + qb * 512,
                          [[S, 128], [1, 512]]), sem=s["ql"])
        for c in range(8):
            r.dma("sync", qrp[0:64, (2 * c) * 512:(2 * c + 1) * 512],
                  bass.AP(ap["qscr"].tensor, (16 + c) * 128 * S + qb * 512,
                          [[S, 64], [1, 512]]), sem=s["ql"])
            r.dma("sync", qrp[64:128, (2 * c + 1) * 512:(2 * c + 2) * 512],
                  bass.AP(ap["qscr"].tensor, (16 + c) * 128 * S + 64 * S + qb * 512,
                          [[S, 64], [1, 512]]), sem=s["ql"])
        if ev.get("xblk"):
            r.wait("sync", *ev["xblk"])
        for dc in range(NCD):
            r.dma("sync", xblk[:, dc * 512:(dc + 1) * 512],
                  ap["xscr"][dc * 128:(dc + 1) * 128, qb * 512:(qb + 1) * 512],
                  sem=xld[qslot])
        ql_ev = (s["ql"], s["ql"].n)
        for kh in range(2):
            kslot = kh
            if ev.get("kvb"):
                r.wait("sync", *ev["kvb"])
            for kc8 in range(8):
                kc = kh * 8 + kc8
                r.dma("sync", kbb[:, kc8 * 3072:(kc8 + 1) * 3072],
                      bass.AP(ap["kscr"].tensor, kc * 128 * 3072,
                              [[3072, 128], [1, 3072]]), sem=kld[kslot])
                r.dma("sync", vbb[:, kc8 * 2048:(kc8 + 1) * 2048],
                      bass.AP(ap["vscr"].tensor, kc * 128 * 2048,
                              [[2048, 128], [1, 2048]]), sem=kld[kslot])
            r.wait("tensor", kld[kslot], kld[kslot].n)
            r.wait("tensor", *ql_ev)
            if kh == 0:
                r.wait("tensor", s["zp"], s["zp"].n)
            r.wait("tensor", s["dloc"], s["dloc"].n)
            bD = pb[4 + kh]
            def sc_emit(ii):
                h, kc8 = divmod(ii, 8)
                bank = pb[ii % 2]
                if ev.get("sb%d" % (ii % 2)):
                    r.wait("tensor", *ev["sb%d" % (ii % 2)])
                r.mm(bank[:], kbb[:, kc8 * 3072 + h * 128: kc8 * 3072 + (h + 1) * 128],
                     qbn[:, h * 512:(h + 1) * 512], True, False)
                r.mm(bank[:],
                     kbb[:, kc8 * 3072 + 2048 + (h // 2) * 128: kc8 * 3072 + 2048 + (h // 2 + 1) * 128],
                     qrp[:, h * 512:(h + 1) * 512], False, True, sem=s["sc"])
                r.wait("scalar", s["sc"], s["sc"].n)
                if ev.get("et%d" % (ii % 2)):
                    r.wait("scalar", *ev["et%d" % (ii % 2)])
                r.act("scalar", expt[:, (ii % 2) * 512: (ii % 2 + 1) * 512],
                      bank[:], AF.Exp, sem=s["ex"])
                ev["sb%d" % (ii % 2)] = (s["ex"], s["ex"].n)
                ev["exv%d" % (ii % 2)] = (s["ex"], s["ex"].n)

            def dnpv_emit(ii):
                h, kc8 = divmod(ii, 8)
                bO = pb[2 + (h % 2)]
                if kc8 == 0 and s["oc"].n >= 1:
                    r.wait("tensor", s["oc"], s["oc"].n - 1)
                r.wait("tensor", *ev["exv%d" % (ii % 2)])
                r.mm(bD[0:16, :], c_seld[:, h * 16:(h + 1) * 16],
                     expt[:, (ii % 2) * 512: (ii % 2 + 1) * 512],
                     ii == 0, ii == 127, sem=(s["dn"] if ii == 127 else None))
                r.mm(bO[:], vbb[:, kc8 * 2048 + h * 128: kc8 * 2048 + (h + 1) * 128],
                     expt[:, (ii % 2) * 512: (ii % 2 + 1) * 512],
                     kc8 == 0, kc8 == 7, sem=s["pvc"])
                ev["et%d" % (ii % 2)] = (s["pvc"], s["pvc"].n)
                if kc8 == 7:
                    r.wait("vector", s["pvc"], s["pvc"].n)
                    if kh == 0:
                        r.ew("vector", "tensor_copy", obuf[:, h * 512:(h + 1) * 512],
                             bO[:], sem=s["oc"])
                    else:
                        r.ew("vector", "tensor_add", obuf[:, h * 512:(h + 1) * 512],
                             obuf[:, h * 512:(h + 1) * 512], bO[:], sem=s["oc"])

            for ii in range(128):
                sc_emit(ii)
                if ii > 0:
                    dnpv_emit(ii - 1)
            dnpv_emit(127)
            r.wait("vector", s["dn"], s["dn"].n)
            if kh == 0:
                r.ew("vector", "tensor_copy", dloc[:], bD[0:16, :], sem=s["dloc"])
            else:
                r.ew("vector", "tensor_add", dloc[:], dloc[:], bD[0:16, :],
                     sem=s["dloc"])
            ev["kvb"] = (s["oc"], s["oc"].n)
        r.wait("vector", s["dloc"], s["dloc"].n)
        r.ew("vector", "reciprocal", drec[:], dloc[:], sem=s["rc"])
        r.wait("tensor", s["rc"], s["rc"].n)
        for h in range(16):
            bE = pb[6 + (h % 2)]
            if ev.get("be%d" % (h % 2)):
                r.wait("tensor", *ev["be%d" % (h % 2)])
            r.mm(bE[:], expd[:, h * 128:(h + 1) * 128], drec[:], True, True,
                 sem=s["exf"])
            r.wait("vector", s["exf"], s["exf"].n)
            r.ew("vector", "tensor_mul", obuf[:, h * 512:(h + 1) * 512],
                 obuf[:, h * 512:(h + 1) * 512], bE[:], sem=s["at"])
            ev["be%d" % (h % 2)] = (s["at"], s["at"].n)
        ev["qbuf"] = (s["at"], s["at"].n)
        r.wait("tensor", s["at"], s["at"].n)
        r.wait("tensor", xld[qslot], xld[qslot].n)
        for half in range(2):
            for vc in range(NCD):
                slw, wsem, lv = stream(half * 1024, vc * 128)
                r.wait("tensor", wsem, lv)
                if vc == 0:
                    r.wait("tensor", s["res"], s["res"].n)
                for oc8 in range(8):
                    r.mm(pb[oc8][:],
                         ws_a[:, slw * 2048 + oc8 * 128: slw * 2048 + (oc8 + 1) * 128],
                         obuf[:, vc * 512:(vc + 1) * 512],
                         vc == 0, vc == NCD - 1,
                         sem=(s["wo"] if oc8 == 7 else None))
                ev["ws%d" % slw] = (s["wo"], s["wo"].n)
            r.wait("vector", s["wo"], s["wo"].n)
            r.wait("vector", xld[qslot], xld[qslot].n)
            for oc8 in range(8):
                oc = half * 8 + oc8
                if DBG_WO:
                    r.ew("vector", "tensor_copy", dbgs[:], pb[oc8][:], sem=dbsem)
                    r.wait("gpsimd", dbsem, dbsem.n)
                    r.dma("gpsimd", ap["dbg"][oc * 128:(oc + 1) * 128,
                                              qb * 512:(qb + 1) * 512],
                          dbgs[:], sem=dbsd)
                    r.wait("vector", dbsd, dbsd.n)
                r.ew("vector", "tensor_add", xblk[:, oc * 512:(oc + 1) * 512],
                     xblk[:, oc * 512:(oc + 1) * 512], pb[oc8][:],
                     sem=(s["res"] if oc8 == 7 else None))
        r.wait("gpsimd", s["res"], s["res"].n)
        for dc in range(NCD):
            r.dma("gpsimd", ap["xscr"][dc * 128:(dc + 1) * 128, qb * 512:(qb + 1) * 512],
                  xblk[:, dc * 512:(dc + 1) * 512], sem=xout)
        ev["xblk"] = (xout, xout.n)
        if qb < NBLK - 1:
            r.wait("tensor", s["res"], s["res"].n)
    r.wait("gpsimd", xout, xout.n)
    r.wait("sync", xout, xout.n)



def g2b_exit(r, nc, ap, E, pb, c_idf):
    xin = [r.sem("xld0"), r.sem("xld1")]
    tp = r.sem("tp")
    evd = r.sem("evd")
    ost = r.sem("xout")

    xcol = E(nc.sbuf_tensor("2b_xcol", [128, 2 * 2048], F32))
    opan = E(nc.sbuf_tensor("2b_opan", [128, 2 * 2048], F32))

    tp_base = tp.n
    evd_base = evd.n
    ost_base = ost.n
    for g in range(16):
        slot = g % 2
        if g >= 2:
            r.wait("sync", tp, tp_base + (g - 1) * NCD)
        for dc in range(NCD):
            r.dma("sync", xcol[:, slot * 2048 + dc * 128: slot * 2048 + (dc + 1) * 128],
                  ap["xscr"][dc * 128:(dc + 1) * 128, g * 128:(g + 1) * 128],
                  sem=xin[slot])
        r.wait("tensor", xin[slot], xin[slot].n)
        for dc in range(NCD):
            bank = pb[dc % 2]
            j = g * NCD + dc
            if j >= 2:
                r.wait("tensor", evd, evd_base + j - 1)
            r.tr(bank[:, 0:128],
                 xcol[:, slot * 2048 + dc * 128: slot * 2048 + (dc + 1) * 128],
                 c_idf[:], sem=tp)
            r.wait("vector", tp, tp.n)
            if g >= 2 and dc == 0:
                r.wait("vector", ost, ost_base + 16 * (g - 1))
            r.ew("vector", "tensor_copy",
                 opan[:, slot * 2048 + dc * 128: slot * 2048 + (dc + 1) * 128],
                 bank[:, 0:128], sem=evd)
        r.wait("gpsimd", evd, evd.n)
        r.dma("gpsimd", ap["out"][g * 128:(g + 1) * 128, :],
              opan[:, slot * 2048:(slot + 1) * 2048], sem=ost)
    r.wait("gpsimd", ost, ost.n)
    r.wait("sync", ost, ost.n)



_NC_CACHE = [None]

def kernel(L0_attn_a_x, L0_attn_a_Wq, L0_attn_a_Wkv, L0_attn_a_Wo,
           L0_attn_a_qk_temp, L0_attn_b_Wq, L0_attn_b_Wk, L0_attn_b_Wv,
           L0_attn_b_Wo, L0_ffn_W1, L0_ffn_W2, L0_ffn_W3):
    x = np.ascontiguousarray(np.asarray(L0_attn_a_x, dtype=np.float32))
    shared = {
        "wqa": np.ascontiguousarray(L0_attn_a_Wq, dtype=np.float32),
        "wkva": np.ascontiguousarray(L0_attn_a_Wkv, dtype=np.float32),
        "woa": np.ascontiguousarray(L0_attn_a_Wo, dtype=np.float32),
        "temp": np.ascontiguousarray(np.asarray(L0_attn_a_qk_temp, dtype=np.float32).reshape(16, 1)),
        "wqb": np.ascontiguousarray(L0_attn_b_Wq, dtype=np.float32),
        "wkb": np.ascontiguousarray(L0_attn_b_Wk, dtype=np.float32),
        "wvb": np.ascontiguousarray(L0_attn_b_Wv, dtype=np.float32),
        "wob": np.ascontiguousarray(L0_attn_b_Wo, dtype=np.float32),
        "w1": np.ascontiguousarray(L0_ffn_W1, dtype=np.float32),
        "w2": np.ascontiguousarray(L0_ffn_W2, dtype=np.float32),
        "w3": np.ascontiguousarray(L0_ffn_W3, dtype=np.float32),
    }
    shared.update(_CONSTS)
    if _NC_CACHE[0] is None:
        _NC_CACHE[0] = build_program()
    nc = _NC_CACHE[0]
    in_maps = [dict(shared, x=x[i]) for i in range(8)]
    res = run_bass_kernel_spmd(nc, in_maps, core_ids=list(range(8)))
    return np.stack([np.asarray(res.results[i]["out"]) for i in range(8)], axis=0)
